# version 09
# baseline (speedup 1.0000x reference)
import os
from dataclasses import dataclass
from contextlib import ExitStack

import numpy as np
import ml_dtypes

import concourse.bass as bass
import concourse.mybir as mybir
from concourse.alu_op_type import AluOpType as OP

F32 = mybir.dt.float32
BF16 = mybir.dt.bfloat16
I32 = mybir.dt.int32
AF = mybir.ActivationFunctionType
AX = mybir.AxisListType.X
BF = ml_dtypes.bfloat16


@dataclass
class Cfg:
    D: int = 1024
    DI: int = 2048
    NST: int = 16
    DTR: int = 64
    DCONV: int = 4
    L: int = 4096
    E: int = 8
    DF: int = 4096
    LC: int = 512
    EPS_LN: float = 1e-5

    @property
    def KD(self): return self.D // 128
    @property
    def TI(self): return self.DI // 128
    @property
    def TS(self): return self.TI // 2
    @property
    def NCH(self): return self.L // self.LC
    @property
    def OWN(self): return self.L // 2
    @property
    def BLK(self): return self.OWN // 2
    @property
    def NSUB(self): return max(1, self.BLK // 512)
    @property
    def SUB(self): return min(512, self.BLK)
    @property
    def DFT(self): return self.DF // 128
    @property
    def MGA(self): return min(4, self.KD)
    @property
    def DG2(self): return min(4, self.KD)
    @property
    def MG1(self): return self.DF // 256
    @property
    def XG(self): return self.TI // 4
    @property
    def NTC(self): return self.BLK // 128
    @property
    def NTL(self): return self.LC // 128


FULL = Cfg()
SMALL = Cfg(D=256, DI=512, DTR=32, L=512, DF=1024, LC=128)

ENG_ATTR = {"T": "tensor", "V": "vector", "S": "scalar", "G": "gpsimd", "Y": "sync"}


class DmaChan:

    def __init__(self, nc, es, name):
        self.nc, self.es, self.name = nc, es, name
        self.sems, self.idx, self.cnt = [], -1, 0
        self._new()

    def _new(self):
        self.idx += 1
        self.sems.append(self.es.enter_context(
            self.nc.semaphore(f"dq_{self.name}_{self.idx}")))
        self.cnt = 0

    def next_handle(self, n_dmas):
        if self.cnt + 16 * n_dmas > 60000:
            self._new()
        sem = self.sems[self.idx]
        self.cnt += 16 * n_dmas
        return (sem, self.cnt)


class Prog:

    def __init__(self, nc, es):
        self.nc, self.es = nc, es
        self.sems = {k: es.enter_context(nc.semaphore(f"sem_{k}")) for k in "TVSG"}
        self.cnt = {k: 0 for k in "TVSG"}
        self.chans = {}
        self.stages = {k: [] for k in ENG_ATTR}

    @staticmethod
    def _flat(deps):
        out = []
        for d in deps:
            if d is None:
                continue
            if isinstance(d, list):
                out.extend(x for x in d if x is not None)
            else:
                out.append(d)
        return out

    def stage(self, eng, deps, fn):
        waits = self._flat(deps)
        self.cnt[eng] += 1
        handle = (self.sems[eng], self.cnt[eng])
        self.stages[eng].append(("c", waits, fn, handle))
        return handle

    def dma_stage(self, eng, deps, fn, n_dmas, chan="misc"):
        waits = self._flat(deps)
        if chan not in self.chans:
            self.chans[chan] = DmaChan(self.nc, self.es, chan)
        h = self.chans[chan].next_handle(n_dmas)
        self.stages[eng].append(("d", waits, fn, (h, n_dmas)))
        return h

    def barrier(self):
        hs = [(self.sems[k], self.cnt[k]) for k in "TVSG" if self.cnt[k] > 0]
        for q in self.chans.values():
            if q.cnt > 0:
                hs.append((q.sems[q.idx], q.cnt))
        return hs

    def emit(self, block):
        for eng, attr in ENG_ATTR.items():
            stages = self.stages[eng]
            if not stages:
                continue

            def runner(e, stages=stages):
                watermark = {}
                import os as _os
                nodedup = bool(_os.environ.get("KNODEDUP"))
                for kind, waits, fn, info in stages:
                    for (sem, val) in waits:
                        if not nodedup and watermark.get(sem, -1) >= val:
                            continue
                        e.wait_ge(sem, val)
                        watermark[sem] = val
                    if kind == "c":
                        sem, val = info
                        last = fn(e)
                        assert last is not None, "stage fn returned None"
                        last.then_inc(sem, 1)
                    else:
                        (sem, endval), n = info
                        issued = [0]

                        def dma(out, in_, e=e, sem=sem, issued=issued):
                            ins = e.dma_start(out=out, in_=in_)
                            ins.then_inc(sem, 16)
                            issued[0] += 1
                            return ins

                        fn(e, dma)
                        assert issued[0] == n, f"dma stage issued {issued[0]} != {n}"

            getattr(block, attr)(runner)


class WRing:

    def __init__(self, P, bufs, name, init_dep):
        self.P = P
        self.bufs = bufs
        self.name = name
        self.n = len(bufs)
        self.last_use = [init_dep] * self.n
        self.i = 0

    def load(self, src_ap, width):
        slot = self.i % self.n
        self.i += 1
        buf = self.bufs[slot]

        def f(e, dma, buf=buf, src_ap=src_ap, width=width):
            dma(out=buf[:, 0:width], in_=src_ap)

        h = self.P.dma_stage("G", [self.last_use[slot]], f, 1,
                             chan=f"{self.name}{slot}")
        return slot, h

    def used(self, slot, handle):
        self.last_use[slot] = handle


def build_program(cfg: Cfg):
    PH = int(os.environ.get("KPHASES", "99"))
    c = cfg
    nc = bass.Bass("TRN2", target_bir_lowering=False, debug=False)
    es = ExitStack()

    KD, TI, TS, NST, DTR = c.KD, c.TI, c.TS, c.NST, c.DTR
    LC, NCH, L, D = c.LC, c.NCH, c.L, c.D
    BLK, NSUB, SUB, DFT, E = c.BLK, c.NSUB, c.SUB, c.DFT, c.E
    MGA, DG2, MG1, XG, NTC, NTL = c.MGA, c.DG2, c.MG1, c.XG, c.NTC, c.NTL
    NSD = -(-D // 512)
    XP = DTR + 2 * NST

    def inp(name, shape, dtype=F32):
        return nc.dram_tensor(name, list(shape), dtype, kind="ExternalInput")

    xfull = inp("xfull", [L, D])
    xown = inp("xown", [c.OWN, D])
    winx = inp("winx", [KD, XG, 128, 512])
    winz = inp("winz", [KD, XG, 128, 512])
    wout = inp("wout", [TI, KD // MGA, 128, MGA * 128])
    xprojt = inp("xprojt", [TI, 128, XP])
    dtprojt = inp("dtprojt", [DTR, TS * 128])
    convw = inp("convw", [128, TI * c.DCONV])
    convb = inp("convb", [128, TI])
    dvec = inp("dvec", [128, TS])
    alog = inp("alog", [128, TS * NST])
    dtb = inp("dtb", [128, TS])
    lng = inp("lng", [128, KD])
    lnb = inp("lnb", [128, KD])
    abelb = inp("abelb", [128, KD])
    abelw = inp("abelw", [KD, KD // MGA, 128, MGA * 128])
    gatew = inp("gatew", [KD, 128, E])
    gateb = inp("gateb", [1, E])
    mixw = inp("mixw", [128, 2 * KD])
    mixb = inp("mixb", [1, 1])
    w1t = inp("w1t", [E, KD, MG1, 128, 256])
    w2t = inp("w2t", [E, KD // DG2, DFT, 128, DG2 * 128])
    b1v = inp("b1v", [128, E * DFT])
    b2v = inp("b2v", [E, D])
    gmat = inp("gmat", [D, D])
    gmatT = inp("gmatT", [D, D])
    identf = inp("identf", [128, 128])
    identb = inp("identb", [128, 128], BF16)
    onesf = inp("onesf", [1, 128])
    onescol = inp("onescol", [128, 1])
    onesb = inp("onesb", [1, 128], BF16)
    cvals = inp("cvals", [128, 4])
    sel32 = inp("sel32", [32, 32 * 128], BF16)
    sel8 = inp("sel8", [8, 8 * 128])
    yoff = inp("yoff", [1, 4], I32)

    xin_sc = nc.dram_tensor("xin_sc", [TS * 128, L], BF16, kind="Internal")
    dt_dram = nc.dram_tensor("dt_dram", [DTR, L], F32, kind="Internal")
    bc_dram = nc.dram_tensor("bc_dram", [2 * NST, L], F32, kind="Internal")
    yagin = nc.dram_tensor("yagin", [TS * 128, L], BF16, kind="Internal")
    yago = nc.dram_tensor("yago", [8 * TS * 128, L], BF16, kind="Internal", addr_space="Shared")
    eagin = nc.dram_tensor("eagin", [D, D], BF16, kind="Internal")
    eago = nc.dram_tensor("eago", [8 * D, D], BF16, kind="Internal", addr_space="Shared")
    s2dram = nc.dram_tensor("s2dram", [2 * KD * 128, BLK], F32, kind="Internal")
    rotdram = nc.dram_tensor("rotdram", [KD * 128, D], BF16, kind="Internal")
    yout = nc.dram_tensor("yout", [c.OWN, D], F32, kind="ExternalOutput")

    P = Prog(nc, es)
    sb = lambda nm, shp, dt=F32: es.enter_context(nc.sbuf_tensor(nm, shp, dt))
    ps = [es.enter_context(nc.psum_tensor(f"ps{i}", [128, 512], F32)) for i in range(8)]
    psr = [None] * 8

    s_identf = sb("s_identf", [128, 128])
    s_identb = sb("s_identb", [128, 128], BF16)
    s_onesf = sb("s_onesf", [1, 128])
    s_onescol = sb("s_onescol", [128, 1])
    s_onesb = sb("s_onesb", [1, 128], BF16)
    s_cvals = sb("s_cvals", [128, 4])
    s_sel8 = sb("s_sel8", [8, 8 * 128], BF16)
    s_convw = sb("s_convw", [128, TI * c.DCONV])
    s_convb = sb("s_convb", [128, TI])
    s_dvec = sb("s_dvec", [128, TS])
    s_aneg = sb("s_aneg", [128, TS * NST])
    s_dtb = sb("s_dtb", [128, TS])
    s_lng = sb("s_lng", [128, KD])
    s_lnb = sb("s_lnb", [128, KD])
    s_abelb = sb("s_abelb", [128, KD])
    s_gatew = sb("s_gatew", [128, KD * E], BF16)
    s_gatebbc = sb("s_gatebbc", [128, E])
    s_gateb_row = sb("s_gateb_row", [1, E])
    s_mixw = sb("s_mixw", [128, 2 * KD], BF16)
    s_mixb = sb("s_mixb", [1, 1])
    s_b2 = sb("s_b2", [8, D], BF16)
    s_b1 = sb("s_b1", [128, E * DFT])
    s_yoff = sb("s_yoff", [1, 4], I32)
    s_fin = sb("s_fin", [1, 1])
    x_tm = [sb(f"x_tm{i}", [128, D]) for i in range(2)]
    xn_tm = [sb(f"xn_tm{i}", [128, D], BF16) for i in range(2)]
    s_cst = sb("s_cst", [128, 6])
    s_sq = sb("s_sq", [128, D])

    def ld_params(e, dma):
        for dst, src in [(s_identf, identf), (s_identb, identb), (s_onesf, onesf),
                         (s_onescol, onescol), (s_onesb, onesb),
                         (s_cvals, cvals), (s_convw, convw),
                         (s_convb, convb), (s_dvec, dvec), (s_aneg, alog),
                         (s_dtb, dtb), (s_lng, lng), (s_lnb, lnb),
                         (s_abelb, abelb), (s_mixb, mixb),
                         (s_b1, b1v), (s_yoff, yoff), (s_gateb_row, gateb)]:
            dma(out=dst[:, :], in_=src.ap())

    h_par = P.dma_stage("Y", [], ld_params, 18, chan="par")
    for _i, _v in enumerate([c.EPS_LN, -0.5, -1.0, 1.0]):
        nc.const_aps.register(F32, _v, s_cvals[:, _i:_i + 1])

    def ld_params_cast(e, dma):
        dma(out=s_gatew[:, :].rearrange("p (k e) -> p k e", k=KD), in_=gatew.ap().rearrange("k p e -> p k e"))
        dma(out=s_mixw[:, :], in_=mixw.ap())
        dma(out=s_sel8[:, :], in_=sel8.ap())
        dma(out=s_b2[:, :], in_=b2v.ap())

    h_parc = P.dma_stage("G", [], ld_params_cast, 4, chan="parc")

    h_aexp = P.stage("S", [h_par], lambda e: e.activation(
        s_aneg[:, :], s_aneg[:, :], AF.Exp))
    h_aneg = P.stage("V", [h_aexp], lambda e: e.tensor_scalar_mul(
        s_aneg[:, :], s_aneg[:, :], -1.0))

    h_gb1 = P.stage("T", [h_par, psr[0]], lambda e: e.matmul(
        ps[0][0:128, 0:E], s_onesf[0:1, :], s_gateb_row[0:1, 0:E],
        start=True, stop=True))
    h_gbbc = P.stage("V", [h_gb1], lambda e: e.tensor_copy(
        s_gatebbc[:, :], ps[0][0:128, 0:E]))
    psr[0] = h_gbbc

    def mm_DD(dst, lhsT_sb, rhs_sb, deps, fuse_add_blockpoly=None, scr_tracker=None):
        hs = []
        for m in range(KD):
            for s_ in range(NSD):
                w = min(512, D - s_ * 512)
                lo = m * D + s_ * 512
                pi = (m * NSD + s_) % 6

                def f(e, m=m, s_=s_, w=w, pi=pi, lhsT_sb=lhsT_sb, rhs_sb=rhs_sb):
                    ins = None
                    for k in range(KD):
                        ins = e.matmul(
                            ps[pi][:, 0:w],
                            lhsT_sb[:, k * D + m * 128:k * D + (m + 1) * 128],
                            rhs_sb[:, k * D + s_ * 512:k * D + s_ * 512 + w],
                            start=(k == 0), stop=(k == KD - 1))
                    return ins

                hm = P.stage("T", deps + [psr[pi]], f)
                if fuse_add_blockpoly is None:
                    hc = P.stage("V", [hm], lambda e, lo=lo, w=w, pi=pi, dst=dst:
                                 e.tensor_copy(dst[:, lo:lo + w], ps[pi][:, 0:w]))
                else:
                    hpoly = fuse_add_blockpoly(m, s_, w, lo, deps)
                    hc = P.stage("V", [hm, hpoly], lambda e, lo=lo, w=w, pi=pi, dst=dst:
                                 e.tensor_tensor(dst[:, lo:lo + w], sScr[:, 0:w],
                                                 ps[pi][:, 0:w], OP.add))
                    if scr_tracker is not None:
                        scr_tracker[0] = hc
                psr[pi] = hc
                hs.append(hc)
        return hs

    def transpose_DD(src, dst, deps):
        hs = []
        for km in range(KD):
            for kk in range(KD):
                pi = 6 + (km * KD + kk) % 2

                def ftr(e, km=km, kk=kk, pi=pi, src=src):
                    return e.transpose(
                        ps[pi].ap().bitcast(BF16)[:, 0:128],
                        src[:, kk * D + km * 128:kk * D + (km + 1) * 128],
                        s_identb[:, :])

                ht = P.stage("T", deps + [psr[pi], h_par], ftr)

                def ftc(e, km=km, kk=kk, pi=pi, dst=dst):
                    return e.tensor_copy(
                        dst[:, km * D + kk * 128:km * D + (kk + 1) * 128],
                        ps[pi].ap().bitcast(BF16)[:, 0:128])

                hc = P.stage("V", [ht], ftc)
                psr[pi] = hc
                hs.append(hc)
        return hs


    def _finish_early():
        P.stage("G", P.barrier() + [h_par], lambda e: e.memset(s_fin[0:1, 0:1], 0.0))
        with nc.Block() as block:
            P.emit(block)
        return nc
    p0 = ExitStack()
    sG = p0.enter_context(nc.sbuf_tensor("sG", [128, KD * D], BF16))
    sGT = p0.enter_context(nc.sbuf_tensor("sGT", [128, KD * D], BF16))
    sG2 = p0.enter_context(nc.sbuf_tensor("sG2", [128, KD * D], BF16))
    sG3 = p0.enter_context(nc.sbuf_tensor("sG3", [128, KD * D], BF16))
    sG4 = p0.enter_context(nc.sbuf_tensor("sG4", [128, KD * D], BF16))
    sG4T = p0.enter_context(nc.sbuf_tensor("sG4T", [128, KD * D], BF16))
    sT0 = p0.enter_context(nc.sbuf_tensor("sT0", [128, KD * D], BF16))
    sT1 = p0.enter_context(nc.sbuf_tensor("sT1", [128, KD * D], BF16))
    sScr = p0.enter_context(nc.sbuf_tensor("sScr", [128, 512], F32))

    def ld_g(e, dma):
        dma(out=sG[:, :].rearrange("p (k d) -> p k d", k=KD), in_=gmat.ap().rearrange("(k p) d -> p k d", p=128))
        dma(out=sGT[:, :].rearrange("p (k d) -> p k d", k=KD), in_=gmatT.ap().rearrange("(k p) d -> p k d", p=128))

    h_g = P.dma_stage("G", [], ld_g, 2, chan="gmat")

    hG2 = mm_DD(sG2, sGT, sG, [h_g])
    hG3 = mm_DD(sG3, sGT, sG2, [h_g] + hG2)
    hG4 = mm_DD(sG4, sGT, sG3, [h_g] + hG3)
    hG4T = transpose_DD(sG4, sG4T, hG4)

    fact = [1.0]
    for i in range(1, 12):
        fact.append(fact[-1] * i)
    coef = [1.0 / f for f in fact]

    last_scr = [None]

    def mk_blockpoly(cidx):
        c0, c1, c2, c3 = coef[cidx], coef[cidx + 1], coef[cidx + 2], coef[cidx + 3]

        def fn(m, s_, w, lo, deps):
            h1 = P.stage("V", deps + [last_scr[0]], lambda e, lo=lo, w=w:
                         e.tensor_scalar(sScr[:, 0:w], sG[:, lo:lo + w], c1, None, OP.mult))
            h2 = P.stage("V", [h1], lambda e, lo=lo, w=w: e.scalar_tensor_tensor(
                sScr[:, 0:w], sG2[:, lo:lo + w], c2, sScr[:, 0:w], OP.mult, OP.add))
            h3 = P.stage("V", [h2], lambda e, lo=lo, w=w: e.scalar_tensor_tensor(
                sScr[:, 0:w], sG3[:, lo:lo + w], c3, sScr[:, 0:w], OP.mult, OP.add))
            dcol = m * 128 - s_ * 512
            if 0 <= dcol <= w - 128:
                h3 = P.stage("V", [h3], lambda e, dcol=dcol: e.scalar_tensor_tensor(
                    sScr[:, dcol:dcol + 128], s_identf[:, :], c0,
                    sScr[:, dcol:dcol + 128], OP.mult, OP.add))
            last_scr[0] = h3
            return h3

        return fn

    hB2 = []
    pol2 = mk_blockpoly(8)
    for m in range(KD):
        for s_ in range(NSD):
            w = min(512, D - s_ * 512)
            lo = m * D + s_ * 512
            hp = pol2(m, s_, w, lo, hG2 + hG3 + [h_g, h_par])
            hc = P.stage("V", [hp], lambda e, lo=lo, w=w: e.tensor_copy(
                sT0[:, lo:lo + w], sScr[:, 0:w]))
            last_scr[0] = hc
            hB2.append(hc)

    hT1 = mm_DD(sT1, sG4T, sT0, hG4T + hB2 + [h_g], fuse_add_blockpoly=mk_blockpoly(4), scr_tracker=last_scr)
    hE = mm_DD(sT0, sG4T, sT1, hG4T + hT1 + [h_g], fuse_add_blockpoly=mk_blockpoly(0), scr_tracker=last_scr)

    h_eout = P.dma_stage("Y", hE, lambda e, dma: dma(
        out=eagin.ap().rearrange("(k p) d -> p k d", p=128),
        in_=sT0[:, :].rearrange("p (k d) -> p k d", k=KD)), 1, chan="eout")

    bar_p0 = P.barrier()
    p0.close()
    if PH <= 1:
        return _finish_early()

    ln_state = {"last": None}
    lnwar = {0: [], 1: []}

    def ln1_tokens(src_dram, row0, bi, extra_deps):
        hld = P.dma_stage("Y", extra_deps + [ln_state["last"]] + lnwar[bi],
                          lambda e, dma: dma(
                              out=x_tm[bi][:, :],
                              in_=src_dram.ap()[row0:row0 + 128, :]),
                          1, chan=f"xt{bi}")
        h1 = P.stage("V", [hld, ln_state["last"]], lambda e: e.reduce_sum(
            s_cst[:, 0:1], x_tm[bi][:, :], axis=AX))
        h2 = P.stage("S", [hld, ln_state["last"]], lambda e: e.activation(
            s_sq[:, :], x_tm[bi][:, :], AF.Square))
        h3 = P.stage("V", [h2, h1], lambda e: e.reduce_sum(
            s_cst[:, 1:2], s_sq[:, :], axis=AX))
        h4 = P.stage("V", [h3], lambda e: e.tensor_scalar(
            s_cst[:, 2:3], s_cst[:, 0:1], 1.0 / D, None, OP.mult))
        h5 = P.stage("V", [h4], lambda e: e.tensor_tensor(
            s_cst[:, 3:4], s_cst[:, 2:3], s_cst[:, 2:3], OP.mult))
        h6 = P.stage("V", [h5], lambda e: e.tensor_scalar(
            s_cst[:, 4:5], s_cst[:, 1:2], 1.0 / D, s_cst[:, 3:4], OP.mult, OP.subtract))
        h7 = P.stage("S", [h6, h_par], lambda e: e.activation(
            s_cst[:, 5:6], s_cst[:, 4:5], AF.Ln, bias=c.EPS_LN))
        h8 = P.stage("S", [h7], lambda e: e.activation(
            s_cst[:, 5:6], s_cst[:, 5:6], AF.Exp, scale=-0.5))
        h9 = P.stage("V", [h8], lambda e: e.tensor_scalar(
            xn_tm[bi][:, :], x_tm[bi][:, :], s_cst[:, 2:3], s_cst[:, 5:6],
            OP.subtract, OP.mult))
        ln_state["last"] = h9
        return h9

    fr = ExitStack()
    xn_fm = [fr.enter_context(nc.sbuf_tensor(f"xn_fm{i}", [128, KD * LC], BF16))
             for i in range(2)]
    NWF = 2 * KD
    s_win = [fr.enter_context(nc.sbuf_tensor(f"s_win{i}", [128, 512], BF16))
             for i in range(NWF)]
    xpre = [fr.enter_context(nc.sbuf_tensor(f"xpre{i}", [128, LC + 3], F32))
            for i in range(3)]
    xtail = fr.enter_context(nc.sbuf_tensor("xtail", [128, TI * 3], F32))
    s_xin = fr.enter_context(nc.sbuf_tensor("s_xin", [128, TI * LC], BF16))
    s_xdbl = fr.enter_context(nc.sbuf_tensor("s_xdbl", [XP, LC], F32))
    s_cvs = fr.enter_context(nc.sbuf_tensor("s_cvs", [128, LC], F32))
    s_xprojt = fr.enter_context(nc.sbuf_tensor("s_xprojt", [128, TI * XP], BF16))
    h_xprj = P.dma_stage("G", bar_p0, lambda e, dma: dma(
        out=s_xprojt[:, :].rearrange("p (t n) -> p t n", t=TI),
        in_=xprojt.ap().rearrange("t p n -> p t n")), 1, chan="xprj")

    h_xt0 = P.stage("V", bar_p0, lambda e: e.memset(xtail[:, :], 0.0))
    wring_f = WRing(P, s_win, "wf", bar_p0)

    h_xinsp = [None] * NCH
    h_dtsp = [None] * NCH
    lw = [[h_parc] + bar_p0, [h_parc] + bar_p0]
    l_xnfm = [bar_p0, bar_p0]
    l_xpre = [bar_p0, bar_p0, bar_p0]
    l_xin = [bar_p0]
    l_xdbl = bar_p0
    l_tail = [h_xt0] * TI
    l_cvs = bar_p0

    def emit_front(ch):
        nonlocal l_xdbl, l_cvs, l_xin
        bi = ch % 2
        hs_tr = []
        for tt in range(NTL):
            h9 = ln1_tokens(xfull, ch * LC + tt * 128, bi, bar_p0 if ch == 0 else [])
            lnwar[bi] = [h9]
            for k in range(KD):
                pi = (tt * KD + k) % 4

                def ftr(e, bi=bi, k=k, pi=pi):
                    return e.transpose(ps[pi].ap().bitcast(BF16)[:, 0:128],
                                       xn_tm[bi][:, k * 128:(k + 1) * 128],
                                       s_identb[:, :])

                ht = P.stage("T", [h9, psr[pi], h_par], ftr)

                def ftc(e, bi=bi, k=k, tt=tt, pi=pi):
                    return e.tensor_copy(
                        xn_fm[bi][:, k * LC + tt * 128:k * LC + (tt + 1) * 128],
                        ps[pi].ap().bitcast(BF16)[:, 0:128])

                hc = P.stage("V", [ht, l_xnfm[bi]], ftc)
                psr[pi] = hc
                lnwar[bi].append(ht)
                hs_tr.append(hc)

        hs_sil = {}
        for g in range(XG):
            tiles = [wring_f.load(winx.ap()[k, g], 512) for k in range(KD)]

            def fgrp(e, tiles=tiles, bi=bi):
                ins = None
                for k in range(KD):
                    slot, _ = tiles[k]
                    for m4 in range(4):
                        ins = e.matmul(ps[4 + m4][:, 0:LC],
                                       s_win[slot][:, m4 * 128:(m4 + 1) * 128],
                                       xn_fm[bi][:, k * LC:(k + 1) * LC],
                                       start=(k == 0), stop=(k == KD - 1))
                return ins

            hgrp = P.stage("T", [h for _, h in tiles] + hs_tr +
                           [psr[4 + m4] for m4 in range(4)], fgrp)
            for slot, _ in tiles:
                wring_f.used(slot, hgrp)
            group_last_mm = {m4: hgrp for m4 in range(4)}
            l_xnfm_new = hgrp
            for m4 in range(4):
                t = g * 4 + m4
                pi = 4 + m4
                xb = xpre[t % 3]
                h_t0 = P.stage("V", [l_tail[t], l_xpre[t % 3]],
                               lambda e, xb=xb, t=t: e.tensor_copy(
                                   xb[:, 0:3], xtail[:, t * 3:t * 3 + 3]))
                h_ev = P.stage("S", [group_last_mm[m4], h_t0],
                               lambda e, xb=xb, pi=pi: e.copy(
                                   xb[:, 3:3 + LC], ps[pi][:, 0:LC]))
                psr[pi] = h_ev
                h_tl = P.stage("V", [h_ev], lambda e, xb=xb, t=t: e.tensor_copy(
                    xtail[:, t * 3:t * 3 + 3], xb[:, LC:LC + 3]))
                h_c = P.stage("V", [h_ev, l_cvs], lambda e, xb=xb, t=t: e.tensor_scalar(
                    s_cvs[:, 0:LC], xb[:, 0:LC], s_convw[:, t * 4:t * 4 + 1],
                    None, OP.mult))
                for kk in range(1, c.DCONV):
                    h_c = P.stage("V", [h_c], lambda e, xb=xb, t=t, kk=kk:
                                  e.scalar_tensor_tensor(
                                      s_cvs[:, 0:LC], xb[:, kk:kk + LC],
                                      s_convw[:, t * 4 + kk:t * 4 + kk + 1],
                                      s_cvs[:, 0:LC], OP.mult, OP.add))
                h_si = P.stage("S", [h_c] + l_xin, lambda e, t=t: e.activation(
                    s_xin[:, t * LC:(t + 1) * LC], s_cvs[:, 0:LC], AF.Silu,
                    bias=s_convb[:, t:t + 1]))
                l_cvs = h_si
                l_xpre[t % 3] = h_si
                l_tail[t] = h_tl
                hs_sil[t] = h_si
        l_xnfm[bi] = l_xnfm_new
        l_xin = []

        def sp_xin(e, dma, ch=ch):
            for t in range(TS):
                dma(out=xin_sc.ap().rearrange("(t p) l -> t p l", p=128)
                    [t, :, ch * LC:(ch + 1) * LC],
                    in_=s_xin[:, t * LC:(t + 1) * LC])

        h_xinsp[ch] = P.dma_stage("Y", [hs_sil[t] for t in range(TS)], sp_xin, TS,
                                  chan="spx")
        l_xin.append(h_xinsp[ch])

        def fxp(e):
            ins = None
            for k in range(TI):
                ins = e.matmul(ps[0][0:XP, 0:LC],
                               s_xprojt[:, k * XP:(k + 1) * XP],
                               s_xin[:, k * LC:(k + 1) * LC],
                               start=(k == 0), stop=(k == TI - 1))
            return ins

        h_xp = P.stage("T", [hs_sil[t] for t in range(TI)] + [psr[0], h_xprj], fxp)
        l_xin.append(h_xp)

        h_xd = P.stage("V", [h_xp, l_xdbl], lambda e: e.tensor_copy(
            s_xdbl[0:XP, 0:LC], ps[0][0:XP, 0:LC]))
        psr[0] = h_xd

        def sp_dt(e, dma, ch=ch):
            dma(out=dt_dram.ap()[:, ch * LC:(ch + 1) * LC], in_=s_xdbl[0:DTR, 0:LC])
            dma(out=bc_dram.ap()[:, ch * LC:(ch + 1) * LC],
                in_=s_xdbl[DTR:DTR + 2 * NST, 0:LC])

        h_dtsp[ch] = P.dma_stage("Y", [h_xd], sp_dt, 2, chan=f"spdt{ch}")
        l_xdbl = h_dtsp[ch]


    sc = ExitStack()
    s_sel32 = sc.enter_context(nc.sbuf_tensor("s_sel32", [32, 32 * 128], BF16))
    s_dtprojt = sc.enter_context(nc.sbuf_tensor("s_dtprojt", [DTR, TS * 128], BF16))
    h_sel32 = P.dma_stage("G", bar_p0, lambda e, dma: (
        dma(out=s_sel32[:, :], in_=sel32.ap()),
        dma(out=s_dtprojt[:, :], in_=dtprojt.ap())), 2, chan="sel32")
    z_xin = [sc.enter_context(nc.sbuf_tensor(f"z_xin{i}", [128, TS * LC], BF16))
             for i in range(2)]
    z_dt = [sc.enter_context(nc.sbuf_tensor(f"z_dt{i}", [DTR, LC], BF16))
            for i in range(2)]
    z_bc = [sc.enter_context(nc.sbuf_tensor(f"z_bc{i}", [32, LC], BF16))
            for i in range(2)]
    z_bcr = sc.enter_context(nc.sbuf_tensor("z_bcr", [128, 2 * NST * LC], BF16))
    z_dl = [sc.enter_context(nc.sbuf_tensor(f"z_dl{i}", [128, LC], F32)) for i in range(2)]
    z_du = [sc.enter_context(nc.sbuf_tensor(f"z_du{i}", [128, LC], BF16))
            for i in range(2)]
    z_da = [sc.enter_context(nc.sbuf_tensor(f"z_da{i}", [128, LC], F32))
            for i in range(2)]
    z_db = [sc.enter_context(nc.sbuf_tensor(f"z_db{i}", [128, 2 * LC], F32))
            for i in range(2)]
    z_h = sc.enter_context(nc.sbuf_tensor("z_h", [128, 2 * LC], F32))
    z_tmp = [sc.enter_context(nc.sbuf_tensor(f"z_tmp{i}", [128, 2 * LC], F32))
             for i in range(2)]
    z_y = [sc.enter_context(nc.sbuf_tensor(f"z_y{i}", [128, LC], F32)) for i in range(2)]
    z_y16 = [sc.enter_context(nc.sbuf_tensor(f"z_y16{i}", [128, LC], BF16))
             for i in range(2)]
    z_sp = sc.enter_context(nc.sbuf_tensor("z_sp", [128, LC], F32))
    z_sp2 = sc.enter_context(nc.sbuf_tensor("z_sp2", [128, LC], F32))
    carry_sb = sc.enter_context(nc.sbuf_tensor("carry_sb", [128, TS * NST], F32))

    l_zxin = [bar_p0, bar_p0]
    l_zdt = [bar_p0, bar_p0]
    l_zbc = [bar_p0, bar_p0]
    l_bcr = bar_p0
    l_dl = [bar_p0, bar_p0]
    l_du = [bar_p0, bar_p0]
    l_da = [bar_p0, bar_p0]
    l_db = [bar_p0] * 2
    l_h = [bar_p0, bar_p0]
    l_tmp = [bar_p0] * 2
    l_y = [bar_p0, bar_p0]
    l_y16 = [bar_p0, bar_p0]
    l_sp = bar_p0
    l_sp2 = bar_p0
    h_ysp = []
    carry_h = {}

    def emit_scan(chs):
        nonlocal l_bcr, l_sp, l_sp2
        bi = chs % 2

        def ld_scan(e, dma, chs=chs, bi=bi):
            for t in range(TS):
                dma(out=z_xin[bi][:, t * LC:(t + 1) * LC],
                    in_=xin_sc.ap().rearrange("(t p) l -> t p l", p=128)
                    [t, :, chs * LC:(chs + 1) * LC])

        h_ldx = P.dma_stage("Y", [h_xinsp[chs], l_zxin[bi]], ld_scan, TS,
                            chan=f"lsx{bi}")

        def ld_dtbc(e, dma, chs=chs, bi=bi):
            dma(out=z_dt[bi][:, :], in_=dt_dram.ap()[:, chs * LC:(chs + 1) * LC])
            dma(out=z_bc[bi][:, :], in_=bc_dram.ap()[:, chs * LC:(chs + 1) * LC])

        h_lddt = P.dma_stage("G", [h_dtsp[chs], l_zdt[bi], l_zbc[bi]],
                             ld_dtbc, 2, chan=f"ldt{bi}")

        hs_bcr = []
        for n in range(2 * NST):
            pi = n % 2

            def fb(e, n=n, pi=pi, bi=bi):
                return e.matmul(ps[pi][:, 0:LC],
                                s_sel32[:, n * 128:(n + 1) * 128],
                                z_bc[bi][:, :], start=True, stop=True)

            hbm = P.stage("T", [h_lddt, psr[pi], h_sel32], fb)

            def fbc(e, n=n, pi=pi):
                return e.activation(z_bcr[:, n * LC:(n + 1) * LC],
                                    ps[pi][:, 0:LC], AF.Copy)

            hbc = P.stage("S", [hbm, l_bcr], fbc)
            psr[pi] = hbc
            hs_bcr.append(hbc)
        l_zbc[bi] = hs_bcr[-1]

        last_reader_bcr = None
        for t in range(TS):
            ti = t % 2

            def fdt(e, t=t, bi=bi):
                return e.matmul(ps[2][:, 0:LC],
                                s_dtprojt[:, t * 128:(t + 1) * 128],
                                z_dt[bi][:, :], start=True, stop=True)

            hdtm = P.stage("T", [h_lddt, psr[2], h_sel32], fdt)
            hr = P.stage("S", [hdtm] + [l_dl[ti]], lambda e, t=t, ti=ti: e.activation(
                z_dl[ti][:, :], ps[2][:, 0:LC], AF.Relu, bias=s_dtb[:, t:t + 1]))
            ha = P.stage("S", [hdtm, l_sp], lambda e, t=t: e.activation(
                z_sp[:, :], ps[2][:, 0:LC], AF.Abs, bias=s_dtb[:, t:t + 1]))
            psr[2] = ha
            he = P.stage("S", [ha, l_sp2, h_par], lambda e: e.activation(
                z_sp2[:, :], z_sp[:, :], AF.Exp, scale=-1.0))
            hl = P.stage("S", [he], lambda e: e.activation(
                z_sp[:, :], z_sp2[:, :], AF.Ln, bias=1.0))
            l_sp = hl
            l_sp2 = hl
            hdl = P.stage("V", [hl, hr], lambda e, ti=ti: e.tensor_tensor(
                z_dl[ti][:, :], z_dl[ti][:, :], z_sp[:, :], OP.add))
            l_sp = hdl
            hdu = P.stage("V", [hdl, h_ldx] + [l_du[ti]], lambda e, t=t, ti=ti, bi=bi:
                          e.tensor_tensor(z_du[ti][:, :], z_dl[ti][:, :],
                                          z_xin[bi][:, t * LC:(t + 1) * LC], OP.mult))

            hda_last = None

            def emit_hdb(n2):
                pb = n2 % 2
                h = P.stage("G", [hdu, hs_bcr[2 * n2], hs_bcr[2 * n2 + 1], l_db[pb]],
                            lambda e, n2=n2, ti=ti, pb=pb:
                            e.tensor_tensor(
                                z_db[pb][:, :].rearrange("p (a l) -> p a l", a=2),
                                z_du[ti][:, :].unsqueeze(1).to_broadcast([128, 2, LC]),
                                z_bcr[:, 2 * n2 * LC:(2 * n2 + 2) * LC]
                                .rearrange("p (a l) -> p a l", a=2), OP.mult))
                return h

            def emit_hda(n):
                nb = n % 2
                return P.stage("S", [hdl, l_da[nb], h_aneg],
                               lambda e, t=t, n=n, ti=ti, nb=nb:
                               e.activation(z_da[nb][:, :], z_dl[ti][:, :], AF.Exp,
                                            scale=s_aneg[:, t * NST + n:t * NST + n + 1]))

            hdbs = {0: emit_hdb(0), 1: emit_hdb(1)}
            hdas = {0: emit_hda(0), 1: emit_hda(1)}
            hh_pair = [None, None]
            for n in range(NST):
                nb = n % 2
                pb = (n // 2) % 2
                hda = hdas[n]
                hda_last = hda
                hdb = hdbs[n // 2]
                cidx = t * NST + n
                if chs == 0:
                    hh = P.stage("V", [hda, hdb, l_h[nb]],
                                 lambda e, nb=nb, pb=pb: e.tensor_tensor_scan(
                                     z_h[:, nb * LC:(nb + 1) * LC], z_da[nb][:, :],
                                     z_db[pb][:, nb * LC:(nb + 1) * LC],
                                     0.0, OP.mult, OP.add))
                else:
                    hh = P.stage("V", [hda, hdb, l_h[nb], carry_h.get(cidx)],
                                 lambda e, nb=nb, pb=pb, cidx=cidx:
                                 e.tensor_tensor_scan(
                                     z_h[:, nb * LC:(nb + 1) * LC], z_da[nb][:, :],
                                     z_db[pb][:, nb * LC:(nb + 1) * LC],
                                     carry_sb[:, cidx:cidx + 1], OP.mult, OP.add))
                l_da[nb] = hh
                if nb == 1:
                    l_db[pb] = hh
                hh_pair[nb] = hh
                if n + 2 < NST:
                    hdas[n + 2] = emit_hda(n + 2)
                if nb == 1 and n // 2 + 2 < NST // 2:
                    hdbs[n // 2 + 2] = emit_hdb(n // 2 + 2)
                if nb == 1:
                    htm = P.stage("G", hh_pair + [hs_bcr[NST + n - 1],
                                                  hs_bcr[NST + n], l_tmp[pb]],
                                  lambda e, n=n, pb=pb:
                                  e.tensor_tensor(
                                      z_tmp[pb][:, :], z_h[:, :],
                                      z_bcr[:, (NST + n - 1) * LC:(NST + n + 1) * LC],
                                      OP.mult))
                    l_h[0] = htm
                    l_h[1] = htm
                    last_reader_bcr = htm
                    hs1 = P.stage("G", [htm], lambda e, pb=pb: e.tensor_tensor(
                        z_tmp[pb][:, 0:LC], z_tmp[pb][:, 0:LC],
                        z_tmp[pb][:, LC:2 * LC], OP.add))
                    if n == 1:
                        hy = P.stage("G", [hs1, l_y[ti]], lambda e, ti=ti, pb=pb:
                                     e.tensor_copy(z_y[ti][:, :], z_tmp[pb][:, 0:LC]))
                    else:
                        hy = P.stage("G", [hs1, l_y[ti]], lambda e, ti=ti, pb=pb:
                                     e.tensor_tensor(z_y[ti][:, :], z_y[ti][:, :],
                                                     z_tmp[pb][:, 0:LC], OP.add))
                    l_tmp[pb] = hy
                    l_y[ti] = hy
                    if chs < NCH - 1:
                        cbase = t * NST + n - 1
                        hcar = P.stage("V", hh_pair + [carry_h.get(cbase),
                                                       carry_h.get(cbase + 1)],
                                       lambda e, cbase=cbase: e.tensor_copy(
                                           carry_sb[:, cbase:cbase + 2],
                                           bass.AP(z_h, LC - 1,
                                                   [[2 * LC, 128], [LC, 2]])))
                        carry_h[cbase] = hcar
                        carry_h[cbase + 1] = hcar
                        l_h[0] = [l_h[0], hcar]
                        l_h[1] = [l_h[1], hcar]
            l_dl[ti] = [hda_last, hdu]
            l_du[ti] = hy
            hyd = P.stage("V", [l_y[ti], h_ldx, h_par], lambda e, t=t, ti=ti, bi=bi:
                          e.scalar_tensor_tensor(z_y[ti][:, :],
                                                 z_xin[bi][:, t * LC:(t + 1) * LC],
                                                 s_dvec[:, t:t + 1], z_y[ti][:, :],
                                                 OP.mult, OP.add))
            hcv = P.stage("S", [hyd, l_y16[ti]], lambda e, ti=ti: e.activation(
                z_y16[ti][:, :], z_y[ti][:, :], AF.Copy))
            l_y[ti] = hcv

            def sp_y(e, dma, t=t, ti=ti, chs=chs):
                dma(out=yagin.ap().rearrange("(t p) l -> t p l", p=128)
                    [t, :, chs * LC:(chs + 1) * LC], in_=z_y16[ti][:, :])

            hsp_y = P.dma_stage("Y", [hcv], sp_y, 1, chan=f"ysp{ti}")
            l_y16[ti] = hsp_y
            h_ysp.append(hsp_y)
            l_zxin[bi] = hyd
        l_bcr = last_reader_bcr
        l_zdt[bi] = hdtm

    emit_front(0)
    for ch in range(1, NCH):
        emit_front(ch)
        emit_scan(ch - 1)
    emit_scan(NCH - 1)

    bar_scan = P.barrier()
    sc.close()
    fr.close()
    if PH <= 3:
        return _finish_early()

    cc = es.enter_context(nc.semaphore("sem_cc"))

    def f_ag(e):
        i1 = e.collective_compute("AllGather", OP.bypass,
                                  replica_groups=[list(range(8))],
                                  ins=[yagin.ap()], outs=[yago.ap()])
        i1.then_inc(cc, 1)
        i2 = e.collective_compute("AllGather", OP.bypass,
                                  replica_groups=[list(range(8))],
                                  ins=[eagin.ap()], outs=[eago.ap()])
        return i2

    h_ag = P.stage("G", h_ysp + [h_eout], f_ag)
    h_ag_y = (cc, 1)

    ro = ExitStack()
    rA = ro.enter_context(nc.sbuf_tensor("rA", [128, KD * D], BF16))
    rAT = ro.enter_context(nc.sbuf_tensor("rAT", [128, KD * D], BF16))
    rB = ro.enter_context(nc.sbuf_tensor("rB", [128, KD * D], BF16))
    rC = ro.enter_context(nc.sbuf_tensor("rC", [128, KD * D], BF16))

    def ld_e12(e, dma):
        dma(out=rA[:, :].rearrange("p (k d) -> p k d", k=KD),
            in_=eago.ap()[1 * D:2 * D, :].rearrange("(k p) d -> p k d", p=128))
        dma(out=rB[:, :].rearrange("p (k d) -> p k d", k=KD),
            in_=eago.ap()[2 * D:3 * D, :].rearrange("(k p) d -> p k d", p=128))

    h_lde = P.dma_stage("Y", [h_ag, h_ag_y] + bar_scan, ld_e12, 2, chan="rote")
    h_rAT = transpose_DD(rA, rAT, [h_lde])
    hT1r = mm_DD(rC, rAT, rB, h_rAT + [h_lde])

    h_lde0 = P.dma_stage("Y", h_rAT + [h_lde], lambda e, dma: dma(
        out=rA[:, :].rearrange("p (k d) -> p k d", k=KD),
        in_=eago.ap()[0:D, :].rearrange("(k p) d -> p k d", p=128)), 1, chan="rote")
    h_rAT0 = transpose_DD(rA, rAT, [h_lde0] + hT1r)
    hrot = mm_DD(rB, rAT, rC, h_rAT0 + hT1r)

    h_rot = P.dma_stage("Y", hrot, lambda e, dma: dma(
        out=rotdram.ap().rearrange("(k p) d -> p k d", p=128),
        in_=rB[:, :].rearrange("p (k d) -> p k d", k=KD)), 1, chan="rots")

    bar_rot = P.barrier()
    ro.close()
    if PH <= 4:
        return _finish_early()

    fb_ = ExitStack()
    NWB = 16
    g_w = [fb_.enter_context(nc.sbuf_tensor(f"g_w{i}", [128, 512], BF16))
           for i in range(NWB)]
    s2_sb = fb_.enter_context(nc.sbuf_tensor("s2_sb", [128, KD * BLK], F32))
    xn2 = fb_.enter_context(nc.sbuf_tensor("xn2", [128, KD * BLK], BF16))
    abel = fb_.enter_context(nc.sbuf_tensor("abel", [128, KD * BLK], BF16))
    wbc = fb_.enter_context(nc.sbuf_tensor("wbc", [128, BLK], F32))
    wdf = fb_.enter_context(nc.sbuf_tensor("wdf", [8, BLK], BF16))
    srow = fb_.enter_context(nc.sbuf_tensor("srow", [1, 3 * BLK], BF16))
    frow = fb_.enter_context(nc.sbuf_tensor("frow", [1, 3 * 512], F32))
    scr1 = fb_.enter_context(nc.sbuf_tensor("scr1", [128, 512], F32))
    scr2 = fb_.enter_context(nc.sbuf_tensor("scr2", [128, 512], F32))
    gl8 = fb_.enter_context(nc.sbuf_tensor("gl8", [128, 8], F32))
    gm8 = fb_.enter_context(nc.sbuf_tensor("gm8", [128, 8], F32))
    gs8 = fb_.enter_context(nc.sbuf_tensor("gs8", [128, 8], F32))
    gq8 = fb_.enter_context(nc.sbuf_tensor("gq8", [128, 8], F32))
    gcol = fb_.enter_context(nc.sbuf_tensor("gcol", [128, 4], F32))
    o_stg = fb_.enter_context(nc.sbuf_tensor("o_stg", [128, 2 * 128], F32))
    REGW = max(DFT * BLK + KD * BLK, (KD + 2 * TI) * BLK, KD * D + KD * BLK)
    REG = fb_.enter_context(nc.sbuf_tensor("REG", [128, REGW], BF16))
    g_xn = REG.ap()[:, 0:KD * BLK]
    g_z = REG.ap()[:, KD * BLK:(KD + TI) * BLK]
    g_y = REG.ap()[:, (KD + TI) * BLK:(KD + 2 * TI) * BLK]
    moe = REG.ap()[:, DFT * BLK:DFT * BLK + KD * BLK]
    h_sb = REG.ap()[:, 0:DFT * BLK]
    rot_sb = REG.ap()[:, 0:KD * D]
    nonab = REG.ap()[:, KD * D:KD * D + KD * BLK]

    regs = {}
    dbg2 = {}
    wring_b = WRing(P, g_w, "wb", bar_rot)
    h_blk_prev = bar_rot
    h_out_final = []
    ostg_war = [bar_rot, bar_rot]
    lw = [bar_rot, bar_rot]

    def b_row(q): return srow[0:1, q * SUB:q * SUB + SUB]
    def a_row(q): return srow[0:1, BLK + q * SUB:BLK + q * SUB + SUB]
    def g_row(q): return srow[0:1, 2 * BLK + q * SUB:2 * BLK + q * SUB + SUB]
    f1 = frow[0:1, 0:SUB]
    f2 = frow[0:1, 512:512 + SUB]
    f3 = frow[0:1, 1024:1024 + SUB]

    for j in range(2):
        deps0 = h_blk_prev if isinstance(h_blk_prev, list) else [h_blk_prev]
        hs_xn1 = []
        hs_res = []
        for tt in range(NTC):
            h9 = ln1_tokens(xown, j * BLK + tt * 128, 0, deps0)
            lnwar[0] = [h9]
            for k in range(KD):
                pi = (tt * KD + k) % 2

                def ftr(e, k=k, pi=pi):
                    return e.transpose(ps[pi].ap().bitcast(BF16)[:, 0:128],
                                       xn_tm[0][:, k * 128:(k + 1) * 128],
                                       s_identb[:, :])

                ht = P.stage("T", [h9, psr[pi]], ftr)

                def ftc(e, k=k, tt=tt, pi=pi):
                    return e.tensor_copy(
                        g_xn[:, k * BLK + tt * 128:k * BLK + (tt + 1) * 128],
                        ps[pi].ap().bitcast(BF16)[:, 0:128])

                hc = P.stage("V", [ht] + deps0, ftc)
                psr[pi] = hc
                lnwar[0].append(ht)
                hs_xn1.append(hc)
                pi2 = 2 + (tt * KD + k) % 2

                def ftr2(e, k=k, pi2=pi2):
                    return e.transpose(ps[pi2][:, 0:128],
                                       x_tm[0][:, k * 128:(k + 1) * 128],
                                       s_identf[:, :])

                ht2 = P.stage("T", [h9, psr[pi2]], ftr2)

                def ftc2(e, k=k, tt=tt, pi2=pi2):
                    return e.tensor_copy(
                        s2_sb[:, k * BLK + tt * 128:k * BLK + (tt + 1) * 128],
                        ps[pi2][:, 0:128])

                hc2 = P.stage("V", [ht2] + deps0, ftc2)
                psr[pi2] = hc2
                lnwar[0].append(ht2)
                hs_res.append(hc2)

        if PH <= 5:
            return _finish_early()
        hs_z = {}
        for g in range(XG):
            tiles = [wring_b.load(winz.ap()[k, g], 512) for k in range(KD)]
            hgrps = {}
            for q in range(NSUB):
                base = 4 * (q % 2)

                def fgrp(e, tiles=tiles, q=q, base=base):
                    ins = None
                    for k in range(KD):
                        slot, _ = tiles[k]
                        for m4 in range(4):
                            ins = e.matmul(
                                ps[base + m4][:, 0:SUB],
                                g_w[slot][:, m4 * 128:(m4 + 1) * 128],
                                g_xn[:, k * BLK + q * SUB:k * BLK + q * SUB + SUB],
                                start=(k == 0), stop=(k == KD - 1))
                    return ins

                hgrp = P.stage("T", [h for _, h in tiles] + hs_xn1 +
                               [psr[base + i] for i in range(4)], fgrp)
                hgrps[q] = hgrp
                for slot, _ in tiles:
                    wring_b.used(slot, hgrp)
            for m4 in range(4):
                t = g * 4 + m4
                for q in range(NSUB):
                    base = 4 * (q % 2)
                    pi = base + m4
                    hz = P.stage("S", [hgrps[q]] + deps0,
                                 lambda e, t=t, q=q, pi=pi: e.activation(
                                     g_z[:, t * BLK + q * SUB:t * BLK + q * SUB + SUB],
                                     ps[pi][:, 0:SUB], AF.Silu))
                    psr[pi] = hz
                    hs_z[(t, q)] = hz

        if PH <= 6:
            return _finish_early()
        hs_yg = {}
        h_ldys = []
        for half in range(2):
            def ldy(e, dma, half=half, j=j):
                if "yreg" not in regs:
                    regs["yreg"] = e.alloc_register("yoff_reg")
                rr = regs["yreg"]
                e.reg_load(rr, s_yoff[0:1, j * 2 + half:j * 2 + half + 1])
                src = bass.AP(yago, rr, [[L, 128], [128 * L, TS], [1, BLK]])
                dma(out=g_y[:, half * TS * BLK:(half + 1) * TS * BLK]
                    .rearrange("p (t l) -> p t l", t=TS), in_=src)

            h_ldys.append(P.dma_stage("G", [h_ag, h_ag_y, h_par] + deps0, ldy, 1,
                                      chan="ldy"))
        hly = h_ldys[-1]
        for t in range(TI):
            for q in range(NSUB):
                hyg = P.stage("V", [hly, hs_z[(t, q)]], lambda e, t=t, q=q:
                              e.tensor_tensor(
                                  g_y[:, t * BLK + q * SUB:t * BLK + q * SUB + SUB],
                                  g_y[:, t * BLK + q * SUB:t * BLK + q * SUB + SUB],
                                  g_z[:, t * BLK + q * SUB:t * BLK + q * SUB + SUB],
                                  OP.mult))
                hs_yg[(t, q)] = hyg

        if PH <= 7:
            return _finish_early()
        hs_s2 = {}
        for g in range(KD // MGA):
            tiles = [wring_b.load(wout.ap()[k, g], MGA * 128) for k in range(TI)]
            hgrps = {}
            for q in range(NSUB):
                base = 4 * (q % 2)

                def fgrp(e, tiles=tiles, q=q, base=base):
                    ins = None
                    for k in range(TI):
                        slot, _ = tiles[k]
                        for m4 in range(MGA):
                            ins = e.matmul(
                                ps[base + m4][:, 0:SUB],
                                g_w[slot][:, m4 * 128:(m4 + 1) * 128],
                                g_y[:, k * BLK + q * SUB:k * BLK + q * SUB + SUB],
                                start=(k == 0), stop=(k == TI - 1))
                    return ins

                hgrp = P.stage("T", [h for _, h in tiles] +
                               [hs_yg[(k, q)] for k in range(TI)] +
                               [psr[base + i] for i in range(4)], fgrp)
                hgrps[q] = hgrp
            for slot, _ in tiles:
                wring_b.used(slot, [hgrps[q] for q in range(NSUB)])
            for m4 in range(MGA):
                for q in range(NSUB):
                    base = 4 * (q % 2)
                    pi = base + m4
                    m = g * MGA + m4
                    hadd = P.stage("V", [hgrps[q]] + hs_res,
                                   lambda e, m=m, q=q, pi=pi: e.tensor_tensor(
                                       s2_sb[:, m * BLK + q * SUB:m * BLK + q * SUB + SUB],
                                       s2_sb[:, m * BLK + q * SUB:m * BLK + q * SUB + SUB],
                                       ps[pi][:, 0:SUB], OP.add))
                    psr[pi] = hadd
                    hs_s2[(m, q)] = hadd
        hs_s2_all = list(hs_s2.values())

        def sp_s2(e, dma, j=j):
            for k in range(KD):
                dma(out=s2dram.ap().rearrange("(j k p) l -> j k p l", k=KD, p=128)[j, k],
                    in_=s2_sb[:, k * BLK:(k + 1) * BLK])

        h_s2sp = P.dma_stage("Y", hs_s2_all, sp_s2, KD, chan="s2s")

        if PH <= 8:
            return _finish_early()
        hs_xn2 = {}
        h_ab = []
        l_frow = deps0
        l_srow = deps0
        for q in range(NSUB):
            def fsum(e, q=q):
                ins = None
                for k in range(KD):
                    ins = e.matmul(ps[4][0:1, 0:SUB], s_onescol[:, 0:1],
                                   s2_sb[:, k * BLK + q * SUB:k * BLK + q * SUB + SUB],
                                   start=(k == 0), stop=(k == KD - 1))
                return ins

            hsum = P.stage("T", hs_s2_all + [psr[4]], fsum)
            hsq_mm = None
            for k in range(KD):
                hq = P.stage("S", [hs_s2[(k, q)], ln_state["last"]],
                             lambda e, k=k, q=q: e.activation(
                                 s_sq[:, 0:SUB],
                                 s2_sb[:, k * BLK + q * SUB:k * BLK + q * SUB + SUB],
                                 AF.Square))
                ln_state["last"] = hq

                def fq(e, k=k):
                    return e.matmul(ps[5][0:1, 0:SUB], s_onescol[:, 0:1],
                                    s_sq[:, 0:SUB],
                                    start=(k == 0), stop=(k == KD - 1))

                hsq_mm = P.stage("T", [hq] + ([psr[5]] if k == 0 else []), fq)
                ln_state["last"] = hsq_mm
            h1 = P.stage("V", [hsum, l_frow], lambda e: e.tensor_scalar(
                f1, ps[4][0:1, 0:SUB], 1.0 / D, None, OP.mult))
            psr[4] = h1
            h2 = P.stage("V", [hsq_mm, h1], lambda e: e.tensor_scalar(
                f2, ps[5][0:1, 0:SUB], 1.0 / D, None, OP.mult))
            psr[5] = h2
            h3 = P.stage("V", [h2], lambda e: e.tensor_tensor(
                f3, f1, f1, OP.mult))
            h4 = P.stage("V", [h3], lambda e: e.tensor_tensor(
                f2, f2, f3, OP.subtract))
            h5 = P.stage("S", [h4, h_par], lambda e: e.activation(
                f2, f2, AF.Ln, bias=c.EPS_LN))
            h6s = P.stage("S", [h5], lambda e: e.activation(
                f2, f2, AF.Exp, scale=-0.5))
            h7 = P.stage("V", [h6s], lambda e: e.tensor_tensor(
                f3, f1, f2, OP.mult))
            h8v = P.stage("V", [h7], lambda e: e.tensor_scalar(
                f3, f3, -1.0, None, OP.mult))
            h6 = P.stage("V", [h6s, l_srow], lambda e, q=q: e.tensor_copy(
                a_row(q), f2))
            h8 = P.stage("V", [h8v], lambda e, q=q: e.tensor_copy(
                b_row(q), f3))
            l_frow = [h6, h8]
            l_srow = [h6, h8]
            h_ab.append((h6, h8))

            hba = P.stage("T", [h6, psr[4]], lambda e, q=q: e.matmul(
                ps[4][:, 0:SUB], s_onesb[0:1, :], a_row(q), start=True, stop=True))
            hbb = P.stage("T", [h8, psr[5]], lambda e, q=q: e.matmul(
                ps[5][:, 0:SUB], s_onesb[0:1, :], b_row(q), start=True, stop=True))
            l_scr1 = None
            for k in range(KD):
                ha1 = P.stage("V", [hba, hs_s2[(k, q)], l_scr1],
                              lambda e, k=k, q=q: e.tensor_tensor(
                                  scr1[:, 0:SUB],
                                  s2_sb[:, k * BLK + q * SUB:k * BLK + q * SUB + SUB],
                                  ps[4][:, 0:SUB], OP.mult))
                ha2 = P.stage("V", [ha1, hbb], lambda e: e.tensor_tensor(
                    scr1[:, 0:SUB], scr1[:, 0:SUB], ps[5][:, 0:SUB], OP.add))
                ha3 = P.stage("V", [ha2], lambda e, k=k, q=q: e.tensor_scalar(
                    xn2[:, k * BLK + q * SUB:k * BLK + q * SUB + SUB],
                    scr1[:, 0:SUB], s_lng[:, k:k + 1], s_lnb[:, k:k + 1],
                    OP.mult, OP.add))
                l_scr1 = ha3
                psr[4] = ha3
                psr[5] = ha3
                hs_xn2[(k, q)] = ha3
        hs_xn2_all = list(hs_xn2.values())

        if PH <= 9:
            return _finish_early()
        if os.environ.get("KDEBUG2") and j == 0:
            dbg2["xn2"] = nc.dram_tensor("dbg2_xn2", [128, KD * BLK], BF16,
                                         kind="ExternalOutput")
            P.dma_stage("Y", hs_xn2_all, lambda e, dma: dma(
                out=dbg2["xn2"].ap(), in_=xn2[:, :]), 1, chan="dbg")
        hs_wd = []
        l_g8 = None
        for tc in range(NTC):
            def flog(e, tc=tc):
                ins = None
                for k in range(KD):
                    ins = e.matmul(ps[6][0:128, 0:E],
                                   xn2[:, k * BLK + tc * 128:k * BLK + (tc + 1) * 128],
                                   s_gatew[:, k * E:(k + 1) * E],
                                   start=(k == 0), stop=(k == KD - 1))
                return ins

            hlg = P.stage("T", hs_xn2_all + [psr[6], h_parc], flog)
            hl1 = P.stage("V", [hlg, h_gbbc, l_g8], lambda e: e.tensor_tensor(
                gl8[:, :], ps[6][0:128, 0:E], s_gatebbc[:, :], OP.add))
            psr[6] = hl1
            hm8 = P.stage("V", [hl1], lambda e: e.max(out=gm8[:, :], in_=gl8[:, :]))
            hms = P.stage("V", [hm8], lambda e: e.memset(gs8[:, :], -1e38))
            hcp = P.stage("V", [hms], lambda e: e.tensor_copy(gs8[:, 0:2], gm8[:, 0:2]))
            hmr = P.stage("V", [hcp], lambda e: e.match_replace(
                out=gq8[:, :], in_to_replace=gs8[:, :], in_values=gl8[:, :],
                imm_value=-1e30))
            hmask = P.stage("V", [hmr], lambda e: e.tensor_scalar(
                gq8[:, :], gq8[:, :], -1e30, None, OP.is_equal))
            hs12 = P.stage("V", [hm8], lambda e: e.tensor_tensor(
                gcol[:, 0:1], gm8[:, 0:1], gm8[:, 1:2], OP.add))
            harg = P.stage("V", [hs12, hl1, hmr], lambda e: e.tensor_scalar(
                gl8[:, :], gl8[:, :], 2.0, gcol[:, 0:1], OP.mult, OP.subtract))
            hsg = P.stage("S", [harg], lambda e: e.activation(
                gl8[:, :], gl8[:, :], AF.Sigmoid))
            hwd = P.stage("V", [hsg, hmask], lambda e: e.tensor_tensor(
                gq8[:, :], gq8[:, :], gl8[:, :], OP.mult))
            htr = P.stage("T", [hwd, psr[7]], lambda e: e.transpose(
                ps[7][0:8, 0:128], gq8[:, :], s_identf[:, :]))
            hwf = P.stage("V", [htr], lambda e, tc=tc: e.tensor_copy(
                wdf[:, tc * 128:(tc + 1) * 128], ps[7][0:8, 0:128]))
            psr[7] = hwf
            l_g8 = hwd
            hs_wd.append(hwf)

        if PH <= 10:
            return _finish_early()
        if os.environ.get("KDEBUG2") and j == 0:
            dbg2["wdf"] = nc.dram_tensor("dbg2_wdf", [8, BLK], BF16,
                                         kind="ExternalOutput")
            P.dma_stage("Y", hs_wd, lambda e, dma: dma(
                out=dbg2["wdf"].ap(), in_=wdf[:, :]), 1, chan="dbg")
        h_moe = {}
        for m in range(KD):
            for q in range(NSUB):
                pi = 6 + (m * NSUB + q) % 2

                def fb2(e, m=m, q=q, pi=pi):
                    return e.matmul(ps[pi][:, 0:SUB],
                                    s_b2[0:8, m * 128:(m + 1) * 128],
                                    wdf[0:8, q * SUB:q * SUB + SUB],
                                    start=True, stop=True)

                hb2 = P.stage("T", hs_wd + [psr[pi], h_par], fb2)
                hmi = P.stage("V", [hb2], lambda e, m=m, q=q, pi=pi: e.tensor_copy(
                    moe[:, m * BLK + q * SUB:m * BLK + q * SUB + SUB],
                    ps[pi][:, 0:SUB]))
                psr[pi] = hmi
                h_moe[(m, q)] = hmi

        if PH <= 11:
            return _finish_early()
        hs_abel = {}
        for g in range(KD // MGA):
            tiles = [wring_b.load(abelw.ap()[k, g], MGA * 128) for k in range(KD)]
            hgrps = {}
            for q in range(NSUB):
                base = 4 * (q % 2)

                def fgrp(e, tiles=tiles, q=q, base=base):
                    ins = None
                    for k in range(KD):
                        slot, _ = tiles[k]
                        for m4 in range(MGA):
                            ins = e.matmul(
                                ps[base + m4][:, 0:SUB],
                                g_w[slot][:, m4 * 128:(m4 + 1) * 128],
                                xn2[:, k * BLK + q * SUB:k * BLK + q * SUB + SUB],
                                start=(k == 0), stop=(k == KD - 1))
                    return ins

                hgrp = P.stage("T", [h for _, h in tiles] + hs_xn2_all +
                               [psr[base + i] for i in range(4)], fgrp)
                hgrps[q] = hgrp
            for slot, _ in tiles:
                wring_b.used(slot, [hgrps[q] for q in range(NSUB)])
            for m4 in range(MGA):
                for q in range(NSUB):
                    base = 4 * (q % 2)
                    pi = base + m4
                    m = g * MGA + m4
                    hab = P.stage("S", [hgrps[q]], lambda e, m=m, q=q, pi=pi:
                                  e.activation(
                                      abel[:, m * BLK + q * SUB:m * BLK + q * SUB + SUB],
                                      ps[pi][:, 0:SUB], AF.Gelu,
                                      bias=s_abelb[:, m:m + 1]))
                    psr[pi] = hab
                    hs_abel[(m, q)] = hab

        if PH <= 12:
            return _finish_early()
        if os.environ.get("KDEBUG2") and j == 0:
            dbg2["abel"] = nc.dram_tensor("dbg2_abel", [128, KD * BLK], BF16,
                                          kind="ExternalOutput")
            P.dma_stage("Y", list(hs_abel.values()), lambda e, dma: dma(
                out=dbg2["abel"].ap(), in_=abel[:, :]), 1, chan="dbg")
        hs_h_last = {}
        l_wbc = None
        l_scr2 = None
        for ex in range(E):
            hs_wbc = []
            for q in range(NSUB):
                pi = 6 + q % 2

                def fwb(e, ex=ex, q=q, pi=pi):
                    return e.matmul(ps[pi][:, 0:SUB],
                                    s_sel8[:, ex * 128:(ex + 1) * 128],
                                    wdf[0:8, q * SUB:q * SUB + SUB],
                                    start=True, stop=True)

                hw1 = P.stage("T", hs_wd + [psr[pi], h_par], fwb)
                hw2 = P.stage("V", [hw1, l_wbc], lambda e, q=q, pi=pi: e.tensor_copy(
                    wbc[:, q * SUB:q * SUB + SUB], ps[pi][:, 0:SUB]))
                psr[pi] = hw2
                hs_wbc.append(hw2)

            hs_h = {}
            for mg in range(MG1):
                base = 4 * ((mg + 1) % 2)
                tiles = [wring_b.load(w1t.ap()[ex, k, mg], 256) for k in range(KD)]

                def fgrp(e, tiles=tiles, base=base):
                    ins = None
                    for k in range(KD):
                        slot, _ = tiles[k]
                        for d2 in range(2):
                            for q in range(NSUB):
                                ins = e.matmul(
                                    ps[base + (d2 * NSUB + q) % 4][:, 0:SUB],
                                    g_w[slot][:, d2 * 128:(d2 + 1) * 128],
                                    xn2[:, k * BLK + q * SUB:k * BLK + q * SUB + SUB],
                                    start=(k == 0), stop=(k == KD - 1))
                    return ins

                hgrp = P.stage("T", [h for _, h in tiles] + hs_xn2_all +
                               [psr[base + i] for i in range(4)], fgrp)
                for slot, _ in tiles:
                    wring_b.used(slot, hgrp)
                for d2 in range(2):
                    for q in range(NSUB):
                        pi = base + (d2 * NSUB + q) % 4
                        df = mg * 2 + d2
                        dep_h = hs_h_last.get((df, q))
                        hh = P.stage("S", [hgrp, dep_h],
                                     lambda e, df=df, q=q, pi=pi, ex=ex:
                                     e.activation(
                                         h_sb[:, df * BLK + q * SUB:df * BLK + q * SUB + SUB],
                                         ps[pi][:, 0:SUB], AF.Gelu,
                                         bias=s_b1[:, ex * DFT + df:ex * DFT + df + 1]))
                        psr[pi] = hh
                        hs_h[(df, q)] = hh

            DFG = min(8, DFT)
            for dg in range(KD // DG2):
                hlast = {q: None for q in range(NSUB)}
                for dfg in range(DFT // DFG):
                    tiles = [wring_b.load(w2t.ap()[ex, dg, dfg * DFG + i],
                                          DG2 * 128) for i in range(DFG)]
                    first = dfg == 0
                    for q in range(NSUB):
                        base = 4 * (q % 2)
                        dep_h = [hs_h[(dfg * DFG + i, q)] for i in range(DFG)]

                        def fgrp(e, tiles=tiles, dfg=dfg, q=q, base=base):
                            ins = None
                            for i in range(DFG):
                                df = dfg * DFG + i
                                slot, _ = tiles[i]
                                for d4 in range(DG2):
                                    ins = e.matmul(
                                        ps[base + d4][:, 0:SUB],
                                        g_w[slot][:, d4 * 128:(d4 + 1) * 128],
                                        h_sb[:, df * BLK + q * SUB:df * BLK + q * SUB + SUB],
                                        start=(df == 0), stop=(df == DFT - 1))
                            return ins

                        deps_mm = [h for _, h in tiles] + dep_h +                             ([psr[base + i] for i in range(4)] if first
                             else [hlast[q]])
                        hlast[q] = P.stage("T", deps_mm, fgrp)
                    for slot, _ in tiles:
                        wring_b.used(slot, [hlast[q] for q in range(NSUB)])
                for d4 in range(DG2):
                    for q in range(NSUB):
                        base = 4 * (q % 2)
                        pi = base + d4
                        m = dg * DG2 + d4
                        he1 = P.stage("V", [hlast[q], l_scr2] + hs_wbc,
                                      lambda e, q=q, pi=pi: e.tensor_tensor(
                                          scr2[:, 0:SUB], ps[pi][:, 0:SUB],
                                          wbc[:, q * SUB:q * SUB + SUB],
                                          OP.mult))
                        psr[pi] = he1
                        he2 = P.stage("G", [he1, h_moe[(m, q)]],
                                      lambda e, m=m, q=q: e.tensor_tensor(
                                          moe[:, m * BLK + q * SUB:m * BLK + q * SUB + SUB],
                                          moe[:, m * BLK + q * SUB:m * BLK + q * SUB + SUB],
                                          scr2[:, 0:SUB], OP.add))
                        h_moe[(m, q)] = he2
                        l_scr2 = he2
            hs_h_last = hs_h
            l_wbc = he1
        hs_moe_all = list(h_moe.values())
        hs_h_reads = list(hs_h_last.values())

        if PH <= 13:
            return _finish_early()
        if os.environ.get("KDEBUG2") and j == 0:
            dbg2["moe"] = nc.dram_tensor("dbg2_moe", [128, KD * BLK], BF16,
                                         kind="ExternalOutput")
            P.dma_stage("Y", hs_moe_all, lambda e, dma: dma(
                out=dbg2["moe"].ap(), in_=moe), 1, chan="dbg")
        h_ldrot = P.dma_stage("G", hs_moe_all + [h_rot] + hs_h_reads +
                              [lw[0], lw[1]], ld_rot_fn(rot_sb, rotdram, KD), 1, chan="ldrot")
        hs_nonab = {}
        for m in range(KD):
            for q in range(NSUB):
                pi = 4 + (m * NSUB + q) % 4

                def fmm(e, m=m, q=q, pi=pi):
                    ins = None
                    for k in range(KD):
                        ins = e.matmul(ps[pi][:, 0:SUB],
                                       rot_sb[:, k * D + m * 128:k * D + (m + 1) * 128],
                                       moe[:, k * BLK + q * SUB:k * BLK + q * SUB + SUB],
                                       start=(k == 0), stop=(k == KD - 1))
                    return ins

                hmm = P.stage("T", [h_ldrot, psr[pi]] + hs_moe_all, fmm)
                hnb = P.stage("V", [hmm], lambda e, m=m, q=q, pi=pi: e.tensor_copy(
                    nonab[:, m * BLK + q * SUB:m * BLK + q * SUB + SUB],
                    ps[pi][:, 0:SUB]))
                psr[pi] = hnb
                hs_nonab[(m, q)] = hnb

        if PH <= 14:
            return _finish_early()
        if os.environ.get("KDEBUG2") and j == 0:
            dbg2["nonab"] = nc.dram_tensor("dbg2_nonab", [128, KD * BLK], BF16,
                                           kind="ExternalOutput")
            P.dma_stage("Y", list(hs_nonab.values()), lambda e, dma: dma(
                out=dbg2["nonab"].ap(), in_=nonab), 1, chan="dbg")
        hs_g = []
        for q in range(NSUB):
            def fmx(e, q=q):
                ins = None
                for k in range(KD):
                    ins = e.matmul(ps[6][0:1, 0:SUB], s_mixw[:, k:k + 1],
                                   abel[:, k * BLK + q * SUB:k * BLK + q * SUB + SUB],
                                   start=(k == 0), stop=False)
                for k in range(KD):
                    ins = e.matmul(ps[6][0:1, 0:SUB], s_mixw[:, KD + k:KD + k + 1],
                                   nonab[:, k * BLK + q * SUB:k * BLK + q * SUB + SUB],
                                   start=False, stop=(k == KD - 1))
                return ins

            dep_ab = [hs_abel[(k, q)] for k in range(KD)]
            dep_nb = [hs_nonab[(k, q)] for k in range(KD)]
            hmx = P.stage("T", dep_ab + dep_nb + [psr[6], h_parc], fmx)
            hg1 = P.stage("V", [hmx, l_srow], lambda e, q=q: e.tensor_scalar(
                g_row(q), ps[6][0:1, 0:SUB], s_mixb[0:1, 0:1], None, OP.add))
            psr[6] = hg1
            hg2 = P.stage("S", [hg1], lambda e, q=q: e.activation(
                g_row(q), g_row(q), AF.Sigmoid))
            l_srow = [hg2]
            hs_g.append(hg2)

        if PH <= 15:
            return _finish_early()
        h_lds2 = P.dma_stage("Y", [h_s2sp] + hs_xn2_all + [ln_state["last"]],
                             lambda e, dma, j=j: [
                                 dma(out=s2_sb[:, k * BLK:(k + 1) * BLK],
                                     in_=s2dram.ap().rearrange(
                                         "(j k p) l -> j k p l", k=KD, p=128)[j, k])
                                 for k in range(KD)] and None or None, KD,
                             chan="s2s")

        out_tiles = {}
        h_out_dmas = []
        l_scr1f = None
        l_scr2f = l_scr2
        for q in range(NSUB):
            (h6a, h8b) = h_ab[q]
            hba = P.stage("T", [h6a, psr[4]], lambda e, q=q: e.matmul(
                ps[4][:, 0:SUB], s_onesb[0:1, :], a_row(q), start=True, stop=True))
            hbb = P.stage("T", [h8b, psr[5]], lambda e, q=q: e.matmul(
                ps[5][:, 0:SUB], s_onesb[0:1, :], b_row(q), start=True, stop=True))
            hgb = P.stage("T", [hs_g[q], psr[6]], lambda e, q=q: e.matmul(
                ps[6][:, 0:SUB], s_onesb[0:1, :], g_row(q), start=True, stop=True))
            for k in range(KD):
                h1 = P.stage("V", [hs_nonab[(k, q)], hs_abel[(k, q)], l_scr1f],
                             lambda e, k=k, q=q: e.tensor_tensor(
                                 scr1[:, 0:SUB],
                                 nonab[:, k * BLK + q * SUB:k * BLK + q * SUB + SUB],
                                 abel[:, k * BLK + q * SUB:k * BLK + q * SUB + SUB],
                                 OP.subtract))
                h2 = P.stage("V", [h1, hgb], lambda e: e.tensor_tensor(
                    scr1[:, 0:SUB], scr1[:, 0:SUB], ps[6][:, 0:SUB], OP.mult))
                psr[6] = h2
                h3 = P.stage("V", [h_lds2, hba, l_scr2f], lambda e, k=k, q=q:
                             e.tensor_tensor(
                                 scr2[:, 0:SUB],
                                 s2_sb[:, k * BLK + q * SUB:k * BLK + q * SUB + SUB],
                                 ps[4][:, 0:SUB], OP.mult))
                psr[4] = h3
                h4 = P.stage("V", [h3, hbb], lambda e: e.tensor_tensor(
                    scr2[:, 0:SUB], scr2[:, 0:SUB], ps[5][:, 0:SUB], OP.add))
                psr[5] = h4
                h5 = P.stage("V", [h4], lambda e, k=k, q=q: e.tensor_tensor(
                    scr2[:, 0:SUB], scr2[:, 0:SUB],
                    s2_sb[:, k * BLK + q * SUB:k * BLK + q * SUB + SUB], OP.add))
                h6 = P.stage("V", [h5], lambda e, k=k, q=q: e.tensor_tensor(
                    scr2[:, 0:SUB], scr2[:, 0:SUB],
                    abel[:, k * BLK + q * SUB:k * BLK + q * SUB + SUB], OP.add))
                h7 = P.stage("V", [h6, h2], lambda e: e.tensor_tensor(
                    scr2[:, 0:SUB], scr2[:, 0:SUB], scr1[:, 0:SUB], OP.add))
                l_scr1f = h7
                l_scr2f = h7
                for pz in range(SUB // 128):
                    htr = P.stage("T", [h7, psr[7]], lambda e, pz=pz: e.transpose(
                        ps[7][:, 0:128], scr2[:, pz * 128:(pz + 1) * 128],
                        s_identf[:, :]))
                    tglob = q * (SUB // 128) + pz
                    og = (k * (SUB // 128) + pz) % 2

                    def fst(e, og=og):
                        return e.tensor_copy(o_stg[:, og * 128:(og + 1) * 128],
                                             ps[7][:, 0:128])

                    hst = P.stage("V", [htr, ostg_war[og]], fst)
                    psr[7] = hst

                    def st_out1(e, dma, j=j, k=k, tglob=tglob, og=og):
                        dma(out=yout.ap()[j * BLK + tglob * 128:
                                          j * BLK + (tglob + 1) * 128,
                                          k * 128:(k + 1) * 128],
                            in_=o_stg[:, og * 128:(og + 1) * 128])

                    h_o1 = P.dma_stage("Y", [hst], st_out1, 1, chan=f"out{og}")
                    ostg_war[og] = h_o1
                    h_out_dmas.append(h_o1)
        h_out_final.extend(h_out_dmas)
        h_blk_prev = P.barrier()

    if os.environ.get("KDEBUG"):
        dbg_specs = [("xin_sc", xin_sc), ("dt_dram", dt_dram), ("bc_dram", bc_dram),
                     ("yagin", yagin), ("eagin", eagin), ("s2dram", s2dram),
                     ("rotdram", rotdram), ("yago8", yago)]
        dbg_outs = {}
        for nm, t_ in dbg_specs:
            dbg_outs[nm] = nc.dram_tensor(
                "dbg_" + nm, list(t_.shape), t_.dtype, kind="ExternalOutput")

        def f_dbg(e, dma):
            for nm, t_ in dbg_specs:
                dma(out=dbg_outs[nm].ap(), in_=t_.ap())

        h_dbg = P.dma_stage("G", P.barrier(), f_dbg, len(dbg_specs), chan="dbg")
        h_out_final = h_out_final + [h_dbg]

    P.stage("G", h_out_final + [h_par], lambda e: e.memset(s_fin[0:1, 0:1], 0.0))

    fb_.close()

    with nc.Block() as block:
        P.emit(block)
    es.close()
    return nc


def ld_rot_fn(rot_sb, rotdram, KD):
    def f(e, dma):
        dma(out=rot_sb.rearrange("p (k d) -> p k d", k=KD),
            in_=rotdram.ap().rearrange("(k p) d -> p k d", p=128))
    return f



def stage_core_inputs(cfg: Cfg, inp: dict, core: int):
    c = cfg
    KD, TI, TS, NST, DTR = c.KD, c.TI, c.TS, c.NST, c.DTR
    D, DI, L, E, DF = c.D, c.DI, c.L, c.E, c.DF
    MGA, DG2, MG1, BLK = c.MGA, c.DG2, c.MG1, c.BLK
    XG = c.XG
    b, q = core // 2, core % 2
    f32 = np.float32

    half = DI // 2
    if q == 0:
        sig = np.arange(DI)
    else:
        sig = np.concatenate([np.arange(half, DI), np.arange(0, half)])

    x = np.asarray(inp["L0_attn_x"], f32)
    xfull = np.ascontiguousarray(x[b])
    blkpos = [q * BLK, 2 * BLK + q * BLK]
    xown = np.ascontiguousarray(
        np.concatenate([x[b, blkpos[0]:blkpos[0] + BLK],
                        x[b, blkpos[1]:blkpos[1] + BLK]], axis=0))

    ipw = np.asarray(inp["L0_attn_in_proj_w"], f32)
    wx = ipw[:DI][sig]
    wz = ipw[DI:][sig]
    winx = np.ascontiguousarray(wx.T.reshape(KD, 128, XG, 512).transpose(0, 2, 1, 3))
    winz = np.ascontiguousarray(wz.T.reshape(KD, 128, XG, 512).transpose(0, 2, 1, 3))

    opw = np.asarray(inp["L0_attn_out_proj_w"], f32)
    wout = np.ascontiguousarray(
        opw[:, sig].T.reshape(TI, 128, KD // MGA, MGA * 128).transpose(0, 2, 1, 3))
    XP = DTR + 2 * NST
    xprojt = np.ascontiguousarray(
        np.asarray(inp["L0_attn_x_proj_w"], f32)[:, sig].T.reshape(TI, 128, XP))
    dtprojt = np.ascontiguousarray(
        np.asarray(inp["L0_attn_dt_proj_w"], f32)[sig][:TS * 128].T)
    cw = np.asarray(inp["L0_attn_conv_w"], f32)[sig, 0, :]
    convw = np.ascontiguousarray(
        cw.reshape(TI, 128, c.DCONV).transpose(1, 0, 2).reshape(128, TI * c.DCONV))
    convb = np.ascontiguousarray(
        np.asarray(inp["L0_attn_conv_b"], f32)[sig].reshape(TI, 128).T)
    dvec = np.ascontiguousarray(
        np.asarray(inp["L0_attn_D"], f32)[sig][:TS * 128].reshape(TS, 128).T)
    al = np.asarray(inp["L0_attn_A_log"], f32)[sig][:TS * 128]
    alog = np.ascontiguousarray(
        al.reshape(TS, 128, NST).transpose(1, 0, 2).reshape(128, TS * NST))
    dtb = np.ascontiguousarray(
        np.asarray(inp["L0_attn_dt_proj_b"], f32)[sig][:TS * 128].reshape(TS, 128).T)

    lng = np.ascontiguousarray(np.asarray(inp["L0_ffn_ln_g"], f32).reshape(KD, 128).T)
    lnb = np.ascontiguousarray(np.asarray(inp["L0_ffn_ln_b"], f32).reshape(KD, 128).T)
    abelb = np.ascontiguousarray(
        np.asarray(inp["L0_ffn_abel_b"], f32).reshape(KD, 128).T)
    abelw = np.ascontiguousarray(
        np.asarray(inp["L0_ffn_abel_w"], f32)
        .reshape(KD, 128, KD // MGA, MGA * 128).transpose(0, 2, 1, 3))
    gatew = np.ascontiguousarray(
        np.asarray(inp["L0_ffn_gate_w_moe"], f32).reshape(KD, 128, E))
    gateb = np.asarray(inp["L0_ffn_gate_b_moe"], f32).reshape(1, E)
    mixw = np.ascontiguousarray(
        np.asarray(inp["L0_ffn_mix_w"], f32)[:, 0].reshape(2 * KD, 128).T)
    mixb = np.asarray(inp["L0_ffn_mix_b"], f32).reshape(1, 1)
    w1t = np.ascontiguousarray(
        np.asarray(inp["L0_ffn_w1"], f32)
        .reshape(E, KD, 128, MG1, 256).transpose(0, 1, 3, 2, 4))
    w2t = np.ascontiguousarray(
        np.asarray(inp["L0_ffn_w2"], f32)
        .reshape(E, c.DFT, 128, KD // DG2, DG2 * 128).transpose(0, 3, 1, 2, 4))
    b1v = np.ascontiguousarray(
        np.asarray(inp["L0_ffn_b1"], f32)
        .reshape(E, c.DFT, 128).transpose(2, 0, 1).reshape(128, E * c.DFT))
    b2v = np.ascontiguousarray(np.asarray(inp["L0_ffn_b2"], f32))
    gens = np.asarray(inp["L0_ffn_gens"], f32)
    gmat = np.ascontiguousarray(gens[core % 3])
    gmatT = np.ascontiguousarray(gmat.T)

    identf = np.eye(128, dtype=f32)
    identb = np.eye(128, dtype=BF)
    onesf = np.ones((1, 128), dtype=f32)
    sel32 = np.zeros((32, 32 * 128), dtype=BF)
    for n in range(32):
        sel32[n, n * 128:(n + 1) * 128] = 1
    sel8 = np.zeros((8, 8 * 128), dtype=f32)
    for n in range(8):
        sel8[n, n * 128:(n + 1) * 128] = 1

    yoff = np.zeros((1, 4), dtype=np.int32)
    part = core ^ 1
    for j in range(2):
        for half, r in enumerate([core, part]):
            yoff[0, j * 2 + half] = (r * TS * 128) * L + blkpos[j]

    return {
        "xfull": xfull, "xown": xown, "winx": winx, "winz": winz, "wout": wout,
        "xprojt": xprojt, "dtprojt": dtprojt, "convw": convw, "convb": convb,
        "dvec": dvec, "alog": alog, "dtb": dtb, "lng": lng, "lnb": lnb,
        "abelb": abelb, "abelw": abelw, "gatew": gatew, "gateb": gateb,
        "mixw": mixw, "mixb": mixb, "w1t": w1t, "w2t": w2t, "b1v": b1v,
        "b2v": b2v, "gmat": gmat, "gmatT": gmatT, "identf": identf,
        "identb": identb, "onesf": onesf, "onescol": np.ones((128, 1), dtype=f32),
        "onesb": np.ones((1, 128), dtype=BF),
        "cvals": np.tile(np.array([[c.EPS_LN, -0.5, -1.0, 1.0]], dtype=f32), (128, 1)),
        "sel32": sel32, "sel8": sel8,
        "yoff": yoff,
    }


def assemble_output(cfg: Cfg, results):
    c = cfg
    out = np.zeros((4, c.L, c.D), dtype=np.float32)
    for core in range(8):
        b, q = core // 2, core % 2
        yo = np.asarray(results[core]["yout"])
        blkpos = [q * c.BLK, 2 * c.BLK + q * c.BLK]
        out[b, blkpos[0]:blkpos[0] + c.BLK] = yo[:c.BLK]
        out[b, blkpos[1]:blkpos[1] + c.BLK] = yo[c.BLK:]
    return out


def kernel(**inputs):
    from concourse.bass_utils import run_bass_kernel_spmd
    cfg = FULL
    nc = build_program(cfg)
    in_maps = [stage_core_inputs(cfg, inputs, core) for core in range(8)]
    res = run_bass_kernel_spmd(nc, in_maps, core_ids=list(range(8)))
    return assemble_output(cfg, res.results)


# version 11
# speedup vs baseline: 1.0950x; 1.0950x over previous
import os
from dataclasses import dataclass
from contextlib import ExitStack

import numpy as np
import ml_dtypes

import concourse.bass as bass
import concourse.mybir as mybir
from concourse.alu_op_type import AluOpType as OP

F32 = mybir.dt.float32
BF16 = mybir.dt.bfloat16
I32 = mybir.dt.int32
AF = mybir.ActivationFunctionType
AX = mybir.AxisListType.X
BF = ml_dtypes.bfloat16


@dataclass
class Cfg:
    D: int = 1024
    DI: int = 2048
    NST: int = 16
    DTR: int = 64
    DCONV: int = 4
    L: int = 4096
    E: int = 8
    DF: int = 4096
    LC: int = 512
    EPS_LN: float = 1e-5

    @property
    def KD(self): return self.D // 128
    @property
    def TI(self): return self.DI // 128
    @property
    def TS(self): return self.TI // 2
    @property
    def NCH(self): return self.L // self.LC
    @property
    def OWN(self): return self.L // 2
    @property
    def BLK(self): return self.OWN // 2
    @property
    def NSUB(self): return max(1, self.BLK // 512)
    @property
    def SUB(self): return min(512, self.BLK)
    @property
    def DFT(self): return self.DF // 128
    @property
    def MGA(self): return min(4, self.KD)
    @property
    def DG2(self): return min(4, self.KD)
    @property
    def MG1(self): return self.DF // 256
    @property
    def XG(self): return self.TI // 4
    @property
    def NTC(self): return self.BLK // 128
    @property
    def NTL(self): return self.LC // 128


FULL = Cfg()
SMALL = Cfg(D=256, DI=512, DTR=32, L=512, DF=1024, LC=128)

ENG_ATTR = {"T": "tensor", "V": "vector", "S": "scalar", "G": "gpsimd", "Y": "sync"}


class DmaChan:

    def __init__(self, nc, es, name):
        self.nc, self.es, self.name = nc, es, name
        self.sems, self.idx, self.cnt = [], -1, 0
        self._new()

    def _new(self):
        self.idx += 1
        self.sems.append(self.es.enter_context(
            self.nc.semaphore(f"dq_{self.name}_{self.idx}")))
        self.cnt = 0

    def next_handle(self, n_dmas):
        if self.cnt + 16 * n_dmas > 60000:
            self._new()
        sem = self.sems[self.idx]
        self.cnt += 16 * n_dmas
        return (sem, self.cnt)


class Prog:

    def __init__(self, nc, es):
        self.nc, self.es = nc, es
        self.sems = {k: es.enter_context(nc.semaphore(f"sem_{k}")) for k in "TVSG"}
        self.cnt = {k: 0 for k in "TVSG"}
        self.chans = {}
        self.stages = {k: [] for k in ENG_ATTR}

    @staticmethod
    def _flat(deps):
        out = []
        for d in deps:
            if d is None:
                continue
            if isinstance(d, list):
                out.extend(x for x in d if x is not None)
            else:
                out.append(d)
        return out

    def stage(self, eng, deps, fn):
        waits = self._flat(deps)
        self.cnt[eng] += 1
        handle = (self.sems[eng], self.cnt[eng])
        self.stages[eng].append(("c", waits, fn, handle))
        return handle

    def dma_stage(self, eng, deps, fn, n_dmas, chan="misc"):
        waits = self._flat(deps)
        if chan not in self.chans:
            self.chans[chan] = DmaChan(self.nc, self.es, chan)
        h = self.chans[chan].next_handle(n_dmas)
        self.stages[eng].append(("d", waits, fn, (h, n_dmas)))
        return h

    def barrier(self):
        hs = [(self.sems[k], self.cnt[k]) for k in "TVSG" if self.cnt[k] > 0]
        for q in self.chans.values():
            if q.cnt > 0:
                hs.append((q.sems[q.idx], q.cnt))
        return hs

    def emit(self, block):
        for eng, attr in ENG_ATTR.items():
            stages = self.stages[eng]
            if not stages:
                continue

            def runner(e, stages=stages):
                watermark = {}
                import os as _os
                nodedup = bool(_os.environ.get("KNODEDUP"))
                for kind, waits, fn, info in stages:
                    for (sem, val) in waits:
                        if not nodedup and watermark.get(sem, -1) >= val:
                            continue
                        e.wait_ge(sem, val)
                        watermark[sem] = val
                    if kind == "c":
                        sem, val = info
                        last = fn(e)
                        assert last is not None, "stage fn returned None"
                        last.then_inc(sem, 1)
                    else:
                        (sem, endval), n = info
                        issued = [0]

                        def dma(out, in_, e=e, sem=sem, issued=issued):
                            ins = e.dma_start(out=out, in_=in_)
                            ins.then_inc(sem, 16)
                            issued[0] += 1
                            return ins

                        fn(e, dma)
                        assert issued[0] == n, f"dma stage issued {issued[0]} != {n}"

            getattr(block, attr)(runner)


class WRing:

    def __init__(self, P, bufs, name, init_dep):
        self.P = P
        self.bufs = bufs
        self.name = name
        self.n = len(bufs)
        self.last_use = [init_dep] * self.n
        self.i = 0

    def load(self, src_ap, width):
        slot = self.i % self.n
        self.i += 1
        buf = self.bufs[slot]

        def f(e, dma, buf=buf, src_ap=src_ap, width=width):
            dma(out=buf[:, 0:width], in_=src_ap)

        h = self.P.dma_stage("G", [self.last_use[slot]], f, 1,
                             chan=f"{self.name}{slot}")
        return slot, h

    def used(self, slot, handle):
        self.last_use[slot] = handle


def build_program(cfg: Cfg):
    PH = int(os.environ.get("KPHASES", "99"))
    c = cfg
    nc = bass.Bass("TRN2", target_bir_lowering=False, debug=False)
    es = ExitStack()

    KD, TI, TS, NST, DTR = c.KD, c.TI, c.TS, c.NST, c.DTR
    LC, NCH, L, D = c.LC, c.NCH, c.L, c.D
    BLK, NSUB, SUB, DFT, E = c.BLK, c.NSUB, c.SUB, c.DFT, c.E
    MGA, DG2, MG1, XG, NTC, NTL = c.MGA, c.DG2, c.MG1, c.XG, c.NTC, c.NTL
    NSD = -(-D // 512)
    XP = DTR + 2 * NST

    def inp(name, shape, dtype=F32):
        return nc.dram_tensor(name, list(shape), dtype, kind="ExternalInput")

    xfull = inp("xfull", [L, D])
    xown = inp("xown", [c.OWN, D])
    winx = inp("winx", [KD, XG, 128, 512])
    winz = inp("winz", [KD, XG, 128, 512])
    wout = inp("wout", [TI, KD // MGA, 128, MGA * 128])
    xprojt = inp("xprojt", [TI, 128, XP])
    dtprojt = inp("dtprojt", [DTR, TS * 128])
    convw = inp("convw", [128, TI * c.DCONV])
    convb = inp("convb", [128, TI])
    dvec = inp("dvec", [128, TS])
    alog = inp("alog", [128, TS * NST])
    dtb = inp("dtb", [128, TS])
    lng = inp("lng", [128, KD])
    lnb = inp("lnb", [128, KD])
    abelb = inp("abelb", [128, KD])
    abelw = inp("abelw", [KD, KD // MGA, 128, MGA * 128])
    gatew = inp("gatew", [KD, 128, E])
    gateb = inp("gateb", [1, E])
    mixw = inp("mixw", [128, 2 * KD])
    mixb = inp("mixb", [1, 1])
    w1t = inp("w1t", [E, KD, MG1, 128, 256])
    w2t = inp("w2t", [E, KD // DG2, DFT, 128, DG2 * 128])
    b1v = inp("b1v", [128, E * DFT])
    b2v = inp("b2v", [E, D])
    gmat = inp("gmat", [D, D])
    gmatT = inp("gmatT", [D, D])
    identf = inp("identf", [128, 128])
    identb = inp("identb", [128, 128], BF16)
    onesf = inp("onesf", [1, 128])
    onescol = inp("onescol", [128, 1])
    onesb = inp("onesb", [1, 128], BF16)
    cvals = inp("cvals", [128, 4])
    sel32 = inp("sel32", [32, 32 * 128], BF16)
    sel8 = inp("sel8", [8, 8 * 128])
    yoff = inp("yoff", [1, 4], I32)

    xin_sc = nc.dram_tensor("xin_sc", [TS * 128, L], BF16, kind="Internal")
    dt_dram = nc.dram_tensor("dt_dram", [DTR, L], F32, kind="Internal")
    bc_dram = nc.dram_tensor("bc_dram", [2 * NST, L], F32, kind="Internal")
    yagin = nc.dram_tensor("yagin", [TS * 128, L], BF16, kind="Internal")
    yago = nc.dram_tensor("yago", [8 * TS * 128, L], BF16, kind="Internal", addr_space="Shared")
    eagin = nc.dram_tensor("eagin", [D, D], BF16, kind="Internal")
    eago = nc.dram_tensor("eago", [8 * D, D], BF16, kind="Internal", addr_space="Shared")
    s2dram = nc.dram_tensor("s2dram", [2 * KD * 128, BLK], F32, kind="Internal")
    rotdram = nc.dram_tensor("rotdram", [KD * 128, D], BF16, kind="Internal")
    yout = nc.dram_tensor("yout", [c.OWN, D], F32, kind="ExternalOutput")

    P = Prog(nc, es)
    sb = lambda nm, shp, dt=F32: es.enter_context(nc.sbuf_tensor(nm, shp, dt))
    ps = [es.enter_context(nc.psum_tensor(f"ps{i}", [128, 512], F32)) for i in range(8)]
    psr = [None] * 8

    s_identf = sb("s_identf", [128, 128])
    s_identb = sb("s_identb", [128, 128], BF16)
    s_onesf = sb("s_onesf", [1, 128])
    s_onescol = sb("s_onescol", [128, 1])
    s_onesb = sb("s_onesb", [1, 128], BF16)
    s_cvals = sb("s_cvals", [128, 4])
    s_sel8 = sb("s_sel8", [8, 8 * 128], BF16)
    s_convw = sb("s_convw", [128, TI * c.DCONV])
    s_convb = sb("s_convb", [128, TI])
    s_dvec = sb("s_dvec", [128, TS])
    s_aneg = sb("s_aneg", [128, TS * NST])
    s_dtb = sb("s_dtb", [128, TS])
    s_lng = sb("s_lng", [128, KD])
    s_lnb = sb("s_lnb", [128, KD])
    s_abelb = sb("s_abelb", [128, KD])
    s_gatew = sb("s_gatew", [128, KD * E], BF16)
    s_gatebbc = sb("s_gatebbc", [128, E])
    s_gateb_row = sb("s_gateb_row", [1, E])
    s_mixw = sb("s_mixw", [128, 2 * KD], BF16)
    s_mixb = sb("s_mixb", [1, 1])
    s_b2 = sb("s_b2", [8, D], BF16)
    s_b1 = sb("s_b1", [128, E * DFT])
    s_yoff = sb("s_yoff", [1, 4], I32)
    s_fin = sb("s_fin", [1, 1])
    x_tm = [sb(f"x_tm{i}", [128, D]) for i in range(2)]
    xn_tm = [sb(f"xn_tm{i}", [128, D], BF16) for i in range(2)]
    s_cst = sb("s_cst", [128, 6])
    s_sq = sb("s_sq", [128, D])

    def ld_params(e, dma):
        for dst, src in [(s_identf, identf), (s_identb, identb), (s_onesf, onesf),
                         (s_onescol, onescol), (s_onesb, onesb),
                         (s_cvals, cvals), (s_convw, convw),
                         (s_convb, convb), (s_dvec, dvec), (s_aneg, alog),
                         (s_dtb, dtb), (s_lng, lng), (s_lnb, lnb),
                         (s_abelb, abelb), (s_mixb, mixb),
                         (s_b1, b1v), (s_yoff, yoff), (s_gateb_row, gateb)]:
            dma(out=dst[:, :], in_=src.ap())

    h_par = P.dma_stage("Y", [], ld_params, 18, chan="par")
    for _i, _v in enumerate([c.EPS_LN, -0.5, -1.0, 1.0]):
        nc.const_aps.register(F32, _v, s_cvals[:, _i:_i + 1])

    def ld_params_cast(e, dma):
        dma(out=s_gatew[:, :].rearrange("p (k e) -> p k e", k=KD), in_=gatew.ap().rearrange("k p e -> p k e"))
        dma(out=s_mixw[:, :], in_=mixw.ap())
        dma(out=s_sel8[:, :], in_=sel8.ap())
        dma(out=s_b2[:, :], in_=b2v.ap())

    h_parc = P.dma_stage("G", [], ld_params_cast, 4, chan="parc")

    h_aexp = P.stage("S", [h_par], lambda e: e.activation(
        s_aneg[:, :], s_aneg[:, :], AF.Exp))
    h_aneg = P.stage("V", [h_aexp], lambda e: e.tensor_scalar_mul(
        s_aneg[:, :], s_aneg[:, :], -1.0))

    h_gb1 = P.stage("T", [h_par, psr[0]], lambda e: e.matmul(
        ps[0][0:128, 0:E], s_onesf[0:1, :], s_gateb_row[0:1, 0:E],
        start=True, stop=True))
    h_gbbc = P.stage("V", [h_gb1], lambda e: e.tensor_copy(
        s_gatebbc[:, :], ps[0][0:128, 0:E]))
    psr[0] = h_gbbc

    def mm_DD(dst, lhsT_sb, rhs_sb, deps, fuse_add_blockpoly=None, scr_tracker=None):
        hs = []
        for m in range(KD):
            for s_ in range(NSD):
                w = min(512, D - s_ * 512)
                lo = m * D + s_ * 512
                pi = (m * NSD + s_) % 6

                def f(e, m=m, s_=s_, w=w, pi=pi, lhsT_sb=lhsT_sb, rhs_sb=rhs_sb):
                    ins = None
                    for k in range(KD):
                        ins = e.matmul(
                            ps[pi][:, 0:w],
                            lhsT_sb[:, k * D + m * 128:k * D + (m + 1) * 128],
                            rhs_sb[:, k * D + s_ * 512:k * D + s_ * 512 + w],
                            start=(k == 0), stop=(k == KD - 1))
                    return ins

                hm = P.stage("T", deps + [psr[pi]], f)
                if fuse_add_blockpoly is None:
                    hc = P.stage("V", [hm], lambda e, lo=lo, w=w, pi=pi, dst=dst:
                                 e.tensor_copy(dst[:, lo:lo + w], ps[pi][:, 0:w]))
                else:
                    hpoly = fuse_add_blockpoly(m, s_, w, lo, deps)
                    hc = P.stage("V", [hm, hpoly], lambda e, lo=lo, w=w, pi=pi, dst=dst:
                                 e.tensor_tensor(dst[:, lo:lo + w], sScr[:, 0:w],
                                                 ps[pi][:, 0:w], OP.add))
                    if scr_tracker is not None:
                        scr_tracker[0] = hc
                psr[pi] = hc
                hs.append(hc)
        return hs

    def transpose_DD(src, dst, deps):
        hs = []
        for km in range(KD):
            for kk in range(KD):
                pi = 6 + (km * KD + kk) % 2

                def ftr(e, km=km, kk=kk, pi=pi, src=src):
                    return e.transpose(
                        ps[pi].ap().bitcast(BF16)[:, 0:128],
                        src[:, kk * D + km * 128:kk * D + (km + 1) * 128],
                        s_identb[:, :])

                ht = P.stage("T", deps + [psr[pi], h_par], ftr)

                def ftc(e, km=km, kk=kk, pi=pi, dst=dst):
                    return e.tensor_copy(
                        dst[:, km * D + kk * 128:km * D + (kk + 1) * 128],
                        ps[pi].ap().bitcast(BF16)[:, 0:128])

                hc = P.stage("V", [ht], ftc)
                psr[pi] = hc
                hs.append(hc)
        return hs


    def _finish_early():
        P.stage("G", P.barrier() + [h_par], lambda e: e.memset(s_fin[0:1, 0:1], 0.0))
        with nc.Block() as block:
            P.emit(block)
        return nc
    p0 = ExitStack()
    sG = p0.enter_context(nc.sbuf_tensor("sG", [128, KD * D], BF16))
    sGT = p0.enter_context(nc.sbuf_tensor("sGT", [128, KD * D], BF16))
    sG2 = p0.enter_context(nc.sbuf_tensor("sG2", [128, KD * D], BF16))
    sG3 = p0.enter_context(nc.sbuf_tensor("sG3", [128, KD * D], BF16))
    sG4 = p0.enter_context(nc.sbuf_tensor("sG4", [128, KD * D], BF16))
    sG4T = p0.enter_context(nc.sbuf_tensor("sG4T", [128, KD * D], BF16))
    sT0 = p0.enter_context(nc.sbuf_tensor("sT0", [128, KD * D], BF16))
    sT1 = p0.enter_context(nc.sbuf_tensor("sT1", [128, KD * D], BF16))
    sScr = p0.enter_context(nc.sbuf_tensor("sScr", [128, 512], F32))

    def ld_g(e, dma):
        dma(out=sG[:, :].rearrange("p (k d) -> p k d", k=KD), in_=gmat.ap().rearrange("(k p) d -> p k d", p=128))
        dma(out=sGT[:, :].rearrange("p (k d) -> p k d", k=KD), in_=gmatT.ap().rearrange("(k p) d -> p k d", p=128))

    h_g = P.dma_stage("G", [], ld_g, 2, chan="gmat")

    hG2 = mm_DD(sG2, sGT, sG, [h_g])
    hG3 = mm_DD(sG3, sGT, sG2, [h_g] + hG2)
    hG4 = mm_DD(sG4, sGT, sG3, [h_g] + hG3)
    hG4T = transpose_DD(sG4, sG4T, hG4)

    fact = [1.0]
    for i in range(1, 12):
        fact.append(fact[-1] * i)
    coef = [1.0 / f for f in fact]

    last_scr = [None]

    def mk_blockpoly(cidx):
        c0, c1, c2, c3 = coef[cidx], coef[cidx + 1], coef[cidx + 2], coef[cidx + 3]

        def fn(m, s_, w, lo, deps):
            h1 = P.stage("V", deps + [last_scr[0]], lambda e, lo=lo, w=w:
                         e.tensor_scalar(sScr[:, 0:w], sG[:, lo:lo + w], c1, None, OP.mult))
            h2 = P.stage("V", [h1], lambda e, lo=lo, w=w: e.scalar_tensor_tensor(
                sScr[:, 0:w], sG2[:, lo:lo + w], c2, sScr[:, 0:w], OP.mult, OP.add))
            h3 = P.stage("V", [h2], lambda e, lo=lo, w=w: e.scalar_tensor_tensor(
                sScr[:, 0:w], sG3[:, lo:lo + w], c3, sScr[:, 0:w], OP.mult, OP.add))
            dcol = m * 128 - s_ * 512
            if 0 <= dcol <= w - 128:
                h3 = P.stage("V", [h3], lambda e, dcol=dcol: e.scalar_tensor_tensor(
                    sScr[:, dcol:dcol + 128], s_identf[:, :], c0,
                    sScr[:, dcol:dcol + 128], OP.mult, OP.add))
            last_scr[0] = h3
            return h3

        return fn

    hB2 = []
    pol2 = mk_blockpoly(8)
    for m in range(KD):
        for s_ in range(NSD):
            w = min(512, D - s_ * 512)
            lo = m * D + s_ * 512
            hp = pol2(m, s_, w, lo, hG2 + hG3 + [h_g, h_par])
            hc = P.stage("V", [hp], lambda e, lo=lo, w=w: e.tensor_copy(
                sT0[:, lo:lo + w], sScr[:, 0:w]))
            last_scr[0] = hc
            hB2.append(hc)

    hT1 = mm_DD(sT1, sG4T, sT0, hG4T + hB2 + [h_g], fuse_add_blockpoly=mk_blockpoly(4), scr_tracker=last_scr)
    hE = mm_DD(sT0, sG4T, sT1, hG4T + hT1 + [h_g], fuse_add_blockpoly=mk_blockpoly(0), scr_tracker=last_scr)

    h_eout = P.dma_stage("Y", hE, lambda e, dma: dma(
        out=eagin.ap().rearrange("(k p) d -> p k d", p=128),
        in_=sT0[:, :].rearrange("p (k d) -> p k d", k=KD)), 1, chan="eout")

    bar_p0 = P.barrier()
    p0.close()
    if PH <= 1:
        return _finish_early()

    ln_state = {"last": None}
    lnwar = {0: [], 1: []}

    def ln1_tokens(src_dram, row0, bi, extra_deps):
        hld = P.dma_stage("Y", extra_deps + [ln_state["last"]] + lnwar[bi],
                          lambda e, dma: dma(
                              out=x_tm[bi][:, :],
                              in_=src_dram.ap()[row0:row0 + 128, :]),
                          1, chan=f"xt{bi}")
        h1 = P.stage("V", [hld, ln_state["last"]], lambda e: e.reduce_sum(
            s_cst[:, 0:1], x_tm[bi][:, :], axis=AX))
        h2 = P.stage("S", [hld, ln_state["last"]], lambda e: e.activation(
            s_sq[:, :], x_tm[bi][:, :], AF.Square))
        h3 = P.stage("V", [h2, h1], lambda e: e.reduce_sum(
            s_cst[:, 1:2], s_sq[:, :], axis=AX))
        h4 = P.stage("V", [h3], lambda e: e.tensor_scalar(
            s_cst[:, 2:3], s_cst[:, 0:1], 1.0 / D, None, OP.mult))
        h5 = P.stage("V", [h4], lambda e: e.tensor_tensor(
            s_cst[:, 3:4], s_cst[:, 2:3], s_cst[:, 2:3], OP.mult))
        h6 = P.stage("V", [h5], lambda e: e.tensor_scalar(
            s_cst[:, 4:5], s_cst[:, 1:2], 1.0 / D, s_cst[:, 3:4], OP.mult, OP.subtract))
        h7 = P.stage("S", [h6, h_par], lambda e: e.activation(
            s_cst[:, 5:6], s_cst[:, 4:5], AF.Ln, bias=c.EPS_LN))
        h8 = P.stage("S", [h7], lambda e: e.activation(
            s_cst[:, 5:6], s_cst[:, 5:6], AF.Exp, scale=-0.5))
        h9 = P.stage("V", [h8], lambda e: e.tensor_scalar(
            xn_tm[bi][:, :], x_tm[bi][:, :], s_cst[:, 2:3], s_cst[:, 5:6],
            OP.subtract, OP.mult))
        ln_state["last"] = h9
        return h9

    fr = ExitStack()
    xn_fm = [fr.enter_context(nc.sbuf_tensor(f"xn_fm{i}", [128, KD * LC], BF16))
             for i in range(2)]
    NWF = 2 * KD
    s_win = [fr.enter_context(nc.sbuf_tensor(f"s_win{i}", [128, 512], BF16))
             for i in range(NWF)]
    xpre = [fr.enter_context(nc.sbuf_tensor(f"xpre{i}", [128, LC + 3], F32))
            for i in range(3)]
    xtail = fr.enter_context(nc.sbuf_tensor("xtail", [128, TI * 3], F32))
    s_xin = fr.enter_context(nc.sbuf_tensor("s_xin", [128, TI * LC], BF16))
    s_xdbl = fr.enter_context(nc.sbuf_tensor("s_xdbl", [XP, LC], F32))
    s_cvs = fr.enter_context(nc.sbuf_tensor("s_cvs", [128, LC], F32))
    s_xprojt = fr.enter_context(nc.sbuf_tensor("s_xprojt", [128, TI * XP], BF16))
    h_xprj = P.dma_stage("G", bar_p0, lambda e, dma: dma(
        out=s_xprojt[:, :].rearrange("p (t n) -> p t n", t=TI),
        in_=xprojt.ap().rearrange("t p n -> p t n")), 1, chan="xprj")

    h_xt0 = P.stage("V", bar_p0, lambda e: e.memset(xtail[:, :], 0.0))
    wring_f = WRing(P, s_win, "wf", bar_p0)

    h_xinsp = [None] * NCH
    h_dtsp = [None] * NCH
    lw = [[h_parc] + bar_p0, [h_parc] + bar_p0]
    l_xnfm = [bar_p0, bar_p0]
    l_xpre = [bar_p0, bar_p0, bar_p0]
    l_xin = [bar_p0]
    l_xdbl = bar_p0
    l_tail = [h_xt0] * TI
    l_cvs = bar_p0

    def emit_front(ch):
        nonlocal l_xdbl, l_cvs, l_xin
        bi = ch % 2
        hs_tr = []
        for tt in range(NTL):
            h9 = ln1_tokens(xfull, ch * LC + tt * 128, bi, bar_p0 if ch == 0 else [])
            lnwar[bi] = [h9]
            for k in range(KD):
                pi = (tt * KD + k) % 4

                def ftr(e, bi=bi, k=k, pi=pi):
                    return e.transpose(ps[pi].ap().bitcast(BF16)[:, 0:128],
                                       xn_tm[bi][:, k * 128:(k + 1) * 128],
                                       s_identb[:, :])

                ht = P.stage("T", [h9, psr[pi], h_par], ftr)

                def ftc(e, bi=bi, k=k, tt=tt, pi=pi):
                    return e.tensor_copy(
                        xn_fm[bi][:, k * LC + tt * 128:k * LC + (tt + 1) * 128],
                        ps[pi].ap().bitcast(BF16)[:, 0:128])

                hc = P.stage("V", [ht, l_xnfm[bi]], ftc)
                psr[pi] = hc
                lnwar[bi].append(ht)
                hs_tr.append(hc)

        hs_sil = {}
        for g in range(XG):
            tiles = [wring_f.load(winx.ap()[k, g], 512) for k in range(KD)]

            def fgrp(e, tiles=tiles, bi=bi):
                ins = None
                for k in range(KD):
                    slot, _ = tiles[k]
                    for m4 in range(4):
                        ins = e.matmul(ps[4 + m4][:, 0:LC],
                                       s_win[slot][:, m4 * 128:(m4 + 1) * 128],
                                       xn_fm[bi][:, k * LC:(k + 1) * LC],
                                       start=(k == 0), stop=(k == KD - 1))
                return ins

            hgrp = P.stage("T", [h for _, h in tiles] + hs_tr +
                           [psr[4 + m4] for m4 in range(4)], fgrp)
            for slot, _ in tiles:
                wring_f.used(slot, hgrp)
            group_last_mm = {m4: hgrp for m4 in range(4)}
            l_xnfm_new = hgrp
            for m4 in range(4):
                t = g * 4 + m4
                pi = 4 + m4
                xb = xpre[t % 3]
                h_t0 = P.stage("V", [l_tail[t], l_xpre[t % 3]],
                               lambda e, xb=xb, t=t: e.tensor_copy(
                                   xb[:, 0:3], xtail[:, t * 3:t * 3 + 3]))
                h_ev = P.stage("S", [group_last_mm[m4], h_t0],
                               lambda e, xb=xb, pi=pi: e.copy(
                                   xb[:, 3:3 + LC], ps[pi][:, 0:LC]))
                psr[pi] = h_ev
                h_tl = P.stage("V", [h_ev], lambda e, xb=xb, t=t: e.tensor_copy(
                    xtail[:, t * 3:t * 3 + 3], xb[:, LC:LC + 3]))
                h_c = P.stage("V", [h_ev, l_cvs], lambda e, xb=xb, t=t: e.tensor_scalar(
                    s_cvs[:, 0:LC], xb[:, 0:LC], s_convw[:, t * 4:t * 4 + 1],
                    None, OP.mult))
                for kk in range(1, c.DCONV):
                    h_c = P.stage("V", [h_c], lambda e, xb=xb, t=t, kk=kk:
                                  e.scalar_tensor_tensor(
                                      s_cvs[:, 0:LC], xb[:, kk:kk + LC],
                                      s_convw[:, t * 4 + kk:t * 4 + kk + 1],
                                      s_cvs[:, 0:LC], OP.mult, OP.add))
                h_si = P.stage("S", [h_c] + l_xin, lambda e, t=t: e.activation(
                    s_xin[:, t * LC:(t + 1) * LC], s_cvs[:, 0:LC], AF.Silu,
                    bias=s_convb[:, t:t + 1]))
                l_cvs = h_si
                l_xpre[t % 3] = h_si
                l_tail[t] = h_tl
                hs_sil[t] = h_si
        l_xnfm[bi] = l_xnfm_new
        l_xin = []

        def sp_xin(e, dma, ch=ch):
            for t in range(TS):
                dma(out=xin_sc.ap().rearrange("(t p) l -> t p l", p=128)
                    [t, :, ch * LC:(ch + 1) * LC],
                    in_=s_xin[:, t * LC:(t + 1) * LC])

        h_xinsp[ch] = P.dma_stage("Y", [hs_sil[t] for t in range(TS)], sp_xin, TS,
                                  chan="spx")
        l_xin.append(h_xinsp[ch])

        def fxp(e):
            ins = None
            for k in range(TI):
                ins = e.matmul(ps[0][0:XP, 0:LC],
                               s_xprojt[:, k * XP:(k + 1) * XP],
                               s_xin[:, k * LC:(k + 1) * LC],
                               start=(k == 0), stop=(k == TI - 1))
            return ins

        h_xp = P.stage("T", [hs_sil[t] for t in range(TI)] + [psr[0], h_xprj], fxp)
        l_xin.append(h_xp)

        h_xd = P.stage("V", [h_xp, l_xdbl], lambda e: e.tensor_copy(
            s_xdbl[0:XP, 0:LC], ps[0][0:XP, 0:LC]))
        psr[0] = h_xd

        def sp_dt(e, dma, ch=ch):
            dma(out=dt_dram.ap()[:, ch * LC:(ch + 1) * LC], in_=s_xdbl[0:DTR, 0:LC])
            dma(out=bc_dram.ap()[:, ch * LC:(ch + 1) * LC],
                in_=s_xdbl[DTR:DTR + 2 * NST, 0:LC])

        h_dtsp[ch] = P.dma_stage("Y", [h_xd], sp_dt, 2, chan=f"spdt{ch}")
        l_xdbl = h_dtsp[ch]


    sc = ExitStack()
    s_sel32 = sc.enter_context(nc.sbuf_tensor("s_sel32", [32, 32 * 128], BF16))
    s_dtprojt = sc.enter_context(nc.sbuf_tensor("s_dtprojt", [DTR, TS * 128], BF16))
    h_sel32 = P.dma_stage("G", bar_p0, lambda e, dma: (
        dma(out=s_sel32[:, :], in_=sel32.ap()),
        dma(out=s_dtprojt[:, :], in_=dtprojt.ap())), 2, chan="sel32")
    z_xin = [sc.enter_context(nc.sbuf_tensor(f"z_xin{i}", [128, TS * LC], BF16))
             for i in range(2)]
    z_dt = [sc.enter_context(nc.sbuf_tensor(f"z_dt{i}", [DTR, LC], BF16))
            for i in range(2)]
    z_bc = [sc.enter_context(nc.sbuf_tensor(f"z_bc{i}", [32, LC], BF16))
            for i in range(2)]
    z_bcr = sc.enter_context(nc.sbuf_tensor("z_bcr", [128, 2 * NST * LC], BF16))
    z_dl = [sc.enter_context(nc.sbuf_tensor(f"z_dl{i}", [128, LC], F32)) for i in range(2)]
    z_du = [sc.enter_context(nc.sbuf_tensor(f"z_du{i}", [128, LC], BF16))
            for i in range(2)]
    z_da = [sc.enter_context(nc.sbuf_tensor(f"z_da{i}", [128, LC], F32))
            for i in range(2)]
    z_db = [sc.enter_context(nc.sbuf_tensor(f"z_db{i}", [128, 2 * LC], F32))
            for i in range(2)]
    z_h = sc.enter_context(nc.sbuf_tensor("z_h", [128, 2 * LC], F32))
    z_tmp = [sc.enter_context(nc.sbuf_tensor(f"z_tmp{i}", [128, 2 * LC], F32))
             for i in range(2)]
    z_y = [sc.enter_context(nc.sbuf_tensor(f"z_y{i}", [128, LC], F32)) for i in range(2)]
    z_y16 = [sc.enter_context(nc.sbuf_tensor(f"z_y16{i}", [128, LC], BF16))
             for i in range(2)]
    z_sp = sc.enter_context(nc.sbuf_tensor("z_sp", [128, LC], F32))
    z_sp2 = sc.enter_context(nc.sbuf_tensor("z_sp2", [128, LC], F32))
    carry_sb = sc.enter_context(nc.sbuf_tensor("carry_sb", [128, TS * NST], F32))

    l_zxin = [bar_p0, bar_p0]
    l_zdt = [bar_p0, bar_p0]
    l_zbc = [bar_p0, bar_p0]
    l_bcr = bar_p0
    l_dl = [bar_p0, bar_p0]
    l_du = [bar_p0, bar_p0]
    l_da = [bar_p0, bar_p0]
    l_db = [bar_p0] * 2
    l_h = [bar_p0, bar_p0]
    l_tmp = [bar_p0] * 2
    l_y = [bar_p0, bar_p0]
    l_y16 = [bar_p0, bar_p0]
    l_sp = bar_p0
    l_sp2 = bar_p0
    h_ysp = []
    carry_h = {}

    def emit_scan(chs):
        nonlocal l_bcr, l_sp, l_sp2
        bi = chs % 2

        def ld_scan(e, dma, chs=chs, bi=bi):
            for t in range(TS):
                dma(out=z_xin[bi][:, t * LC:(t + 1) * LC],
                    in_=xin_sc.ap().rearrange("(t p) l -> t p l", p=128)
                    [t, :, chs * LC:(chs + 1) * LC])

        h_ldx = P.dma_stage("Y", [h_xinsp[chs], l_zxin[bi]], ld_scan, TS,
                            chan=f"lsx{bi}")

        def ld_dtbc(e, dma, chs=chs, bi=bi):
            dma(out=z_dt[bi][:, :], in_=dt_dram.ap()[:, chs * LC:(chs + 1) * LC])
            dma(out=z_bc[bi][:, :], in_=bc_dram.ap()[:, chs * LC:(chs + 1) * LC])

        h_lddt = P.dma_stage("G", [h_dtsp[chs], l_zdt[bi], l_zbc[bi]],
                             ld_dtbc, 2, chan=f"ldt{bi}")

        hs_bcr = []
        for n in range(2 * NST):
            pi = n % 2

            def fb(e, n=n, pi=pi, bi=bi):
                return e.matmul(ps[pi][:, 0:LC],
                                s_sel32[:, n * 128:(n + 1) * 128],
                                z_bc[bi][:, :], start=True, stop=True)

            hbm = P.stage("T", [h_lddt, psr[pi], h_sel32], fb)

            def fbc(e, n=n, pi=pi):
                return e.tensor_copy(z_bcr[:, n * LC:(n + 1) * LC],
                                     ps[pi][:, 0:LC])

            hbc = P.stage("V", [hbm, l_bcr], fbc)
            psr[pi] = hbc
            hs_bcr.append(hbc)
        l_zbc[bi] = hs_bcr[-1]

        last_reader_bcr = None
        for t in range(TS):
            ti = t % 2

            def fdt(e, t=t, bi=bi):
                return e.matmul(ps[2][:, 0:LC],
                                s_dtprojt[:, t * 128:(t + 1) * 128],
                                z_dt[bi][:, :], start=True, stop=True)

            hdtm = P.stage("T", [h_lddt, psr[2], h_sel32], fdt)
            hr = P.stage("S", [hdtm] + [l_dl[ti]], lambda e, t=t, ti=ti: e.activation(
                z_dl[ti][:, :], ps[2][:, 0:LC], AF.Relu, bias=s_dtb[:, t:t + 1]))
            ha = P.stage("S", [hdtm, l_sp], lambda e, t=t: e.activation(
                z_sp[:, :], ps[2][:, 0:LC], AF.Abs, bias=s_dtb[:, t:t + 1]))
            psr[2] = ha
            he = P.stage("S", [ha, l_sp2, h_par], lambda e: e.activation(
                z_sp2[:, :], z_sp[:, :], AF.Exp, scale=-1.0))
            hl = P.stage("S", [he], lambda e: e.activation(
                z_sp[:, :], z_sp2[:, :], AF.Ln, bias=1.0))
            l_sp = hl
            l_sp2 = hl
            hdl = P.stage("V", [hl, hr], lambda e, ti=ti: e.tensor_tensor(
                z_dl[ti][:, :], z_dl[ti][:, :], z_sp[:, :], OP.add))
            l_sp = hdl
            hdu = P.stage("V", [hdl, h_ldx] + [l_du[ti]], lambda e, t=t, ti=ti, bi=bi:
                          e.tensor_tensor(z_du[ti][:, :], z_dl[ti][:, :],
                                          z_xin[bi][:, t * LC:(t + 1) * LC], OP.mult))

            hda_last = None

            def emit_hdb(n2):
                pb = n2 % 2
                h = P.stage("G", [hdu, hs_bcr[2 * n2], hs_bcr[2 * n2 + 1], l_db[pb]],
                            lambda e, n2=n2, ti=ti, pb=pb:
                            e.tensor_tensor(
                                z_db[pb][:, :].rearrange("p (a l) -> p a l", a=2),
                                z_du[ti][:, :].unsqueeze(1).to_broadcast([128, 2, LC]),
                                z_bcr[:, 2 * n2 * LC:(2 * n2 + 2) * LC]
                                .rearrange("p (a l) -> p a l", a=2), OP.mult))
                return h

            def emit_hda(n):
                nb = n % 2
                return P.stage("S", [hdl, l_da[nb], h_aneg],
                               lambda e, t=t, n=n, ti=ti, nb=nb:
                               e.activation(z_da[nb][:, :], z_dl[ti][:, :], AF.Exp,
                                            scale=s_aneg[:, t * NST + n:t * NST + n + 1]))

            hdbs = {0: emit_hdb(0), 1: emit_hdb(1)}
            hdas = {0: emit_hda(0), 1: emit_hda(1)}
            hh_pair = [None, None]
            for n in range(NST):
                nb = n % 2
                pb = (n // 2) % 2
                hda = hdas[n]
                hda_last = hda
                hdb = hdbs[n // 2]
                cidx = t * NST + n
                if chs == 0:
                    hh = P.stage("V", [hda, hdb, l_h[nb]],
                                 lambda e, nb=nb, pb=pb: e.tensor_tensor_scan(
                                     z_h[:, nb * LC:(nb + 1) * LC], z_da[nb][:, :],
                                     z_db[pb][:, nb * LC:(nb + 1) * LC],
                                     0.0, OP.mult, OP.add))
                else:
                    hh = P.stage("V", [hda, hdb, l_h[nb], carry_h.get(cidx)],
                                 lambda e, nb=nb, pb=pb, cidx=cidx:
                                 e.tensor_tensor_scan(
                                     z_h[:, nb * LC:(nb + 1) * LC], z_da[nb][:, :],
                                     z_db[pb][:, nb * LC:(nb + 1) * LC],
                                     carry_sb[:, cidx:cidx + 1], OP.mult, OP.add))
                l_da[nb] = hh
                if nb == 1:
                    l_db[pb] = hh
                hh_pair[nb] = hh
                if n + 2 < NST:
                    hdas[n + 2] = emit_hda(n + 2)
                if nb == 1 and n // 2 + 2 < NST // 2:
                    hdbs[n // 2 + 2] = emit_hdb(n // 2 + 2)
                if nb == 1:
                    htm = P.stage("V", hh_pair + [hs_bcr[NST + n - 1],
                                                  hs_bcr[NST + n], l_tmp[pb]],
                                  lambda e, n=n, pb=pb:
                                  e.tensor_tensor(
                                      z_tmp[pb][:, :], z_h[:, :],
                                      z_bcr[:, (NST + n - 1) * LC:(NST + n + 1) * LC],
                                      OP.mult))
                    l_h[0] = htm
                    l_h[1] = htm
                    last_reader_bcr = htm
                    hs1 = P.stage("G", [htm], lambda e, pb=pb: e.tensor_tensor(
                        z_tmp[pb][:, 0:LC], z_tmp[pb][:, 0:LC],
                        z_tmp[pb][:, LC:2 * LC], OP.add))
                    if n == 1:
                        hy = P.stage("G", [hs1, l_y[ti]], lambda e, ti=ti, pb=pb:
                                     e.tensor_copy(z_y[ti][:, :], z_tmp[pb][:, 0:LC]))
                    else:
                        hy = P.stage("G", [hs1, l_y[ti]], lambda e, ti=ti, pb=pb:
                                     e.tensor_tensor(z_y[ti][:, :], z_y[ti][:, :],
                                                     z_tmp[pb][:, 0:LC], OP.add))
                    l_tmp[pb] = hy
                    l_y[ti] = hy
                    if chs < NCH - 1:
                        cbase = t * NST + n - 1
                        hcar = P.stage("V", hh_pair + [carry_h.get(cbase),
                                                       carry_h.get(cbase + 1)],
                                       lambda e, cbase=cbase: e.tensor_copy(
                                           carry_sb[:, cbase:cbase + 2],
                                           bass.AP(z_h, LC - 1,
                                                   [[2 * LC, 128], [LC, 2]])))
                        carry_h[cbase] = hcar
                        carry_h[cbase + 1] = hcar
                        l_h[0] = [l_h[0], hcar]
                        l_h[1] = [l_h[1], hcar]
            l_dl[ti] = [hda_last, hdu]
            l_du[ti] = hy
            hyd = P.stage("V", [l_y[ti], h_ldx, h_par], lambda e, t=t, ti=ti, bi=bi:
                          e.scalar_tensor_tensor(z_y[ti][:, :],
                                                 z_xin[bi][:, t * LC:(t + 1) * LC],
                                                 s_dvec[:, t:t + 1], z_y[ti][:, :],
                                                 OP.mult, OP.add))
            hcv = P.stage("V", [hyd, l_y16[ti]], lambda e, ti=ti: e.tensor_copy(
                z_y16[ti][:, :], z_y[ti][:, :]))
            l_y[ti] = hcv

            def sp_y(e, dma, t=t, ti=ti, chs=chs):
                dma(out=yagin.ap().rearrange("(t p) l -> t p l", p=128)
                    [t, :, chs * LC:(chs + 1) * LC], in_=z_y16[ti][:, :])

            hsp_y = P.dma_stage("Y", [hcv], sp_y, 1, chan=f"ysp{ti}")
            l_y16[ti] = hsp_y
            h_ysp.append(hsp_y)
            l_zxin[bi] = hyd
        l_bcr = last_reader_bcr
        l_zdt[bi] = hdtm

    emit_front(0)
    for ch in range(1, NCH):
        emit_front(ch)
        emit_scan(ch - 1)
    emit_scan(NCH - 1)

    bar_scan = P.barrier()
    sc.close()
    fr.close()
    if PH <= 3:
        return _finish_early()

    cc = es.enter_context(nc.semaphore("sem_cc"))

    def f_ag(e):
        i1 = e.collective_compute("AllGather", OP.bypass,
                                  replica_groups=[list(range(8))],
                                  ins=[yagin.ap()], outs=[yago.ap()])
        i1.then_inc(cc, 1)
        i2 = e.collective_compute("AllGather", OP.bypass,
                                  replica_groups=[list(range(8))],
                                  ins=[eagin.ap()], outs=[eago.ap()])
        return i2

    h_ag = P.stage("G", h_ysp + [h_eout], f_ag)
    h_ag_y = (cc, 1)

    ro = ExitStack()
    rA = ro.enter_context(nc.sbuf_tensor("rA", [128, KD * D], BF16))
    rAT = ro.enter_context(nc.sbuf_tensor("rAT", [128, KD * D], BF16))
    rB = ro.enter_context(nc.sbuf_tensor("rB", [128, KD * D], BF16))
    rC = ro.enter_context(nc.sbuf_tensor("rC", [128, KD * D], BF16))

    def ld_e12(e, dma):
        dma(out=rA[:, :].rearrange("p (k d) -> p k d", k=KD),
            in_=eago.ap()[1 * D:2 * D, :].rearrange("(k p) d -> p k d", p=128))
        dma(out=rB[:, :].rearrange("p (k d) -> p k d", k=KD),
            in_=eago.ap()[2 * D:3 * D, :].rearrange("(k p) d -> p k d", p=128))

    h_lde = P.dma_stage("Y", [h_ag, h_ag_y] + bar_scan, ld_e12, 2, chan="rote")
    h_rAT = transpose_DD(rA, rAT, [h_lde])
    hT1r = mm_DD(rC, rAT, rB, h_rAT + [h_lde])

    h_lde0 = P.dma_stage("Y", h_rAT + [h_lde], lambda e, dma: dma(
        out=rA[:, :].rearrange("p (k d) -> p k d", k=KD),
        in_=eago.ap()[0:D, :].rearrange("(k p) d -> p k d", p=128)), 1, chan="rote")
    h_rAT0 = transpose_DD(rA, rAT, [h_lde0] + hT1r)
    hrot = mm_DD(rB, rAT, rC, h_rAT0 + hT1r)

    h_rot = P.dma_stage("Y", hrot, lambda e, dma: dma(
        out=rotdram.ap().rearrange("(k p) d -> p k d", p=128),
        in_=rB[:, :].rearrange("p (k d) -> p k d", k=KD)), 1, chan="rots")

    bar_rot = P.barrier()
    ro.close()
    if PH <= 4:
        return _finish_early()

    fb_ = ExitStack()
    NWB = 16
    g_w = [fb_.enter_context(nc.sbuf_tensor(f"g_w{i}", [128, 512], BF16))
           for i in range(NWB)]
    s2_sb = fb_.enter_context(nc.sbuf_tensor("s2_sb", [128, KD * BLK], F32))
    xn2 = fb_.enter_context(nc.sbuf_tensor("xn2", [128, KD * BLK], BF16))
    abel = fb_.enter_context(nc.sbuf_tensor("abel", [128, KD * BLK], BF16))
    wbc = fb_.enter_context(nc.sbuf_tensor("wbc", [128, BLK], F32))
    wdf = fb_.enter_context(nc.sbuf_tensor("wdf", [8, BLK], BF16))
    srow = fb_.enter_context(nc.sbuf_tensor("srow", [1, 3 * BLK], BF16))
    frow = fb_.enter_context(nc.sbuf_tensor("frow", [1, 3 * 512], F32))
    scr1 = fb_.enter_context(nc.sbuf_tensor("scr1", [128, 512], F32))
    scr2 = fb_.enter_context(nc.sbuf_tensor("scr2", [128, 512], F32))
    gl8 = fb_.enter_context(nc.sbuf_tensor("gl8", [128, 8], F32))
    gm8 = fb_.enter_context(nc.sbuf_tensor("gm8", [128, 8], F32))
    gs8 = fb_.enter_context(nc.sbuf_tensor("gs8", [128, 8], F32))
    gq8 = fb_.enter_context(nc.sbuf_tensor("gq8", [128, 8], F32))
    gcol = fb_.enter_context(nc.sbuf_tensor("gcol", [128, 4], F32))
    o_stg = fb_.enter_context(nc.sbuf_tensor("o_stg", [128, 2 * 128], F32))
    REGW = max(DFT * BLK + KD * BLK, (KD + 2 * TI) * BLK, KD * D + KD * BLK)
    REG = fb_.enter_context(nc.sbuf_tensor("REG", [128, REGW], BF16))
    g_xn = REG.ap()[:, 0:KD * BLK]
    g_z = REG.ap()[:, KD * BLK:(KD + TI) * BLK]
    g_y = REG.ap()[:, (KD + TI) * BLK:(KD + 2 * TI) * BLK]
    moe = REG.ap()[:, DFT * BLK:DFT * BLK + KD * BLK]
    h_sb = REG.ap()[:, 0:DFT * BLK]
    rot_sb = REG.ap()[:, 0:KD * D]
    nonab = REG.ap()[:, KD * D:KD * D + KD * BLK]

    regs = {}
    dbg2 = {}
    wring_b = WRing(P, g_w, "wb", bar_rot)
    h_blk_prev = bar_rot
    h_out_final = []
    ostg_war = [bar_rot, bar_rot]
    lw = [bar_rot, bar_rot]

    def b_row(q): return srow[0:1, q * SUB:q * SUB + SUB]
    def a_row(q): return srow[0:1, BLK + q * SUB:BLK + q * SUB + SUB]
    def g_row(q): return srow[0:1, 2 * BLK + q * SUB:2 * BLK + q * SUB + SUB]
    f1 = frow[0:1, 0:SUB]
    f2 = frow[0:1, 512:512 + SUB]
    f3 = frow[0:1, 1024:1024 + SUB]

    for j in range(2):
        deps0 = h_blk_prev if isinstance(h_blk_prev, list) else [h_blk_prev]
        hs_xn1 = []
        hs_res = []
        for tt in range(NTC):
            h9 = ln1_tokens(xown, j * BLK + tt * 128, 0, deps0)
            lnwar[0] = [h9]
            for k in range(KD):
                pi = (tt * KD + k) % 2

                def ftr(e, k=k, pi=pi):
                    return e.transpose(ps[pi].ap().bitcast(BF16)[:, 0:128],
                                       xn_tm[0][:, k * 128:(k + 1) * 128],
                                       s_identb[:, :])

                ht = P.stage("T", [h9, psr[pi]], ftr)

                def ftc(e, k=k, tt=tt, pi=pi):
                    return e.tensor_copy(
                        g_xn[:, k * BLK + tt * 128:k * BLK + (tt + 1) * 128],
                        ps[pi].ap().bitcast(BF16)[:, 0:128])

                hc = P.stage("V", [ht] + deps0, ftc)
                psr[pi] = hc
                lnwar[0].append(ht)
                hs_xn1.append(hc)
                pi2 = 2 + (tt * KD + k) % 2

                def ftr2(e, k=k, pi2=pi2):
                    return e.transpose(ps[pi2][:, 0:128],
                                       x_tm[0][:, k * 128:(k + 1) * 128],
                                       s_identf[:, :])

                ht2 = P.stage("T", [h9, psr[pi2]], ftr2)

                def ftc2(e, k=k, tt=tt, pi2=pi2):
                    return e.tensor_copy(
                        s2_sb[:, k * BLK + tt * 128:k * BLK + (tt + 1) * 128],
                        ps[pi2][:, 0:128])

                hc2 = P.stage("V", [ht2] + deps0, ftc2)
                psr[pi2] = hc2
                lnwar[0].append(ht2)
                hs_res.append(hc2)

        if PH <= 5:
            return _finish_early()
        hs_z = {}
        for g in range(XG):
            tiles = [wring_b.load(winz.ap()[k, g], 512) for k in range(KD)]
            hgrps = {}
            for q in range(NSUB):
                base = 4 * (q % 2)

                def fgrp(e, tiles=tiles, q=q, base=base):
                    ins = None
                    for k in range(KD):
                        slot, _ = tiles[k]
                        for m4 in range(4):
                            ins = e.matmul(
                                ps[base + m4][:, 0:SUB],
                                g_w[slot][:, m4 * 128:(m4 + 1) * 128],
                                g_xn[:, k * BLK + q * SUB:k * BLK + q * SUB + SUB],
                                start=(k == 0), stop=(k == KD - 1))
                    return ins

                hgrp = P.stage("T", [h for _, h in tiles] + hs_xn1 +
                               [psr[base + i] for i in range(4)], fgrp)
                hgrps[q] = hgrp
                for slot, _ in tiles:
                    wring_b.used(slot, hgrp)
            for m4 in range(4):
                t = g * 4 + m4
                for q in range(NSUB):
                    base = 4 * (q % 2)
                    pi = base + m4
                    hz = P.stage("S", [hgrps[q]] + deps0,
                                 lambda e, t=t, q=q, pi=pi: e.activation(
                                     g_z[:, t * BLK + q * SUB:t * BLK + q * SUB + SUB],
                                     ps[pi][:, 0:SUB], AF.Silu))
                    psr[pi] = hz
                    hs_z[(t, q)] = hz

        if PH <= 6:
            return _finish_early()
        hs_yg = {}
        h_ldys = []
        for half in range(2):
            def ldy(e, dma, half=half, j=j):
                if "yreg" not in regs:
                    regs["yreg"] = e.alloc_register("yoff_reg")
                rr = regs["yreg"]
                e.reg_load(rr, s_yoff[0:1, j * 2 + half:j * 2 + half + 1])
                src = bass.AP(yago, rr, [[L, 128], [128 * L, TS], [1, BLK]])
                dma(out=g_y[:, half * TS * BLK:(half + 1) * TS * BLK]
                    .rearrange("p (t l) -> p t l", t=TS), in_=src)

            h_ldys.append(P.dma_stage("G", [h_ag, h_ag_y, h_par] + deps0, ldy, 1,
                                      chan="ldy"))
        hly = h_ldys[-1]
        for t in range(TI):
            for q in range(NSUB):
                hyg = P.stage("V", [hly, hs_z[(t, q)]], lambda e, t=t, q=q:
                              e.tensor_tensor(
                                  g_y[:, t * BLK + q * SUB:t * BLK + q * SUB + SUB],
                                  g_y[:, t * BLK + q * SUB:t * BLK + q * SUB + SUB],
                                  g_z[:, t * BLK + q * SUB:t * BLK + q * SUB + SUB],
                                  OP.mult))
                hs_yg[(t, q)] = hyg

        if PH <= 7:
            return _finish_early()
        hs_s2 = {}
        for g in range(KD // MGA):
            tiles = [wring_b.load(wout.ap()[k, g], MGA * 128) for k in range(TI)]
            hgrps = {}
            for q in range(NSUB):
                base = 4 * (q % 2)

                def fgrp(e, tiles=tiles, q=q, base=base):
                    ins = None
                    for k in range(TI):
                        slot, _ = tiles[k]
                        for m4 in range(MGA):
                            ins = e.matmul(
                                ps[base + m4][:, 0:SUB],
                                g_w[slot][:, m4 * 128:(m4 + 1) * 128],
                                g_y[:, k * BLK + q * SUB:k * BLK + q * SUB + SUB],
                                start=(k == 0), stop=(k == TI - 1))
                    return ins

                hgrp = P.stage("T", [h for _, h in tiles] +
                               [hs_yg[(k, q)] for k in range(TI)] +
                               [psr[base + i] for i in range(4)], fgrp)
                hgrps[q] = hgrp
            for slot, _ in tiles:
                wring_b.used(slot, [hgrps[q] for q in range(NSUB)])
            for m4 in range(MGA):
                for q in range(NSUB):
                    base = 4 * (q % 2)
                    pi = base + m4
                    m = g * MGA + m4
                    hadd = P.stage("V", [hgrps[q]] + hs_res,
                                   lambda e, m=m, q=q, pi=pi: e.tensor_tensor(
                                       s2_sb[:, m * BLK + q * SUB:m * BLK + q * SUB + SUB],
                                       s2_sb[:, m * BLK + q * SUB:m * BLK + q * SUB + SUB],
                                       ps[pi][:, 0:SUB], OP.add))
                    psr[pi] = hadd
                    hs_s2[(m, q)] = hadd
        hs_s2_all = list(hs_s2.values())

        def sp_s2(e, dma, j=j):
            for k in range(KD):
                dma(out=s2dram.ap().rearrange("(j k p) l -> j k p l", k=KD, p=128)[j, k],
                    in_=s2_sb[:, k * BLK:(k + 1) * BLK])

        h_s2sp = P.dma_stage("Y", hs_s2_all, sp_s2, KD, chan="s2s")

        if PH <= 8:
            return _finish_early()
        hs_xn2 = {}
        h_ab = []
        l_frow = deps0
        l_srow = deps0
        for q in range(NSUB):
            def fsum(e, q=q):
                ins = None
                for k in range(KD):
                    ins = e.matmul(ps[4][0:1, 0:SUB], s_onescol[:, 0:1],
                                   s2_sb[:, k * BLK + q * SUB:k * BLK + q * SUB + SUB],
                                   start=(k == 0), stop=(k == KD - 1))
                return ins

            hsum = P.stage("T", hs_s2_all + [psr[4]], fsum)
            hsq_mm = None
            for k in range(KD):
                hq = P.stage("S", [hs_s2[(k, q)], ln_state["last"]],
                             lambda e, k=k, q=q: e.activation(
                                 s_sq[:, 0:SUB],
                                 s2_sb[:, k * BLK + q * SUB:k * BLK + q * SUB + SUB],
                                 AF.Square))
                ln_state["last"] = hq

                def fq(e, k=k):
                    return e.matmul(ps[5][0:1, 0:SUB], s_onescol[:, 0:1],
                                    s_sq[:, 0:SUB],
                                    start=(k == 0), stop=(k == KD - 1))

                hsq_mm = P.stage("T", [hq] + ([psr[5]] if k == 0 else []), fq)
                ln_state["last"] = hsq_mm
            h1 = P.stage("V", [hsum, l_frow], lambda e: e.tensor_scalar(
                f1, ps[4][0:1, 0:SUB], 1.0 / D, None, OP.mult))
            psr[4] = h1
            h2 = P.stage("V", [hsq_mm, h1], lambda e: e.tensor_scalar(
                f2, ps[5][0:1, 0:SUB], 1.0 / D, None, OP.mult))
            psr[5] = h2
            h3 = P.stage("V", [h2], lambda e: e.tensor_tensor(
                f3, f1, f1, OP.mult))
            h4 = P.stage("V", [h3], lambda e: e.tensor_tensor(
                f2, f2, f3, OP.subtract))
            h5 = P.stage("S", [h4, h_par], lambda e: e.activation(
                f2, f2, AF.Ln, bias=c.EPS_LN))
            h6s = P.stage("S", [h5], lambda e: e.activation(
                f2, f2, AF.Exp, scale=-0.5))
            h7 = P.stage("V", [h6s], lambda e: e.tensor_tensor(
                f3, f1, f2, OP.mult))
            h8v = P.stage("V", [h7], lambda e: e.tensor_scalar(
                f3, f3, -1.0, None, OP.mult))
            h6 = P.stage("V", [h6s, l_srow], lambda e, q=q: e.tensor_copy(
                a_row(q), f2))
            h8 = P.stage("V", [h8v], lambda e, q=q: e.tensor_copy(
                b_row(q), f3))
            l_frow = [h6, h8]
            l_srow = [h6, h8]
            h_ab.append((h6, h8))

            hba = P.stage("T", [h6, psr[4]], lambda e, q=q: e.matmul(
                ps[4][:, 0:SUB], s_onesb[0:1, :], a_row(q), start=True, stop=True))
            hbb = P.stage("T", [h8, psr[5]], lambda e, q=q: e.matmul(
                ps[5][:, 0:SUB], s_onesb[0:1, :], b_row(q), start=True, stop=True))
            l_scr1 = None
            for k in range(KD):
                ha1 = P.stage("V", [hba, hs_s2[(k, q)], l_scr1],
                              lambda e, k=k, q=q: e.tensor_tensor(
                                  scr1[:, 0:SUB],
                                  s2_sb[:, k * BLK + q * SUB:k * BLK + q * SUB + SUB],
                                  ps[4][:, 0:SUB], OP.mult))
                ha2 = P.stage("V", [ha1, hbb], lambda e: e.tensor_tensor(
                    scr1[:, 0:SUB], scr1[:, 0:SUB], ps[5][:, 0:SUB], OP.add))
                ha3 = P.stage("V", [ha2], lambda e, k=k, q=q: e.tensor_scalar(
                    xn2[:, k * BLK + q * SUB:k * BLK + q * SUB + SUB],
                    scr1[:, 0:SUB], s_lng[:, k:k + 1], s_lnb[:, k:k + 1],
                    OP.mult, OP.add))
                l_scr1 = ha3
                psr[4] = ha3
                psr[5] = ha3
                hs_xn2[(k, q)] = ha3
        hs_xn2_all = list(hs_xn2.values())

        if PH <= 9:
            return _finish_early()
        if os.environ.get("KDEBUG2") and j == 0:
            dbg2["xn2"] = nc.dram_tensor("dbg2_xn2", [128, KD * BLK], BF16,
                                         kind="ExternalOutput")
            P.dma_stage("Y", hs_xn2_all, lambda e, dma: dma(
                out=dbg2["xn2"].ap(), in_=xn2[:, :]), 1, chan="dbg")
        hs_wd = []
        l_g8 = None
        for tc in range(NTC):
            def flog(e, tc=tc):
                ins = None
                for k in range(KD):
                    ins = e.matmul(ps[6][0:128, 0:E],
                                   xn2[:, k * BLK + tc * 128:k * BLK + (tc + 1) * 128],
                                   s_gatew[:, k * E:(k + 1) * E],
                                   start=(k == 0), stop=(k == KD - 1))
                return ins

            hlg = P.stage("T", hs_xn2_all + [psr[6], h_parc], flog)
            hl1 = P.stage("V", [hlg, h_gbbc, l_g8], lambda e: e.tensor_tensor(
                gl8[:, :], ps[6][0:128, 0:E], s_gatebbc[:, :], OP.add))
            psr[6] = hl1
            hm8 = P.stage("V", [hl1], lambda e: e.max(out=gm8[:, :], in_=gl8[:, :]))
            hms = P.stage("V", [hm8], lambda e: e.memset(gs8[:, :], -1e38))
            hcp = P.stage("V", [hms], lambda e: e.tensor_copy(gs8[:, 0:2], gm8[:, 0:2]))
            hmr = P.stage("V", [hcp], lambda e: e.match_replace(
                out=gq8[:, :], in_to_replace=gs8[:, :], in_values=gl8[:, :],
                imm_value=-1e30))
            hmask = P.stage("V", [hmr], lambda e: e.tensor_scalar(
                gq8[:, :], gq8[:, :], -1e30, None, OP.is_equal))
            hs12 = P.stage("V", [hm8], lambda e: e.tensor_tensor(
                gcol[:, 0:1], gm8[:, 0:1], gm8[:, 1:2], OP.add))
            harg = P.stage("V", [hs12, hl1, hmr], lambda e: e.tensor_scalar(
                gl8[:, :], gl8[:, :], 2.0, gcol[:, 0:1], OP.mult, OP.subtract))
            hsg = P.stage("S", [harg], lambda e: e.activation(
                gl8[:, :], gl8[:, :], AF.Sigmoid))
            hwd = P.stage("V", [hsg, hmask], lambda e: e.tensor_tensor(
                gq8[:, :], gq8[:, :], gl8[:, :], OP.mult))
            htr = P.stage("T", [hwd, psr[7]], lambda e: e.transpose(
                ps[7][0:8, 0:128], gq8[:, :], s_identf[:, :]))
            hwf = P.stage("V", [htr], lambda e, tc=tc: e.tensor_copy(
                wdf[:, tc * 128:(tc + 1) * 128], ps[7][0:8, 0:128]))
            psr[7] = hwf
            l_g8 = hwd
            hs_wd.append(hwf)

        if PH <= 10:
            return _finish_early()
        if os.environ.get("KDEBUG2") and j == 0:
            dbg2["wdf"] = nc.dram_tensor("dbg2_wdf", [8, BLK], BF16,
                                         kind="ExternalOutput")
            P.dma_stage("Y", hs_wd, lambda e, dma: dma(
                out=dbg2["wdf"].ap(), in_=wdf[:, :]), 1, chan="dbg")
        h_moe = {}
        for m in range(KD):
            for q in range(NSUB):
                pi = 6 + (m * NSUB + q) % 2

                def fb2(e, m=m, q=q, pi=pi):
                    return e.matmul(ps[pi][:, 0:SUB],
                                    s_b2[0:8, m * 128:(m + 1) * 128],
                                    wdf[0:8, q * SUB:q * SUB + SUB],
                                    start=True, stop=True)

                hb2 = P.stage("T", hs_wd + [psr[pi], h_par], fb2)
                hmi = P.stage("V", [hb2], lambda e, m=m, q=q, pi=pi: e.tensor_copy(
                    moe[:, m * BLK + q * SUB:m * BLK + q * SUB + SUB],
                    ps[pi][:, 0:SUB]))
                psr[pi] = hmi
                h_moe[(m, q)] = hmi

        if PH <= 11:
            return _finish_early()
        hs_abel = {}
        for g in range(KD // MGA):
            tiles = [wring_b.load(abelw.ap()[k, g], MGA * 128) for k in range(KD)]
            hgrps = {}
            for q in range(NSUB):
                base = 4 * (q % 2)

                def fgrp(e, tiles=tiles, q=q, base=base):
                    ins = None
                    for k in range(KD):
                        slot, _ = tiles[k]
                        for m4 in range(MGA):
                            ins = e.matmul(
                                ps[base + m4][:, 0:SUB],
                                g_w[slot][:, m4 * 128:(m4 + 1) * 128],
                                xn2[:, k * BLK + q * SUB:k * BLK + q * SUB + SUB],
                                start=(k == 0), stop=(k == KD - 1))
                    return ins

                hgrp = P.stage("T", [h for _, h in tiles] + hs_xn2_all +
                               [psr[base + i] for i in range(4)], fgrp)
                hgrps[q] = hgrp
            for slot, _ in tiles:
                wring_b.used(slot, [hgrps[q] for q in range(NSUB)])
            for m4 in range(MGA):
                for q in range(NSUB):
                    base = 4 * (q % 2)
                    pi = base + m4
                    m = g * MGA + m4
                    hab = P.stage("S", [hgrps[q]], lambda e, m=m, q=q, pi=pi:
                                  e.activation(
                                      abel[:, m * BLK + q * SUB:m * BLK + q * SUB + SUB],
                                      ps[pi][:, 0:SUB], AF.Gelu,
                                      bias=s_abelb[:, m:m + 1]))
                    psr[pi] = hab
                    hs_abel[(m, q)] = hab

        if PH <= 12:
            return _finish_early()
        if os.environ.get("KDEBUG2") and j == 0:
            dbg2["abel"] = nc.dram_tensor("dbg2_abel", [128, KD * BLK], BF16,
                                          kind="ExternalOutput")
            P.dma_stage("Y", list(hs_abel.values()), lambda e, dma: dma(
                out=dbg2["abel"].ap(), in_=abel[:, :]), 1, chan="dbg")
        hs_h_last = {}
        l_wbc = None
        l_scr2 = None
        for ex in range(E):
            hs_wbc = []
            for q in range(NSUB):
                pi = 6 + q % 2

                def fwb(e, ex=ex, q=q, pi=pi):
                    return e.matmul(ps[pi][:, 0:SUB],
                                    s_sel8[:, ex * 128:(ex + 1) * 128],
                                    wdf[0:8, q * SUB:q * SUB + SUB],
                                    start=True, stop=True)

                hw1 = P.stage("T", hs_wd + [psr[pi], h_par], fwb)
                hw2 = P.stage("V", [hw1, l_wbc], lambda e, q=q, pi=pi: e.tensor_copy(
                    wbc[:, q * SUB:q * SUB + SUB], ps[pi][:, 0:SUB]))
                psr[pi] = hw2
                hs_wbc.append(hw2)

            hs_h = {}
            for mg in range(MG1):
                base = 4 * ((mg + 1) % 2)
                tiles = [wring_b.load(w1t.ap()[ex, k, mg], 256) for k in range(KD)]

                def fgrp(e, tiles=tiles, base=base):
                    ins = None
                    for k in range(KD):
                        slot, _ = tiles[k]
                        for d2 in range(2):
                            for q in range(NSUB):
                                ins = e.matmul(
                                    ps[base + (d2 * NSUB + q) % 4][:, 0:SUB],
                                    g_w[slot][:, d2 * 128:(d2 + 1) * 128],
                                    xn2[:, k * BLK + q * SUB:k * BLK + q * SUB + SUB],
                                    start=(k == 0), stop=(k == KD - 1))
                    return ins

                hgrp = P.stage("T", [h for _, h in tiles] + hs_xn2_all +
                               [psr[base + i] for i in range(4)], fgrp)
                for slot, _ in tiles:
                    wring_b.used(slot, hgrp)
                for d2 in range(2):
                    for q in range(NSUB):
                        pi = base + (d2 * NSUB + q) % 4
                        df = mg * 2 + d2
                        dep_h = hs_h_last.get((df, q))
                        hh = P.stage("S", [hgrp, dep_h],
                                     lambda e, df=df, q=q, pi=pi, ex=ex:
                                     e.activation(
                                         h_sb[:, df * BLK + q * SUB:df * BLK + q * SUB + SUB],
                                         ps[pi][:, 0:SUB], AF.Gelu,
                                         bias=s_b1[:, ex * DFT + df:ex * DFT + df + 1]))
                        psr[pi] = hh
                        hs_h[(df, q)] = hh

            DFG = min(8, DFT)
            for dg in range(KD // DG2):
                hlast = {q: None for q in range(NSUB)}
                for dfg in range(DFT // DFG):
                    tiles = [wring_b.load(w2t.ap()[ex, dg, dfg * DFG + i],
                                          DG2 * 128) for i in range(DFG)]
                    first = dfg == 0
                    for q in range(NSUB):
                        base = 4 * (q % 2)
                        dep_h = [hs_h[(dfg * DFG + i, q)] for i in range(DFG)]

                        def fgrp(e, tiles=tiles, dfg=dfg, q=q, base=base):
                            ins = None
                            for i in range(DFG):
                                df = dfg * DFG + i
                                slot, _ = tiles[i]
                                for d4 in range(DG2):
                                    ins = e.matmul(
                                        ps[base + d4][:, 0:SUB],
                                        g_w[slot][:, d4 * 128:(d4 + 1) * 128],
                                        h_sb[:, df * BLK + q * SUB:df * BLK + q * SUB + SUB],
                                        start=(df == 0), stop=(df == DFT - 1))
                            return ins

                        deps_mm = [h for _, h in tiles] + dep_h +                             ([psr[base + i] for i in range(4)] if first
                             else [hlast[q]])
                        hlast[q] = P.stage("T", deps_mm, fgrp)
                    for slot, _ in tiles:
                        wring_b.used(slot, [hlast[q] for q in range(NSUB)])
                for d4 in range(DG2):
                    for q in range(NSUB):
                        base = 4 * (q % 2)
                        pi = base + d4
                        m = dg * DG2 + d4
                        he1 = P.stage("V", [hlast[q], l_scr2] + hs_wbc,
                                      lambda e, q=q, pi=pi: e.tensor_tensor(
                                          scr2[:, 0:SUB], ps[pi][:, 0:SUB],
                                          wbc[:, q * SUB:q * SUB + SUB],
                                          OP.mult))
                        psr[pi] = he1
                        he2 = P.stage("V", [he1, h_moe[(m, q)]],
                                      lambda e, m=m, q=q: e.tensor_tensor(
                                          moe[:, m * BLK + q * SUB:m * BLK + q * SUB + SUB],
                                          moe[:, m * BLK + q * SUB:m * BLK + q * SUB + SUB],
                                          scr2[:, 0:SUB], OP.add))
                        h_moe[(m, q)] = he2
                        l_scr2 = he2
            hs_h_last = hs_h
            l_wbc = he1
        hs_moe_all = list(h_moe.values())
        hs_h_reads = list(hs_h_last.values())

        if PH <= 13:
            return _finish_early()
        if os.environ.get("KDEBUG2") and j == 0:
            dbg2["moe"] = nc.dram_tensor("dbg2_moe", [128, KD * BLK], BF16,
                                         kind="ExternalOutput")
            P.dma_stage("Y", hs_moe_all, lambda e, dma: dma(
                out=dbg2["moe"].ap(), in_=moe), 1, chan="dbg")
        h_ldrot = P.dma_stage("G", hs_moe_all + [h_rot] + hs_h_reads +
                              [lw[0], lw[1]], ld_rot_fn(rot_sb, rotdram, KD), 1, chan="ldrot")
        hs_nonab = {}
        for m in range(KD):
            for q in range(NSUB):
                pi = 4 + (m * NSUB + q) % 4

                def fmm(e, m=m, q=q, pi=pi):
                    ins = None
                    for k in range(KD):
                        ins = e.matmul(ps[pi][:, 0:SUB],
                                       rot_sb[:, k * D + m * 128:k * D + (m + 1) * 128],
                                       moe[:, k * BLK + q * SUB:k * BLK + q * SUB + SUB],
                                       start=(k == 0), stop=(k == KD - 1))
                    return ins

                hmm = P.stage("T", [h_ldrot, psr[pi]] + hs_moe_all, fmm)
                hnb = P.stage("V", [hmm], lambda e, m=m, q=q, pi=pi: e.tensor_copy(
                    nonab[:, m * BLK + q * SUB:m * BLK + q * SUB + SUB],
                    ps[pi][:, 0:SUB]))
                psr[pi] = hnb
                hs_nonab[(m, q)] = hnb

        if PH <= 14:
            return _finish_early()
        if os.environ.get("KDEBUG2") and j == 0:
            dbg2["nonab"] = nc.dram_tensor("dbg2_nonab", [128, KD * BLK], BF16,
                                           kind="ExternalOutput")
            P.dma_stage("Y", list(hs_nonab.values()), lambda e, dma: dma(
                out=dbg2["nonab"].ap(), in_=nonab), 1, chan="dbg")
        hs_g = []
        for q in range(NSUB):
            def fmx(e, q=q):
                ins = None
                for k in range(KD):
                    ins = e.matmul(ps[6][0:1, 0:SUB], s_mixw[:, k:k + 1],
                                   abel[:, k * BLK + q * SUB:k * BLK + q * SUB + SUB],
                                   start=(k == 0), stop=False)
                for k in range(KD):
                    ins = e.matmul(ps[6][0:1, 0:SUB], s_mixw[:, KD + k:KD + k + 1],
                                   nonab[:, k * BLK + q * SUB:k * BLK + q * SUB + SUB],
                                   start=False, stop=(k == KD - 1))
                return ins

            dep_ab = [hs_abel[(k, q)] for k in range(KD)]
            dep_nb = [hs_nonab[(k, q)] for k in range(KD)]
            hmx = P.stage("T", dep_ab + dep_nb + [psr[6], h_parc], fmx)
            hg1 = P.stage("V", [hmx, l_srow], lambda e, q=q: e.tensor_scalar(
                g_row(q), ps[6][0:1, 0:SUB], s_mixb[0:1, 0:1], None, OP.add))
            psr[6] = hg1
            hg2 = P.stage("S", [hg1], lambda e, q=q: e.activation(
                g_row(q), g_row(q), AF.Sigmoid))
            l_srow = [hg2]
            hs_g.append(hg2)

        if PH <= 15:
            return _finish_early()
        h_lds2 = P.dma_stage("Y", [h_s2sp] + hs_xn2_all + [ln_state["last"]],
                             lambda e, dma, j=j: [
                                 dma(out=s2_sb[:, k * BLK:(k + 1) * BLK],
                                     in_=s2dram.ap().rearrange(
                                         "(j k p) l -> j k p l", k=KD, p=128)[j, k])
                                 for k in range(KD)] and None or None, KD,
                             chan="s2s")

        out_tiles = {}
        h_out_dmas = []
        l_scr1f = None
        l_scr2f = l_scr2
        for q in range(NSUB):
            (h6a, h8b) = h_ab[q]
            hba = P.stage("T", [h6a, psr[4]], lambda e, q=q: e.matmul(
                ps[4][:, 0:SUB], s_onesb[0:1, :], a_row(q), start=True, stop=True))
            hbb = P.stage("T", [h8b, psr[5]], lambda e, q=q: e.matmul(
                ps[5][:, 0:SUB], s_onesb[0:1, :], b_row(q), start=True, stop=True))
            hgb = P.stage("T", [hs_g[q], psr[6]], lambda e, q=q: e.matmul(
                ps[6][:, 0:SUB], s_onesb[0:1, :], g_row(q), start=True, stop=True))
            for k in range(KD):
                h1 = P.stage("V", [hs_nonab[(k, q)], hs_abel[(k, q)], l_scr1f],
                             lambda e, k=k, q=q: e.tensor_tensor(
                                 scr1[:, 0:SUB],
                                 nonab[:, k * BLK + q * SUB:k * BLK + q * SUB + SUB],
                                 abel[:, k * BLK + q * SUB:k * BLK + q * SUB + SUB],
                                 OP.subtract))
                h2 = P.stage("V", [h1, hgb], lambda e: e.tensor_tensor(
                    scr1[:, 0:SUB], scr1[:, 0:SUB], ps[6][:, 0:SUB], OP.mult))
                psr[6] = h2
                h3 = P.stage("V", [h_lds2, hba, l_scr2f], lambda e, k=k, q=q:
                             e.tensor_tensor(
                                 scr2[:, 0:SUB],
                                 s2_sb[:, k * BLK + q * SUB:k * BLK + q * SUB + SUB],
                                 ps[4][:, 0:SUB], OP.mult))
                psr[4] = h3
                h4 = P.stage("V", [h3, hbb], lambda e: e.tensor_tensor(
                    scr2[:, 0:SUB], scr2[:, 0:SUB], ps[5][:, 0:SUB], OP.add))
                psr[5] = h4
                h5 = P.stage("V", [h4], lambda e, k=k, q=q: e.tensor_tensor(
                    scr2[:, 0:SUB], scr2[:, 0:SUB],
                    s2_sb[:, k * BLK + q * SUB:k * BLK + q * SUB + SUB], OP.add))
                h6 = P.stage("V", [h5], lambda e, k=k, q=q: e.tensor_tensor(
                    scr2[:, 0:SUB], scr2[:, 0:SUB],
                    abel[:, k * BLK + q * SUB:k * BLK + q * SUB + SUB], OP.add))
                h7 = P.stage("V", [h6, h2], lambda e: e.tensor_tensor(
                    scr2[:, 0:SUB], scr2[:, 0:SUB], scr1[:, 0:SUB], OP.add))
                l_scr1f = h7
                l_scr2f = h7
                for pz in range(SUB // 128):
                    htr = P.stage("T", [h7, psr[7]], lambda e, pz=pz: e.transpose(
                        ps[7][:, 0:128], scr2[:, pz * 128:(pz + 1) * 128],
                        s_identf[:, :]))
                    tglob = q * (SUB // 128) + pz
                    og = (k * (SUB // 128) + pz) % 2

                    def fst(e, og=og):
                        return e.tensor_copy(o_stg[:, og * 128:(og + 1) * 128],
                                             ps[7][:, 0:128])

                    hst = P.stage("V", [htr, ostg_war[og]], fst)
                    psr[7] = hst

                    def st_out1(e, dma, j=j, k=k, tglob=tglob, og=og):
                        dma(out=yout.ap()[j * BLK + tglob * 128:
                                          j * BLK + (tglob + 1) * 128,
                                          k * 128:(k + 1) * 128],
                            in_=o_stg[:, og * 128:(og + 1) * 128])

                    h_o1 = P.dma_stage("Y", [hst], st_out1, 1, chan=f"out{og}")
                    ostg_war[og] = h_o1
                    h_out_dmas.append(h_o1)
        h_out_final.extend(h_out_dmas)
        h_blk_prev = P.barrier()

    if os.environ.get("KDEBUG"):
        dbg_specs = [("xin_sc", xin_sc), ("dt_dram", dt_dram), ("bc_dram", bc_dram),
                     ("yagin", yagin), ("eagin", eagin), ("s2dram", s2dram),
                     ("rotdram", rotdram), ("yago8", yago)]
        dbg_outs = {}
        for nm, t_ in dbg_specs:
            dbg_outs[nm] = nc.dram_tensor(
                "dbg_" + nm, list(t_.shape), t_.dtype, kind="ExternalOutput")

        def f_dbg(e, dma):
            for nm, t_ in dbg_specs:
                dma(out=dbg_outs[nm].ap(), in_=t_.ap())

        h_dbg = P.dma_stage("G", P.barrier(), f_dbg, len(dbg_specs), chan="dbg")
        h_out_final = h_out_final + [h_dbg]

    P.stage("G", h_out_final + [h_par], lambda e: e.memset(s_fin[0:1, 0:1], 0.0))

    fb_.close()

    with nc.Block() as block:
        P.emit(block)
    es.close()
    return nc


def ld_rot_fn(rot_sb, rotdram, KD):
    def f(e, dma):
        dma(out=rot_sb.rearrange("p (k d) -> p k d", k=KD),
            in_=rotdram.ap().rearrange("(k p) d -> p k d", p=128))
    return f



def stage_core_inputs(cfg: Cfg, inp: dict, core: int):
    c = cfg
    KD, TI, TS, NST, DTR = c.KD, c.TI, c.TS, c.NST, c.DTR
    D, DI, L, E, DF = c.D, c.DI, c.L, c.E, c.DF
    MGA, DG2, MG1, BLK = c.MGA, c.DG2, c.MG1, c.BLK
    XG = c.XG
    b, q = core // 2, core % 2
    f32 = np.float32

    half = DI // 2
    if q == 0:
        sig = np.arange(DI)
    else:
        sig = np.concatenate([np.arange(half, DI), np.arange(0, half)])

    x = np.asarray(inp["L0_attn_x"], f32)
    xfull = np.ascontiguousarray(x[b])
    blkpos = [q * BLK, 2 * BLK + q * BLK]
    xown = np.ascontiguousarray(
        np.concatenate([x[b, blkpos[0]:blkpos[0] + BLK],
                        x[b, blkpos[1]:blkpos[1] + BLK]], axis=0))

    ipw = np.asarray(inp["L0_attn_in_proj_w"], f32)
    wx = ipw[:DI][sig]
    wz = ipw[DI:][sig]
    winx = np.ascontiguousarray(wx.T.reshape(KD, 128, XG, 512).transpose(0, 2, 1, 3))
    winz = np.ascontiguousarray(wz.T.reshape(KD, 128, XG, 512).transpose(0, 2, 1, 3))

    opw = np.asarray(inp["L0_attn_out_proj_w"], f32)
    wout = np.ascontiguousarray(
        opw[:, sig].T.reshape(TI, 128, KD // MGA, MGA * 128).transpose(0, 2, 1, 3))
    XP = DTR + 2 * NST
    xprojt = np.ascontiguousarray(
        np.asarray(inp["L0_attn_x_proj_w"], f32)[:, sig].T.reshape(TI, 128, XP))
    dtprojt = np.ascontiguousarray(
        np.asarray(inp["L0_attn_dt_proj_w"], f32)[sig][:TS * 128].T)
    cw = np.asarray(inp["L0_attn_conv_w"], f32)[sig, 0, :]
    convw = np.ascontiguousarray(
        cw.reshape(TI, 128, c.DCONV).transpose(1, 0, 2).reshape(128, TI * c.DCONV))
    convb = np.ascontiguousarray(
        np.asarray(inp["L0_attn_conv_b"], f32)[sig].reshape(TI, 128).T)
    dvec = np.ascontiguousarray(
        np.asarray(inp["L0_attn_D"], f32)[sig][:TS * 128].reshape(TS, 128).T)
    al = np.asarray(inp["L0_attn_A_log"], f32)[sig][:TS * 128]
    alog = np.ascontiguousarray(
        al.reshape(TS, 128, NST).transpose(1, 0, 2).reshape(128, TS * NST))
    dtb = np.ascontiguousarray(
        np.asarray(inp["L0_attn_dt_proj_b"], f32)[sig][:TS * 128].reshape(TS, 128).T)

    lng = np.ascontiguousarray(np.asarray(inp["L0_ffn_ln_g"], f32).reshape(KD, 128).T)
    lnb = np.ascontiguousarray(np.asarray(inp["L0_ffn_ln_b"], f32).reshape(KD, 128).T)
    abelb = np.ascontiguousarray(
        np.asarray(inp["L0_ffn_abel_b"], f32).reshape(KD, 128).T)
    abelw = np.ascontiguousarray(
        np.asarray(inp["L0_ffn_abel_w"], f32)
        .reshape(KD, 128, KD // MGA, MGA * 128).transpose(0, 2, 1, 3))
    gatew = np.ascontiguousarray(
        np.asarray(inp["L0_ffn_gate_w_moe"], f32).reshape(KD, 128, E))
    gateb = np.asarray(inp["L0_ffn_gate_b_moe"], f32).reshape(1, E)
    mixw = np.ascontiguousarray(
        np.asarray(inp["L0_ffn_mix_w"], f32)[:, 0].reshape(2 * KD, 128).T)
    mixb = np.asarray(inp["L0_ffn_mix_b"], f32).reshape(1, 1)
    w1t = np.ascontiguousarray(
        np.asarray(inp["L0_ffn_w1"], f32)
        .reshape(E, KD, 128, MG1, 256).transpose(0, 1, 3, 2, 4))
    w2t = np.ascontiguousarray(
        np.asarray(inp["L0_ffn_w2"], f32)
        .reshape(E, c.DFT, 128, KD // DG2, DG2 * 128).transpose(0, 3, 1, 2, 4))
    b1v = np.ascontiguousarray(
        np.asarray(inp["L0_ffn_b1"], f32)
        .reshape(E, c.DFT, 128).transpose(2, 0, 1).reshape(128, E * c.DFT))
    b2v = np.ascontiguousarray(np.asarray(inp["L0_ffn_b2"], f32))
    gens = np.asarray(inp["L0_ffn_gens"], f32)
    gmat = np.ascontiguousarray(gens[core % 3])
    gmatT = np.ascontiguousarray(gmat.T)

    identf = np.eye(128, dtype=f32)
    identb = np.eye(128, dtype=BF)
    onesf = np.ones((1, 128), dtype=f32)
    sel32 = np.zeros((32, 32 * 128), dtype=BF)
    for n in range(32):
        sel32[n, n * 128:(n + 1) * 128] = 1
    sel8 = np.zeros((8, 8 * 128), dtype=f32)
    for n in range(8):
        sel8[n, n * 128:(n + 1) * 128] = 1

    yoff = np.zeros((1, 4), dtype=np.int32)
    part = core ^ 1
    for j in range(2):
        for half, r in enumerate([core, part]):
            yoff[0, j * 2 + half] = (r * TS * 128) * L + blkpos[j]

    return {
        "xfull": xfull, "xown": xown, "winx": winx, "winz": winz, "wout": wout,
        "xprojt": xprojt, "dtprojt": dtprojt, "convw": convw, "convb": convb,
        "dvec": dvec, "alog": alog, "dtb": dtb, "lng": lng, "lnb": lnb,
        "abelb": abelb, "abelw": abelw, "gatew": gatew, "gateb": gateb,
        "mixw": mixw, "mixb": mixb, "w1t": w1t, "w2t": w2t, "b1v": b1v,
        "b2v": b2v, "gmat": gmat, "gmatT": gmatT, "identf": identf,
        "identb": identb, "onesf": onesf, "onescol": np.ones((128, 1), dtype=f32),
        "onesb": np.ones((1, 128), dtype=BF),
        "cvals": np.tile(np.array([[c.EPS_LN, -0.5, -1.0, 1.0]], dtype=f32), (128, 1)),
        "sel32": sel32, "sel8": sel8,
        "yoff": yoff,
    }


def assemble_output(cfg: Cfg, results):
    c = cfg
    out = np.zeros((4, c.L, c.D), dtype=np.float32)
    for core in range(8):
        b, q = core // 2, core % 2
        yo = np.asarray(results[core]["yout"])
        blkpos = [q * c.BLK, 2 * c.BLK + q * c.BLK]
        out[b, blkpos[0]:blkpos[0] + c.BLK] = yo[:c.BLK]
        out[b, blkpos[1]:blkpos[1] + c.BLK] = yo[c.BLK:]
    return out


def kernel(**inputs):
    from concourse.bass_utils import run_bass_kernel_spmd
    cfg = FULL
    nc = build_program(cfg)
    in_maps = [stage_core_inputs(cfg, inputs, core) for core in range(8)]
    res = run_bass_kernel_spmd(nc, in_maps, core_ids=list(range(8)))
    return assemble_output(cfg, res.results)


# version 13 (final)
# speedup vs baseline: 1.1058x; 1.0099x over previous
import os
from dataclasses import dataclass
from contextlib import ExitStack

import numpy as np
import ml_dtypes

import concourse.bass as bass
import concourse.mybir as mybir
from concourse.alu_op_type import AluOpType as OP

F32 = mybir.dt.float32
BF16 = mybir.dt.bfloat16
I32 = mybir.dt.int32
AF = mybir.ActivationFunctionType
AX = mybir.AxisListType.X
BF = ml_dtypes.bfloat16


@dataclass
class Cfg:
    D: int = 1024
    DI: int = 2048
    NST: int = 16
    DTR: int = 64
    DCONV: int = 4
    L: int = 4096
    E: int = 8
    DF: int = 4096
    LC: int = 512
    EPS_LN: float = 1e-5

    @property
    def KD(self): return self.D // 128
    @property
    def TI(self): return self.DI // 128
    @property
    def TS(self): return self.TI // 2
    @property
    def NCH(self): return self.L // self.LC
    @property
    def OWN(self): return self.L // 2
    @property
    def BLK(self): return self.OWN // 2
    @property
    def NSUB(self): return max(1, self.BLK // 512)
    @property
    def SUB(self): return min(512, self.BLK)
    @property
    def DFT(self): return self.DF // 128
    @property
    def MGA(self): return min(4, self.KD)
    @property
    def DG2(self): return min(4, self.KD)
    @property
    def MG1(self): return self.DF // 256
    @property
    def XG(self): return self.TI // 4
    @property
    def NTC(self): return self.BLK // 128
    @property
    def NTL(self): return self.LC // 128


FULL = Cfg()
SMALL = Cfg(D=256, DI=512, DTR=32, L=512, DF=1024, LC=128)

ENG_ATTR = {"T": "tensor", "V": "vector", "S": "scalar", "G": "gpsimd", "Y": "sync"}


class DmaChan:

    def __init__(self, nc, es, name):
        self.nc, self.es, self.name = nc, es, name
        self.sems, self.idx, self.cnt = [], -1, 0
        self._new()

    def _new(self):
        self.idx += 1
        self.sems.append(self.es.enter_context(
            self.nc.semaphore(f"dq_{self.name}_{self.idx}")))
        self.cnt = 0

    def next_handle(self, n_dmas):
        if self.cnt + 16 * n_dmas > 60000:
            self._new()
        sem = self.sems[self.idx]
        self.cnt += 16 * n_dmas
        return (sem, self.cnt)


class Prog:

    def __init__(self, nc, es):
        self.nc, self.es = nc, es
        self.sems = {k: es.enter_context(nc.semaphore(f"sem_{k}")) for k in "TVSG"}
        self.cnt = {k: 0 for k in "TVSG"}
        self.chans = {}
        self.stages = {k: [] for k in ENG_ATTR}

    @staticmethod
    def _flat(deps):
        out = []
        for d in deps:
            if d is None:
                continue
            if isinstance(d, list):
                out.extend(x for x in d if x is not None)
            else:
                out.append(d)
        return out

    def stage(self, eng, deps, fn):
        waits = self._flat(deps)
        self.cnt[eng] += 1
        handle = (self.sems[eng], self.cnt[eng])
        self.stages[eng].append(("c", waits, fn, handle))
        return handle

    def dma_stage(self, eng, deps, fn, n_dmas, chan="misc"):
        waits = self._flat(deps)
        if chan not in self.chans:
            self.chans[chan] = DmaChan(self.nc, self.es, chan)
        h = self.chans[chan].next_handle(n_dmas)
        self.stages[eng].append(("d", waits, fn, (h, n_dmas)))
        return h

    def barrier(self):
        hs = [(self.sems[k], self.cnt[k]) for k in "TVSG" if self.cnt[k] > 0]
        for q in self.chans.values():
            if q.cnt > 0:
                hs.append((q.sems[q.idx], q.cnt))
        return hs

    def emit(self, block):
        for eng, attr in ENG_ATTR.items():
            stages = self.stages[eng]
            if not stages:
                continue

            def runner(e, stages=stages):
                watermark = {}
                import os as _os
                nodedup = bool(_os.environ.get("KNODEDUP"))
                for kind, waits, fn, info in stages:
                    for (sem, val) in waits:
                        if not nodedup and watermark.get(sem, -1) >= val:
                            continue
                        e.wait_ge(sem, val)
                        watermark[sem] = val
                    if kind == "c":
                        sem, val = info
                        last = fn(e)
                        assert last is not None, "stage fn returned None"
                        last.then_inc(sem, 1)
                    else:
                        (sem, endval), n = info
                        issued = [0]

                        def dma(out, in_, e=e, sem=sem, issued=issued):
                            ins = e.dma_start(out=out, in_=in_)
                            ins.then_inc(sem, 16)
                            issued[0] += 1
                            return ins

                        fn(e, dma)
                        assert issued[0] == n, f"dma stage issued {issued[0]} != {n}"

            getattr(block, attr)(runner)


class WRing:

    def __init__(self, P, bufs, name, init_dep):
        self.P = P
        self.bufs = bufs
        self.name = name
        self.n = len(bufs)
        self.last_use = [init_dep] * self.n
        self.i = 0

    def load(self, src_ap, width):
        slot = self.i % self.n
        self.i += 1
        buf = self.bufs[slot]

        def f(e, dma, buf=buf, src_ap=src_ap, width=width):
            dma(out=buf[:, 0:width], in_=src_ap)

        h = self.P.dma_stage("G", [self.last_use[slot]], f, 1,
                             chan=f"{self.name}{slot}")
        return slot, h

    def used(self, slot, handle):
        self.last_use[slot] = handle


def build_program(cfg: Cfg):
    PH = int(os.environ.get("KPHASES", "99"))
    c = cfg
    nc = bass.Bass("TRN2", target_bir_lowering=False, debug=False)
    es = ExitStack()

    KD, TI, TS, NST, DTR = c.KD, c.TI, c.TS, c.NST, c.DTR
    LC, NCH, L, D = c.LC, c.NCH, c.L, c.D
    BLK, NSUB, SUB, DFT, E = c.BLK, c.NSUB, c.SUB, c.DFT, c.E
    MGA, DG2, MG1, XG, NTC, NTL = c.MGA, c.DG2, c.MG1, c.XG, c.NTC, c.NTL
    NSD = -(-D // 512)
    XP = DTR + 2 * NST

    def inp(name, shape, dtype=F32):
        return nc.dram_tensor(name, list(shape), dtype, kind="ExternalInput")

    xfull = inp("xfull", [L, D])
    xown = inp("xown", [c.OWN, D])
    winx = inp("winx", [KD, XG, 128, 512])
    winz = inp("winz", [KD, XG, 128, 512])
    wout = inp("wout", [TI, KD // MGA, 128, MGA * 128])
    xprojt = inp("xprojt", [TI, 128, XP])
    dtprojt = inp("dtprojt", [DTR, TS * 128])
    convw = inp("convw", [128, TI * c.DCONV])
    convb = inp("convb", [128, TI])
    dvec = inp("dvec", [128, TS])
    alog = inp("alog", [128, TS * NST])
    dtb = inp("dtb", [128, TS])
    lng = inp("lng", [128, KD])
    lnb = inp("lnb", [128, KD])
    abelb = inp("abelb", [128, KD])
    abelw = inp("abelw", [KD, KD // MGA, 128, MGA * 128])
    gatew = inp("gatew", [KD, 128, E])
    gateb = inp("gateb", [1, E])
    mixw = inp("mixw", [128, 2 * KD])
    mixb = inp("mixb", [1, 1])
    w1t = inp("w1t", [E, KD, MG1, 128, 256])
    w2t = inp("w2t", [E, KD // DG2, DFT, 128, DG2 * 128])
    b1v = inp("b1v", [128, E * DFT])
    b2v = inp("b2v", [E, D])
    gmat = inp("gmat", [D, D])
    gmatT = inp("gmatT", [D, D])
    identf = inp("identf", [128, 128])
    identb = inp("identb", [128, 128], BF16)
    onesf = inp("onesf", [1, 128])
    onescol = inp("onescol", [128, 1])
    onesb = inp("onesb", [1, 128], BF16)
    cvals = inp("cvals", [128, 4])
    sel32 = inp("sel32", [32, 32 * 128], BF16)
    sel8 = inp("sel8", [8, 8 * 128])
    yoff = inp("yoff", [1, 4], I32)

    xin_sc = nc.dram_tensor("xin_sc", [TS * 128, L], BF16, kind="Internal")
    dt_dram = nc.dram_tensor("dt_dram", [DTR, L], F32, kind="Internal")
    bc_dram = nc.dram_tensor("bc_dram", [2 * NST, L], F32, kind="Internal")
    yagin_h = [nc.dram_tensor(f"yagin{h}", [TS * 128, L // 2], BF16,
                              kind="Internal") for h in range(2)]
    yago_h = [nc.dram_tensor(f"yago{h}", [8 * TS * 128, L // 2], BF16,
                             kind="Internal", addr_space="Shared")
              for h in range(2)]
    eagin = nc.dram_tensor("eagin", [D, D], BF16, kind="Internal")
    eago = nc.dram_tensor("eago", [8 * D, D], BF16, kind="Internal", addr_space="Shared")
    s2dram = nc.dram_tensor("s2dram", [2 * KD * 128, BLK], F32, kind="Internal")
    rotdram = nc.dram_tensor("rotdram", [KD * 128, D], BF16, kind="Internal")
    yout = nc.dram_tensor("yout", [c.OWN, D], F32, kind="ExternalOutput")

    P = Prog(nc, es)
    sb = lambda nm, shp, dt=F32: es.enter_context(nc.sbuf_tensor(nm, shp, dt))
    ps = [es.enter_context(nc.psum_tensor(f"ps{i}", [128, 512], F32)) for i in range(8)]
    psr = [None] * 8

    s_identf = sb("s_identf", [128, 128])
    s_identb = sb("s_identb", [128, 128], BF16)
    s_onesf = sb("s_onesf", [1, 128])
    s_onescol = sb("s_onescol", [128, 1])
    s_onesb = sb("s_onesb", [1, 128], BF16)
    s_cvals = sb("s_cvals", [128, 4])
    s_sel8 = sb("s_sel8", [8, 8 * 128], BF16)
    s_convw = sb("s_convw", [128, TI * c.DCONV])
    s_convb = sb("s_convb", [128, TI])
    s_dvec = sb("s_dvec", [128, TS])
    s_aneg = sb("s_aneg", [128, TS * NST])
    s_dtb = sb("s_dtb", [128, TS])
    s_lng = sb("s_lng", [128, KD])
    s_lnb = sb("s_lnb", [128, KD])
    s_abelb = sb("s_abelb", [128, KD])
    s_gatew = sb("s_gatew", [128, KD * E], BF16)
    s_gatebbc = sb("s_gatebbc", [128, E])
    s_gateb_row = sb("s_gateb_row", [1, E])
    s_mixw = sb("s_mixw", [128, 2 * KD], BF16)
    s_mixb = sb("s_mixb", [1, 1])
    s_b2 = sb("s_b2", [8, D], BF16)
    s_b1 = sb("s_b1", [128, E * DFT])
    s_yoff = sb("s_yoff", [1, 4], I32)
    s_fin = sb("s_fin", [1, 1])
    s_agn = sb("s_agn", [1, 1])
    x_tm = [sb(f"x_tm{i}", [128, D]) for i in range(2)]
    xn_tm = [sb(f"xn_tm{i}", [128, D], BF16) for i in range(2)]
    s_cst = sb("s_cst", [128, 6])
    s_sq = sb("s_sq", [128, D])

    def ld_params(e, dma):
        for dst, src in [(s_identf, identf), (s_identb, identb), (s_onesf, onesf),
                         (s_onescol, onescol), (s_onesb, onesb),
                         (s_cvals, cvals), (s_convw, convw),
                         (s_convb, convb), (s_dvec, dvec), (s_aneg, alog),
                         (s_dtb, dtb), (s_lng, lng), (s_lnb, lnb),
                         (s_abelb, abelb), (s_mixb, mixb),
                         (s_b1, b1v), (s_yoff, yoff), (s_gateb_row, gateb)]:
            dma(out=dst[:, :], in_=src.ap())

    h_par = P.dma_stage("Y", [], ld_params, 18, chan="par")
    for _i, _v in enumerate([c.EPS_LN, -0.5, -1.0, 1.0]):
        nc.const_aps.register(F32, _v, s_cvals[:, _i:_i + 1])

    def ld_params_cast(e, dma):
        dma(out=s_gatew[:, :].rearrange("p (k e) -> p k e", k=KD), in_=gatew.ap().rearrange("k p e -> p k e"))
        dma(out=s_mixw[:, :], in_=mixw.ap())
        dma(out=s_sel8[:, :], in_=sel8.ap())
        dma(out=s_b2[:, :], in_=b2v.ap())

    h_parc = P.dma_stage("G", [], ld_params_cast, 4, chan="parc")

    h_aexp = P.stage("S", [h_par], lambda e: e.activation(
        s_aneg[:, :], s_aneg[:, :], AF.Exp))
    h_aneg = P.stage("V", [h_aexp], lambda e: e.tensor_scalar_mul(
        s_aneg[:, :], s_aneg[:, :], -1.0))

    h_gb1 = P.stage("T", [h_par, psr[0]], lambda e: e.matmul(
        ps[0][0:128, 0:E], s_onesf[0:1, :], s_gateb_row[0:1, 0:E],
        start=True, stop=True))
    h_gbbc = P.stage("V", [h_gb1], lambda e: e.tensor_copy(
        s_gatebbc[:, :], ps[0][0:128, 0:E]))
    psr[0] = h_gbbc

    def mm_DD(dst, lhsT_sb, rhs_sb, deps, fuse_add_blockpoly=None, scr_tracker=None):
        hs = []
        for m in range(KD):
            for s_ in range(NSD):
                w = min(512, D - s_ * 512)
                lo = m * D + s_ * 512
                pi = (m * NSD + s_) % 6

                def f(e, m=m, s_=s_, w=w, pi=pi, lhsT_sb=lhsT_sb, rhs_sb=rhs_sb):
                    ins = None
                    for k in range(KD):
                        ins = e.matmul(
                            ps[pi][:, 0:w],
                            lhsT_sb[:, k * D + m * 128:k * D + (m + 1) * 128],
                            rhs_sb[:, k * D + s_ * 512:k * D + s_ * 512 + w],
                            start=(k == 0), stop=(k == KD - 1))
                    return ins

                hm = P.stage("T", deps + [psr[pi]], f)
                if fuse_add_blockpoly is None:
                    hc = P.stage("V", [hm], lambda e, lo=lo, w=w, pi=pi, dst=dst:
                                 e.tensor_copy(dst[:, lo:lo + w], ps[pi][:, 0:w]))
                else:
                    hpoly = fuse_add_blockpoly(m, s_, w, lo, deps)
                    hc = P.stage("V", [hm, hpoly], lambda e, lo=lo, w=w, pi=pi, dst=dst:
                                 e.tensor_tensor(dst[:, lo:lo + w], sScr[:, 0:w],
                                                 ps[pi][:, 0:w], OP.add))
                    if scr_tracker is not None:
                        scr_tracker[0] = hc
                psr[pi] = hc
                hs.append(hc)
        return hs

    def transpose_DD(src, dst, deps):
        hs = []
        for km in range(KD):
            for kk in range(KD):
                pi = 6 + (km * KD + kk) % 2

                def ftr(e, km=km, kk=kk, pi=pi, src=src):
                    return e.transpose(
                        ps[pi].ap().bitcast(BF16)[:, 0:128],
                        src[:, kk * D + km * 128:kk * D + (km + 1) * 128],
                        s_identb[:, :])

                ht = P.stage("T", deps + [psr[pi], h_par], ftr)

                def ftc(e, km=km, kk=kk, pi=pi, dst=dst):
                    return e.tensor_copy(
                        dst[:, km * D + kk * 128:km * D + (kk + 1) * 128],
                        ps[pi].ap().bitcast(BF16)[:, 0:128])

                hc = P.stage("V", [ht], ftc)
                psr[pi] = hc
                hs.append(hc)
        return hs


    def _finish_early():
        P.stage("G", P.barrier() + [h_par], lambda e: e.memset(s_fin[0:1, 0:1], 0.0))
        with nc.Block() as block:
            P.emit(block)
        return nc
    p0 = ExitStack()
    sG = p0.enter_context(nc.sbuf_tensor("sG", [128, KD * D], BF16))
    sGT = p0.enter_context(nc.sbuf_tensor("sGT", [128, KD * D], BF16))
    sG2 = p0.enter_context(nc.sbuf_tensor("sG2", [128, KD * D], BF16))
    sG3 = p0.enter_context(nc.sbuf_tensor("sG3", [128, KD * D], BF16))
    sG4 = p0.enter_context(nc.sbuf_tensor("sG4", [128, KD * D], BF16))
    sG4T = p0.enter_context(nc.sbuf_tensor("sG4T", [128, KD * D], BF16))
    sT0 = p0.enter_context(nc.sbuf_tensor("sT0", [128, KD * D], BF16))
    sT1 = p0.enter_context(nc.sbuf_tensor("sT1", [128, KD * D], BF16))
    sScr = p0.enter_context(nc.sbuf_tensor("sScr", [128, 512], F32))

    def ld_g(e, dma):
        dma(out=sG[:, :].rearrange("p (k d) -> p k d", k=KD), in_=gmat.ap().rearrange("(k p) d -> p k d", p=128))
        dma(out=sGT[:, :].rearrange("p (k d) -> p k d", k=KD), in_=gmatT.ap().rearrange("(k p) d -> p k d", p=128))

    h_g = P.dma_stage("G", [], ld_g, 2, chan="gmat")

    hG2 = mm_DD(sG2, sGT, sG, [h_g])
    hG3 = mm_DD(sG3, sGT, sG2, [h_g] + hG2)
    hG4 = mm_DD(sG4, sGT, sG3, [h_g] + hG3)
    hG4T = transpose_DD(sG4, sG4T, hG4)

    fact = [1.0]
    for i in range(1, 12):
        fact.append(fact[-1] * i)
    coef = [1.0 / f for f in fact]

    last_scr = [None]

    def mk_blockpoly(cidx):
        c0, c1, c2, c3 = coef[cidx], coef[cidx + 1], coef[cidx + 2], coef[cidx + 3]

        def fn(m, s_, w, lo, deps):
            h1 = P.stage("V", deps + [last_scr[0]], lambda e, lo=lo, w=w:
                         e.tensor_scalar(sScr[:, 0:w], sG[:, lo:lo + w], c1, None, OP.mult))
            h2 = P.stage("V", [h1], lambda e, lo=lo, w=w: e.scalar_tensor_tensor(
                sScr[:, 0:w], sG2[:, lo:lo + w], c2, sScr[:, 0:w], OP.mult, OP.add))
            h3 = P.stage("V", [h2], lambda e, lo=lo, w=w: e.scalar_tensor_tensor(
                sScr[:, 0:w], sG3[:, lo:lo + w], c3, sScr[:, 0:w], OP.mult, OP.add))
            dcol = m * 128 - s_ * 512
            if 0 <= dcol <= w - 128:
                h3 = P.stage("V", [h3], lambda e, dcol=dcol: e.scalar_tensor_tensor(
                    sScr[:, dcol:dcol + 128], s_identf[:, :], c0,
                    sScr[:, dcol:dcol + 128], OP.mult, OP.add))
            last_scr[0] = h3
            return h3

        return fn

    hB2 = []
    pol2 = mk_blockpoly(8)
    for m in range(KD):
        for s_ in range(NSD):
            w = min(512, D - s_ * 512)
            lo = m * D + s_ * 512
            hp = pol2(m, s_, w, lo, hG2 + hG3 + [h_g, h_par])
            hc = P.stage("V", [hp], lambda e, lo=lo, w=w: e.tensor_copy(
                sT0[:, lo:lo + w], sScr[:, 0:w]))
            last_scr[0] = hc
            hB2.append(hc)

    hT1 = mm_DD(sT1, sG4T, sT0, hG4T + hB2 + [h_g], fuse_add_blockpoly=mk_blockpoly(4), scr_tracker=last_scr)
    hE = mm_DD(sT0, sG4T, sT1, hG4T + hT1 + [h_g], fuse_add_blockpoly=mk_blockpoly(0), scr_tracker=last_scr)

    h_eout = P.dma_stage("Y", hE, lambda e, dma: dma(
        out=eagin.ap().rearrange("(k p) d -> p k d", p=128),
        in_=sT0[:, :].rearrange("p (k d) -> p k d", k=KD)), 1, chan="eout")

    bar_p0 = P.barrier()
    p0.close()
    if PH <= 1:
        return _finish_early()

    ln_state = {"last": None}
    lnwar = {0: [], 1: []}

    def ln1_tokens(src_dram, row0, bi, extra_deps):
        hld = P.dma_stage("Y", extra_deps + [ln_state["last"]] + lnwar[bi],
                          lambda e, dma: dma(
                              out=x_tm[bi][:, :],
                              in_=src_dram.ap()[row0:row0 + 128, :]),
                          1, chan=f"xt{bi}")
        h1 = P.stage("V", [hld, ln_state["last"]], lambda e: e.reduce_sum(
            s_cst[:, 0:1], x_tm[bi][:, :], axis=AX))
        h2 = P.stage("S", [hld, ln_state["last"]], lambda e: e.activation(
            s_sq[:, :], x_tm[bi][:, :], AF.Square))
        h3 = P.stage("V", [h2, h1], lambda e: e.reduce_sum(
            s_cst[:, 1:2], s_sq[:, :], axis=AX))
        h4 = P.stage("V", [h3], lambda e: e.tensor_scalar(
            s_cst[:, 2:3], s_cst[:, 0:1], 1.0 / D, None, OP.mult))
        h5 = P.stage("V", [h4], lambda e: e.tensor_tensor(
            s_cst[:, 3:4], s_cst[:, 2:3], s_cst[:, 2:3], OP.mult))
        h6 = P.stage("V", [h5], lambda e: e.tensor_scalar(
            s_cst[:, 4:5], s_cst[:, 1:2], 1.0 / D, s_cst[:, 3:4], OP.mult, OP.subtract))
        h7 = P.stage("S", [h6, h_par], lambda e: e.activation(
            s_cst[:, 5:6], s_cst[:, 4:5], AF.Ln, bias=c.EPS_LN))
        h8 = P.stage("S", [h7], lambda e: e.activation(
            s_cst[:, 5:6], s_cst[:, 5:6], AF.Exp, scale=-0.5))
        h9 = P.stage("V", [h8], lambda e: e.tensor_scalar(
            xn_tm[bi][:, :], x_tm[bi][:, :], s_cst[:, 2:3], s_cst[:, 5:6],
            OP.subtract, OP.mult))
        ln_state["last"] = h9
        return h9

    fr = ExitStack()
    xn_fm = [fr.enter_context(nc.sbuf_tensor(f"xn_fm{i}", [128, KD * LC], BF16))
             for i in range(2)]
    NWF = 2 * KD
    s_win = [fr.enter_context(nc.sbuf_tensor(f"s_win{i}", [128, 512], BF16))
             for i in range(NWF)]
    xpre = [fr.enter_context(nc.sbuf_tensor(f"xpre{i}", [128, LC + 3], F32))
            for i in range(3)]
    xtail = fr.enter_context(nc.sbuf_tensor("xtail", [128, TI * 3], F32))
    s_xin = fr.enter_context(nc.sbuf_tensor("s_xin", [128, TI * LC], BF16))
    s_xdbl = fr.enter_context(nc.sbuf_tensor("s_xdbl", [XP, LC], F32))
    s_cvs = fr.enter_context(nc.sbuf_tensor("s_cvs", [128, LC], F32))
    s_xprojt = fr.enter_context(nc.sbuf_tensor("s_xprojt", [128, TI * XP], BF16))
    h_xprj = P.dma_stage("G", bar_p0, lambda e, dma: dma(
        out=s_xprojt[:, :].rearrange("p (t n) -> p t n", t=TI),
        in_=xprojt.ap().rearrange("t p n -> p t n")), 1, chan="xprj")

    h_xt0 = P.stage("V", bar_p0, lambda e: e.memset(xtail[:, :], 0.0))
    wring_f = WRing(P, s_win, "wf", bar_p0)

    h_xinsp = [None] * NCH
    h_dtsp = [None] * NCH
    lw = [[h_parc] + bar_p0, [h_parc] + bar_p0]
    l_xnfm = [bar_p0, bar_p0]
    l_xpre = [bar_p0, bar_p0, bar_p0]
    l_xin = [bar_p0]
    l_xdbl = bar_p0
    l_tail = [h_xt0] * TI
    l_cvs = bar_p0

    def emit_front(ch):
        nonlocal l_xdbl, l_cvs, l_xin
        bi = ch % 2
        hs_tr = []
        for tt in range(NTL):
            h9 = ln1_tokens(xfull, ch * LC + tt * 128, bi, bar_p0 if ch == 0 else [])
            lnwar[bi] = [h9]
            for k in range(KD):
                pi = (tt * KD + k) % 4

                def ftr(e, bi=bi, k=k, pi=pi):
                    return e.transpose(ps[pi].ap().bitcast(BF16)[:, 0:128],
                                       xn_tm[bi][:, k * 128:(k + 1) * 128],
                                       s_identb[:, :])

                ht = P.stage("T", [h9, psr[pi], h_par], ftr)

                def ftc(e, bi=bi, k=k, tt=tt, pi=pi):
                    return e.tensor_copy(
                        xn_fm[bi][:, k * LC + tt * 128:k * LC + (tt + 1) * 128],
                        ps[pi].ap().bitcast(BF16)[:, 0:128])

                hc = P.stage("V", [ht, l_xnfm[bi]], ftc)
                psr[pi] = hc
                lnwar[bi].append(ht)
                hs_tr.append(hc)

        hs_sil = {}
        for g in range(XG):
            tiles = [wring_f.load(winx.ap()[k, g], 512) for k in range(KD)]

            def fgrp(e, tiles=tiles, bi=bi):
                ins = None
                for k in range(KD):
                    slot, _ = tiles[k]
                    for m4 in range(4):
                        ins = e.matmul(ps[4 + m4][:, 0:LC],
                                       s_win[slot][:, m4 * 128:(m4 + 1) * 128],
                                       xn_fm[bi][:, k * LC:(k + 1) * LC],
                                       start=(k == 0), stop=(k == KD - 1))
                return ins

            hgrp = P.stage("T", [h for _, h in tiles] + hs_tr +
                           [psr[4 + m4] for m4 in range(4)], fgrp)
            for slot, _ in tiles:
                wring_f.used(slot, hgrp)
            group_last_mm = {m4: hgrp for m4 in range(4)}
            l_xnfm_new = hgrp
            for m4 in range(4):
                t = g * 4 + m4
                pi = 4 + m4
                xb = xpre[t % 3]
                h_t0 = P.stage("V", [l_tail[t], l_xpre[t % 3]],
                               lambda e, xb=xb, t=t: e.tensor_copy(
                                   xb[:, 0:3], xtail[:, t * 3:t * 3 + 3]))
                h_ev = P.stage("S", [group_last_mm[m4], h_t0],
                               lambda e, xb=xb, pi=pi: e.copy(
                                   xb[:, 3:3 + LC], ps[pi][:, 0:LC]))
                psr[pi] = h_ev
                h_tl = P.stage("V", [h_ev], lambda e, xb=xb, t=t: e.tensor_copy(
                    xtail[:, t * 3:t * 3 + 3], xb[:, LC:LC + 3]))
                h_c = P.stage("V", [h_ev, l_cvs], lambda e, xb=xb, t=t: e.tensor_scalar(
                    s_cvs[:, 0:LC], xb[:, 0:LC], s_convw[:, t * 4:t * 4 + 1],
                    None, OP.mult))
                for kk in range(1, c.DCONV):
                    h_c = P.stage("V", [h_c], lambda e, xb=xb, t=t, kk=kk:
                                  e.scalar_tensor_tensor(
                                      s_cvs[:, 0:LC], xb[:, kk:kk + LC],
                                      s_convw[:, t * 4 + kk:t * 4 + kk + 1],
                                      s_cvs[:, 0:LC], OP.mult, OP.add))
                h_si = P.stage("S", [h_c] + l_xin, lambda e, t=t: e.activation(
                    s_xin[:, t * LC:(t + 1) * LC], s_cvs[:, 0:LC], AF.Silu,
                    bias=s_convb[:, t:t + 1]))
                l_cvs = h_si
                l_xpre[t % 3] = h_si
                l_tail[t] = h_tl
                hs_sil[t] = h_si
        l_xnfm[bi] = l_xnfm_new
        l_xin = []

        def sp_xin(e, dma, ch=ch):
            for t in range(TS):
                dma(out=xin_sc.ap().rearrange("(t p) l -> t p l", p=128)
                    [t, :, ch * LC:(ch + 1) * LC],
                    in_=s_xin[:, t * LC:(t + 1) * LC])

        h_xinsp[ch] = P.dma_stage("Y", [hs_sil[t] for t in range(TS)], sp_xin, TS,
                                  chan="spx")
        l_xin.append(h_xinsp[ch])

        def fxp(e):
            ins = None
            for k in range(TI):
                ins = e.matmul(ps[0][0:XP, 0:LC],
                               s_xprojt[:, k * XP:(k + 1) * XP],
                               s_xin[:, k * LC:(k + 1) * LC],
                               start=(k == 0), stop=(k == TI - 1))
            return ins

        h_xp = P.stage("T", [hs_sil[t] for t in range(TI)] + [psr[0], h_xprj], fxp)
        l_xin.append(h_xp)

        h_xd = P.stage("V", [h_xp, l_xdbl], lambda e: e.tensor_copy(
            s_xdbl[0:XP, 0:LC], ps[0][0:XP, 0:LC]))
        psr[0] = h_xd

        def sp_dt(e, dma, ch=ch):
            dma(out=dt_dram.ap()[:, ch * LC:(ch + 1) * LC], in_=s_xdbl[0:DTR, 0:LC])
            dma(out=bc_dram.ap()[:, ch * LC:(ch + 1) * LC],
                in_=s_xdbl[DTR:DTR + 2 * NST, 0:LC])

        h_dtsp[ch] = P.dma_stage("Y", [h_xd], sp_dt, 2, chan=f"spdt{ch}")
        l_xdbl = h_dtsp[ch]


    sc = ExitStack()
    s_sel32 = sc.enter_context(nc.sbuf_tensor("s_sel32", [32, 32 * 128], BF16))
    s_dtprojt = sc.enter_context(nc.sbuf_tensor("s_dtprojt", [DTR, TS * 128], BF16))
    h_sel32 = P.dma_stage("G", bar_p0, lambda e, dma: (
        dma(out=s_sel32[:, :], in_=sel32.ap()),
        dma(out=s_dtprojt[:, :], in_=dtprojt.ap())), 2, chan="sel32")
    z_xin = [sc.enter_context(nc.sbuf_tensor(f"z_xin{i}", [128, TS * LC], BF16))
             for i in range(2)]
    z_dt = [sc.enter_context(nc.sbuf_tensor(f"z_dt{i}", [DTR, LC], BF16))
            for i in range(2)]
    z_bc = [sc.enter_context(nc.sbuf_tensor(f"z_bc{i}", [32, LC], BF16))
            for i in range(2)]
    z_bcr = sc.enter_context(nc.sbuf_tensor("z_bcr", [128, 2 * NST * LC], BF16))
    z_dl = [sc.enter_context(nc.sbuf_tensor(f"z_dl{i}", [128, LC], F32)) for i in range(2)]
    z_du = [sc.enter_context(nc.sbuf_tensor(f"z_du{i}", [128, LC], BF16))
            for i in range(2)]
    z_da = [sc.enter_context(nc.sbuf_tensor(f"z_da{i}", [128, LC], F32))
            for i in range(2)]
    z_db = [sc.enter_context(nc.sbuf_tensor(f"z_db{i}", [128, 2 * LC], F32))
            for i in range(2)]
    z_h = sc.enter_context(nc.sbuf_tensor("z_h", [128, 2 * LC], F32))
    z_tmp = [sc.enter_context(nc.sbuf_tensor(f"z_tmp{i}", [128, 2 * LC], F32))
             for i in range(2)]
    z_y = [sc.enter_context(nc.sbuf_tensor(f"z_y{i}", [128, LC], F32)) for i in range(2)]
    z_y16 = [sc.enter_context(nc.sbuf_tensor(f"z_y16{i}", [128, LC], BF16))
             for i in range(2)]
    z_sp = sc.enter_context(nc.sbuf_tensor("z_sp", [128, LC], F32))
    z_sp2 = sc.enter_context(nc.sbuf_tensor("z_sp2", [128, LC], F32))
    carry_sb = sc.enter_context(nc.sbuf_tensor("carry_sb", [128, TS * NST], F32))

    l_zxin = [bar_p0, bar_p0]
    l_zdt = [bar_p0, bar_p0]
    l_zbc = [bar_p0, bar_p0]
    l_bcr = bar_p0
    l_dl = [bar_p0, bar_p0]
    l_du = [bar_p0, bar_p0]
    l_da = [bar_p0, bar_p0]
    l_db = [bar_p0] * 2
    l_h = [bar_p0, bar_p0]
    l_tmp = [bar_p0] * 2
    l_y = [bar_p0, bar_p0]
    l_y16 = [bar_p0, bar_p0]
    l_sp = bar_p0
    l_sp2 = bar_p0
    h_ysp = []
    carry_h = {}

    def emit_scan(chs):
        nonlocal l_bcr, l_sp, l_sp2
        bi = chs % 2

        def ld_scan(e, dma, chs=chs, bi=bi):
            for t in range(TS):
                dma(out=z_xin[bi][:, t * LC:(t + 1) * LC],
                    in_=xin_sc.ap().rearrange("(t p) l -> t p l", p=128)
                    [t, :, chs * LC:(chs + 1) * LC])

        h_ldx = P.dma_stage("Y", [h_xinsp[chs], l_zxin[bi]], ld_scan, TS,
                            chan=f"lsx{bi}")

        def ld_dtbc(e, dma, chs=chs, bi=bi):
            dma(out=z_dt[bi][:, :], in_=dt_dram.ap()[:, chs * LC:(chs + 1) * LC])
            dma(out=z_bc[bi][:, :], in_=bc_dram.ap()[:, chs * LC:(chs + 1) * LC])

        h_lddt = P.dma_stage("G", [h_dtsp[chs], l_zdt[bi], l_zbc[bi]],
                             ld_dtbc, 2, chan=f"ldt{bi}")

        hs_bcr = []
        for n in range(2 * NST):
            pi = n % 2

            def fb(e, n=n, pi=pi, bi=bi):
                return e.matmul(ps[pi][:, 0:LC],
                                s_sel32[:, n * 128:(n + 1) * 128],
                                z_bc[bi][:, :], start=True, stop=True)

            hbm = P.stage("T", [h_lddt, psr[pi], h_sel32], fb)

            def fbc(e, n=n, pi=pi):
                return e.tensor_copy(z_bcr[:, n * LC:(n + 1) * LC],
                                     ps[pi][:, 0:LC])

            hbc = P.stage("V", [hbm, l_bcr], fbc)
            psr[pi] = hbc
            hs_bcr.append(hbc)
        l_zbc[bi] = hs_bcr[-1]

        last_reader_bcr = None
        for t in range(TS):
            ti = t % 2

            def fdt(e, t=t, bi=bi):
                return e.matmul(ps[2][:, 0:LC],
                                s_dtprojt[:, t * 128:(t + 1) * 128],
                                z_dt[bi][:, :], start=True, stop=True)

            hdtm = P.stage("T", [h_lddt, psr[2], h_sel32], fdt)
            hr = P.stage("S", [hdtm] + [l_dl[ti]], lambda e, t=t, ti=ti: e.activation(
                z_dl[ti][:, :], ps[2][:, 0:LC], AF.Relu, bias=s_dtb[:, t:t + 1]))
            ha = P.stage("S", [hdtm, l_sp], lambda e, t=t: e.activation(
                z_sp[:, :], ps[2][:, 0:LC], AF.Abs, bias=s_dtb[:, t:t + 1]))
            psr[2] = ha
            he = P.stage("S", [ha, l_sp2, h_par], lambda e: e.activation(
                z_sp2[:, :], z_sp[:, :], AF.Exp, scale=-1.0))
            hl = P.stage("S", [he], lambda e: e.activation(
                z_sp[:, :], z_sp2[:, :], AF.Ln, bias=1.0))
            l_sp = hl
            l_sp2 = hl
            hdl = P.stage("V", [hl, hr], lambda e, ti=ti: e.tensor_tensor(
                z_dl[ti][:, :], z_dl[ti][:, :], z_sp[:, :], OP.add))
            l_sp = hdl
            hdu = P.stage("V", [hdl, h_ldx] + [l_du[ti]], lambda e, t=t, ti=ti, bi=bi:
                          e.tensor_tensor(z_du[ti][:, :], z_dl[ti][:, :],
                                          z_xin[bi][:, t * LC:(t + 1) * LC], OP.mult))

            hda_last = None

            def emit_hdb(n2):
                pb = n2 % 2
                h = P.stage("G", [hdu, hs_bcr[2 * n2], hs_bcr[2 * n2 + 1], l_db[pb]],
                            lambda e, n2=n2, ti=ti, pb=pb:
                            e.tensor_tensor(
                                z_db[pb][:, :].rearrange("p (a l) -> p a l", a=2),
                                z_du[ti][:, :].unsqueeze(1).to_broadcast([128, 2, LC]),
                                z_bcr[:, 2 * n2 * LC:(2 * n2 + 2) * LC]
                                .rearrange("p (a l) -> p a l", a=2), OP.mult))
                return h

            def emit_hda(n):
                nb = n % 2
                return P.stage("S", [hdl, l_da[nb], h_aneg],
                               lambda e, t=t, n=n, ti=ti, nb=nb:
                               e.activation(z_da[nb][:, :], z_dl[ti][:, :], AF.Exp,
                                            scale=s_aneg[:, t * NST + n:t * NST + n + 1]))

            hdbs = {0: emit_hdb(0), 1: emit_hdb(1)}
            hdas = {0: emit_hda(0), 1: emit_hda(1)}
            hh_pair = [None, None]
            for n in range(NST):
                nb = n % 2
                pb = (n // 2) % 2
                hda = hdas[n]
                hda_last = hda
                hdb = hdbs[n // 2]
                cidx = t * NST + n
                if chs == 0:
                    hh = P.stage("V", [hda, hdb, l_h[nb]],
                                 lambda e, nb=nb, pb=pb: e.tensor_tensor_scan(
                                     z_h[:, nb * LC:(nb + 1) * LC], z_da[nb][:, :],
                                     z_db[pb][:, nb * LC:(nb + 1) * LC],
                                     0.0, OP.mult, OP.add))
                else:
                    hh = P.stage("V", [hda, hdb, l_h[nb], carry_h.get(cidx)],
                                 lambda e, nb=nb, pb=pb, cidx=cidx:
                                 e.tensor_tensor_scan(
                                     z_h[:, nb * LC:(nb + 1) * LC], z_da[nb][:, :],
                                     z_db[pb][:, nb * LC:(nb + 1) * LC],
                                     carry_sb[:, cidx:cidx + 1], OP.mult, OP.add))
                l_da[nb] = hh
                if nb == 1:
                    l_db[pb] = hh
                hh_pair[nb] = hh
                if n + 2 < NST:
                    hdas[n + 2] = emit_hda(n + 2)
                if nb == 1 and n // 2 + 2 < NST // 2:
                    hdbs[n // 2 + 2] = emit_hdb(n // 2 + 2)
                if nb == 1:
                    htm = P.stage("V", hh_pair + [hs_bcr[NST + n - 1],
                                                  hs_bcr[NST + n], l_tmp[pb]],
                                  lambda e, n=n, pb=pb:
                                  e.tensor_tensor(
                                      z_tmp[pb][:, :], z_h[:, :],
                                      z_bcr[:, (NST + n - 1) * LC:(NST + n + 1) * LC],
                                      OP.mult))
                    l_h[0] = htm
                    l_h[1] = htm
                    last_reader_bcr = htm
                    hs1 = P.stage("G", [htm], lambda e, pb=pb: e.tensor_tensor(
                        z_tmp[pb][:, 0:LC], z_tmp[pb][:, 0:LC],
                        z_tmp[pb][:, LC:2 * LC], OP.add))
                    if n == 1:
                        hy = P.stage("G", [hs1, l_y[ti]], lambda e, ti=ti, pb=pb:
                                     e.tensor_copy(z_y[ti][:, :], z_tmp[pb][:, 0:LC]))
                    else:
                        hy = P.stage("G", [hs1, l_y[ti]], lambda e, ti=ti, pb=pb:
                                     e.tensor_tensor(z_y[ti][:, :], z_y[ti][:, :],
                                                     z_tmp[pb][:, 0:LC], OP.add))
                    l_tmp[pb] = hy
                    l_y[ti] = hy
                    if chs < NCH - 1:
                        cbase = t * NST + n - 1
                        hcar = P.stage("V", hh_pair + [carry_h.get(cbase),
                                                       carry_h.get(cbase + 1)],
                                       lambda e, cbase=cbase: e.tensor_copy(
                                           carry_sb[:, cbase:cbase + 2],
                                           bass.AP(z_h, LC - 1,
                                                   [[2 * LC, 128], [LC, 2]])))
                        carry_h[cbase] = hcar
                        carry_h[cbase + 1] = hcar
                        l_h[0] = [l_h[0], hcar]
                        l_h[1] = [l_h[1], hcar]
            l_dl[ti] = [hda_last, hdu]
            l_du[ti] = hy
            hyd = P.stage("V", [l_y[ti], h_ldx, h_par], lambda e, t=t, ti=ti, bi=bi:
                          e.scalar_tensor_tensor(z_y[ti][:, :],
                                                 z_xin[bi][:, t * LC:(t + 1) * LC],
                                                 s_dvec[:, t:t + 1], z_y[ti][:, :],
                                                 OP.mult, OP.add))
            hcv = P.stage("V", [hyd, l_y16[ti]], lambda e, ti=ti: e.tensor_copy(
                z_y16[ti][:, :], z_y[ti][:, :]))
            l_y[ti] = hcv

            def sp_y(e, dma, t=t, ti=ti, chs=chs):
                hf = 0 if chs < NCH // 2 else 1
                col = chs * LC - hf * (L // 2)
                dma(out=yagin_h[hf].ap().rearrange("(t p) l -> t p l", p=128)
                    [t, :, col:col + LC], in_=z_y16[ti][:, :])

            hsp_y = P.dma_stage("Y", [hcv], sp_y, 1, chan=f"ysp{ti}")
            l_y16[ti] = hsp_y
            h_ysp.append(hsp_y)
            l_zxin[bi] = hyd
        l_bcr = last_reader_bcr
        l_zdt[bi] = hdtm

    cc = es.enter_context(nc.semaphore("sem_cc"))
    h_ag_a = None
    emit_front(0)
    for ch in range(1, NCH):
        emit_front(ch)
        emit_scan(ch - 1)
        if ch == NCH // 2:
            def f_ag_a(e):
                i1 = e.collective_compute(
                    "AllGather", OP.bypass, replica_groups=[list(range(8))],
                    ins=[yagin_h[0].ap()], outs=[yago_h[0].ap()])
                i1.then_inc(cc, 1)
                return e.memset(s_agn[0:1, 0:1], 0.0)

            h_ag_a = P.stage("G", list(h_ysp), f_ag_a)
    emit_scan(NCH - 1)

    bar_scan = P.barrier()
    sc.close()
    fr.close()
    if PH <= 3:
        return _finish_early()

    def f_ag(e):
        i1 = e.collective_compute("AllGather", OP.bypass,
                                  replica_groups=[list(range(8))],
                                  ins=[yagin_h[1].ap()],
                                  outs=[yago_h[1].ap()])
        i1.then_inc(cc, 1)
        i2 = e.collective_compute("AllGather", OP.bypass,
                                  replica_groups=[list(range(8))],
                                  ins=[eagin.ap()], outs=[eago.ap()])
        return i2

    h_ag = P.stage("G", h_ysp + [h_eout], f_ag)
    h_ag_y = (cc, 2)

    ro = ExitStack()
    rA = ro.enter_context(nc.sbuf_tensor("rA", [128, KD * D], BF16))
    rAT = ro.enter_context(nc.sbuf_tensor("rAT", [128, KD * D], BF16))
    rB = ro.enter_context(nc.sbuf_tensor("rB", [128, KD * D], BF16))
    rC = ro.enter_context(nc.sbuf_tensor("rC", [128, KD * D], BF16))

    def ld_e12(e, dma):
        dma(out=rA[:, :].rearrange("p (k d) -> p k d", k=KD),
            in_=eago.ap()[1 * D:2 * D, :].rearrange("(k p) d -> p k d", p=128))
        dma(out=rB[:, :].rearrange("p (k d) -> p k d", k=KD),
            in_=eago.ap()[2 * D:3 * D, :].rearrange("(k p) d -> p k d", p=128))

    h_lde = P.dma_stage("Y", [h_ag, h_ag_y] + bar_scan, ld_e12, 2, chan="rote")
    h_rAT = transpose_DD(rA, rAT, [h_lde])
    hT1r = mm_DD(rC, rAT, rB, h_rAT + [h_lde])

    h_lde0 = P.dma_stage("Y", h_rAT + [h_lde], lambda e, dma: dma(
        out=rA[:, :].rearrange("p (k d) -> p k d", k=KD),
        in_=eago.ap()[0:D, :].rearrange("(k p) d -> p k d", p=128)), 1, chan="rote")
    h_rAT0 = transpose_DD(rA, rAT, [h_lde0] + hT1r)
    hrot = mm_DD(rB, rAT, rC, h_rAT0 + hT1r)

    h_rot = P.dma_stage("Y", hrot, lambda e, dma: dma(
        out=rotdram.ap().rearrange("(k p) d -> p k d", p=128),
        in_=rB[:, :].rearrange("p (k d) -> p k d", k=KD)), 1, chan="rots")

    bar_rot = P.barrier()
    ro.close()
    if PH <= 4:
        return _finish_early()

    fb_ = ExitStack()
    NWB = 16
    g_w = [fb_.enter_context(nc.sbuf_tensor(f"g_w{i}", [128, 512], BF16))
           for i in range(NWB)]
    s2_sb = fb_.enter_context(nc.sbuf_tensor("s2_sb", [128, KD * BLK], F32))
    xn2 = fb_.enter_context(nc.sbuf_tensor("xn2", [128, KD * BLK], BF16))
    abel = fb_.enter_context(nc.sbuf_tensor("abel", [128, KD * BLK], BF16))
    wbc = fb_.enter_context(nc.sbuf_tensor("wbc", [128, BLK], F32))
    wdf = fb_.enter_context(nc.sbuf_tensor("wdf", [8, BLK], BF16))
    srow = fb_.enter_context(nc.sbuf_tensor("srow", [1, 3 * BLK], BF16))
    frow = fb_.enter_context(nc.sbuf_tensor("frow", [1, 3 * 512], F32))
    scr1 = fb_.enter_context(nc.sbuf_tensor("scr1", [128, 512], F32))
    scr2 = fb_.enter_context(nc.sbuf_tensor("scr2", [128, 512], F32))
    gl8 = fb_.enter_context(nc.sbuf_tensor("gl8", [128, 8], F32))
    gm8 = fb_.enter_context(nc.sbuf_tensor("gm8", [128, 8], F32))
    gs8 = fb_.enter_context(nc.sbuf_tensor("gs8", [128, 8], F32))
    gq8 = fb_.enter_context(nc.sbuf_tensor("gq8", [128, 8], F32))
    gcol = fb_.enter_context(nc.sbuf_tensor("gcol", [128, 4], F32))
    o_stg = fb_.enter_context(nc.sbuf_tensor("o_stg", [128, 2 * 128], F32))
    REGW = max(DFT * BLK + KD * BLK, (KD + 2 * TI) * BLK, KD * D + KD * BLK)
    REG = fb_.enter_context(nc.sbuf_tensor("REG", [128, REGW], BF16))
    g_xn = REG.ap()[:, 0:KD * BLK]
    g_z = REG.ap()[:, KD * BLK:(KD + TI) * BLK]
    g_y = REG.ap()[:, (KD + TI) * BLK:(KD + 2 * TI) * BLK]
    moe = REG.ap()[:, DFT * BLK:DFT * BLK + KD * BLK]
    h_sb = REG.ap()[:, 0:DFT * BLK]
    rot_sb = REG.ap()[:, 0:KD * D]
    nonab = REG.ap()[:, KD * D:KD * D + KD * BLK]

    regs = {}
    dbg2 = {}
    wring_b = WRing(P, g_w, "wb", bar_rot)
    h_blk_prev = bar_rot
    h_out_final = []
    ostg_war = [bar_rot, bar_rot]
    lw = [bar_rot, bar_rot]

    def b_row(q): return srow[0:1, q * SUB:q * SUB + SUB]
    def a_row(q): return srow[0:1, BLK + q * SUB:BLK + q * SUB + SUB]
    def g_row(q): return srow[0:1, 2 * BLK + q * SUB:2 * BLK + q * SUB + SUB]
    f1 = frow[0:1, 0:SUB]
    f2 = frow[0:1, 512:512 + SUB]
    f3 = frow[0:1, 1024:1024 + SUB]

    for j in range(2):
        deps0 = h_blk_prev if isinstance(h_blk_prev, list) else [h_blk_prev]
        hs_xn1 = []
        hs_res = []
        for tt in range(NTC):
            h9 = ln1_tokens(xown, j * BLK + tt * 128, 0, deps0)
            lnwar[0] = [h9]
            for k in range(KD):
                pi = (tt * KD + k) % 2

                def ftr(e, k=k, pi=pi):
                    return e.transpose(ps[pi].ap().bitcast(BF16)[:, 0:128],
                                       xn_tm[0][:, k * 128:(k + 1) * 128],
                                       s_identb[:, :])

                ht = P.stage("T", [h9, psr[pi]], ftr)

                def ftc(e, k=k, tt=tt, pi=pi):
                    return e.tensor_copy(
                        g_xn[:, k * BLK + tt * 128:k * BLK + (tt + 1) * 128],
                        ps[pi].ap().bitcast(BF16)[:, 0:128])

                hc = P.stage("V", [ht] + deps0, ftc)
                psr[pi] = hc
                lnwar[0].append(ht)
                hs_xn1.append(hc)
                pi2 = 2 + (tt * KD + k) % 2

                def ftr2(e, k=k, pi2=pi2):
                    return e.transpose(ps[pi2][:, 0:128],
                                       x_tm[0][:, k * 128:(k + 1) * 128],
                                       s_identf[:, :])

                ht2 = P.stage("T", [h9, psr[pi2]], ftr2)

                def ftc2(e, k=k, tt=tt, pi2=pi2):
                    return e.tensor_copy(
                        s2_sb[:, k * BLK + tt * 128:k * BLK + (tt + 1) * 128],
                        ps[pi2][:, 0:128])

                hc2 = P.stage("V", [ht2] + deps0, ftc2)
                psr[pi2] = hc2
                lnwar[0].append(ht2)
                hs_res.append(hc2)

        if PH <= 5:
            return _finish_early()
        hs_z = {}
        for g in range(XG):
            tiles = [wring_b.load(winz.ap()[k, g], 512) for k in range(KD)]
            hgrps = {}
            for q in range(NSUB):
                base = 4 * (q % 2)

                def fgrp(e, tiles=tiles, q=q, base=base):
                    ins = None
                    for k in range(KD):
                        slot, _ = tiles[k]
                        for m4 in range(4):
                            ins = e.matmul(
                                ps[base + m4][:, 0:SUB],
                                g_w[slot][:, m4 * 128:(m4 + 1) * 128],
                                g_xn[:, k * BLK + q * SUB:k * BLK + q * SUB + SUB],
                                start=(k == 0), stop=(k == KD - 1))
                    return ins

                hgrp = P.stage("T", [h for _, h in tiles] + hs_xn1 +
                               [psr[base + i] for i in range(4)], fgrp)
                hgrps[q] = hgrp
                for slot, _ in tiles:
                    wring_b.used(slot, hgrp)
            for m4 in range(4):
                t = g * 4 + m4
                for q in range(NSUB):
                    base = 4 * (q % 2)
                    pi = base + m4
                    hz = P.stage("S", [hgrps[q]] + deps0,
                                 lambda e, t=t, q=q, pi=pi: e.activation(
                                     g_z[:, t * BLK + q * SUB:t * BLK + q * SUB + SUB],
                                     ps[pi][:, 0:SUB], AF.Silu))
                    psr[pi] = hz
                    hs_z[(t, q)] = hz

        if PH <= 6:
            return _finish_early()
        hs_yg = {}
        h_ldys = []
        for half in range(2):
            def ldy(e, dma, half=half, j=j):
                if "yreg" not in regs:
                    regs["yreg"] = e.alloc_register("yoff_reg")
                rr = regs["yreg"]
                e.reg_load(rr, s_yoff[0:1, j * 2 + half:j * 2 + half + 1])
                src = bass.AP(yago_h[j], rr,
                              [[L // 2, 128], [128 * (L // 2), TS], [1, BLK]])
                dma(out=g_y[:, half * TS * BLK:(half + 1) * TS * BLK]
                    .rearrange("p (t l) -> p t l", t=TS), in_=src)

            ag_dep = [h_ag_a, (cc, 1)] if j == 0 else [h_ag, (cc, 2)]
            h_ldys.append(P.dma_stage("G", ag_dep + [h_par] + deps0, ldy, 1,
                                      chan="ldy"))
        hly = h_ldys[-1]
        for t in range(TI):
            for q in range(NSUB):
                hyg = P.stage("V", [hly, hs_z[(t, q)]], lambda e, t=t, q=q:
                              e.tensor_tensor(
                                  g_y[:, t * BLK + q * SUB:t * BLK + q * SUB + SUB],
                                  g_y[:, t * BLK + q * SUB:t * BLK + q * SUB + SUB],
                                  g_z[:, t * BLK + q * SUB:t * BLK + q * SUB + SUB],
                                  OP.mult))
                hs_yg[(t, q)] = hyg

        if PH <= 7:
            return _finish_early()
        hs_s2 = {}
        for g in range(KD // MGA):
            tiles = [wring_b.load(wout.ap()[k, g], MGA * 128) for k in range(TI)]
            hgrps = {}
            for q in range(NSUB):
                base = 4 * (q % 2)

                def fgrp(e, tiles=tiles, q=q, base=base):
                    ins = None
                    for k in range(TI):
                        slot, _ = tiles[k]
                        for m4 in range(MGA):
                            ins = e.matmul(
                                ps[base + m4][:, 0:SUB],
                                g_w[slot][:, m4 * 128:(m4 + 1) * 128],
                                g_y[:, k * BLK + q * SUB:k * BLK + q * SUB + SUB],
                                start=(k == 0), stop=(k == TI - 1))
                    return ins

                hgrp = P.stage("T", [h for _, h in tiles] +
                               [hs_yg[(k, q)] for k in range(TI)] +
                               [psr[base + i] for i in range(4)], fgrp)
                hgrps[q] = hgrp
            for slot, _ in tiles:
                wring_b.used(slot, [hgrps[q] for q in range(NSUB)])
            for m4 in range(MGA):
                for q in range(NSUB):
                    base = 4 * (q % 2)
                    pi = base + m4
                    m = g * MGA + m4
                    hadd = P.stage("V", [hgrps[q]] + hs_res,
                                   lambda e, m=m, q=q, pi=pi: e.tensor_tensor(
                                       s2_sb[:, m * BLK + q * SUB:m * BLK + q * SUB + SUB],
                                       s2_sb[:, m * BLK + q * SUB:m * BLK + q * SUB + SUB],
                                       ps[pi][:, 0:SUB], OP.add))
                    psr[pi] = hadd
                    hs_s2[(m, q)] = hadd
        hs_s2_all = list(hs_s2.values())

        def sp_s2(e, dma, j=j):
            for k in range(KD):
                dma(out=s2dram.ap().rearrange("(j k p) l -> j k p l", k=KD, p=128)[j, k],
                    in_=s2_sb[:, k * BLK:(k + 1) * BLK])

        h_s2sp = P.dma_stage("Y", hs_s2_all, sp_s2, KD, chan="s2s")

        if PH <= 8:
            return _finish_early()
        hs_xn2 = {}
        h_ab = []
        l_frow = deps0
        l_srow = deps0
        for q in range(NSUB):
            def fsum(e, q=q):
                ins = None
                for k in range(KD):
                    ins = e.matmul(ps[4][0:1, 0:SUB], s_onescol[:, 0:1],
                                   s2_sb[:, k * BLK + q * SUB:k * BLK + q * SUB + SUB],
                                   start=(k == 0), stop=(k == KD - 1))
                return ins

            hsum = P.stage("T", hs_s2_all + [psr[4]], fsum)
            hsq_mm = None
            for k in range(KD):
                hq = P.stage("S", [hs_s2[(k, q)], ln_state["last"]],
                             lambda e, k=k, q=q: e.activation(
                                 s_sq[:, 0:SUB],
                                 s2_sb[:, k * BLK + q * SUB:k * BLK + q * SUB + SUB],
                                 AF.Square))
                ln_state["last"] = hq

                def fq(e, k=k):
                    return e.matmul(ps[5][0:1, 0:SUB], s_onescol[:, 0:1],
                                    s_sq[:, 0:SUB],
                                    start=(k == 0), stop=(k == KD - 1))

                hsq_mm = P.stage("T", [hq] + ([psr[5]] if k == 0 else []), fq)
                ln_state["last"] = hsq_mm
            h1 = P.stage("V", [hsum, l_frow], lambda e: e.tensor_scalar(
                f1, ps[4][0:1, 0:SUB], 1.0 / D, None, OP.mult))
            psr[4] = h1
            h2 = P.stage("V", [hsq_mm, h1], lambda e: e.tensor_scalar(
                f2, ps[5][0:1, 0:SUB], 1.0 / D, None, OP.mult))
            psr[5] = h2
            h3 = P.stage("V", [h2], lambda e: e.tensor_tensor(
                f3, f1, f1, OP.mult))
            h4 = P.stage("V", [h3], lambda e: e.tensor_tensor(
                f2, f2, f3, OP.subtract))
            h5 = P.stage("S", [h4, h_par], lambda e: e.activation(
                f2, f2, AF.Ln, bias=c.EPS_LN))
            h6s = P.stage("S", [h5], lambda e: e.activation(
                f2, f2, AF.Exp, scale=-0.5))
            h7 = P.stage("V", [h6s], lambda e: e.tensor_tensor(
                f3, f1, f2, OP.mult))
            h8v = P.stage("V", [h7], lambda e: e.tensor_scalar(
                f3, f3, -1.0, None, OP.mult))
            h6 = P.stage("V", [h6s, l_srow], lambda e, q=q: e.tensor_copy(
                a_row(q), f2))
            h8 = P.stage("V", [h8v], lambda e, q=q: e.tensor_copy(
                b_row(q), f3))
            l_frow = [h6, h8]
            l_srow = [h6, h8]
            h_ab.append((h6, h8))

            hba = P.stage("T", [h6, psr[4]], lambda e, q=q: e.matmul(
                ps[4][:, 0:SUB], s_onesb[0:1, :], a_row(q), start=True, stop=True))
            hbb = P.stage("T", [h8, psr[5]], lambda e, q=q: e.matmul(
                ps[5][:, 0:SUB], s_onesb[0:1, :], b_row(q), start=True, stop=True))
            l_scr1 = None
            for k in range(KD):
                ha1 = P.stage("V", [hba, hs_s2[(k, q)], l_scr1],
                              lambda e, k=k, q=q: e.tensor_tensor(
                                  scr1[:, 0:SUB],
                                  s2_sb[:, k * BLK + q * SUB:k * BLK + q * SUB + SUB],
                                  ps[4][:, 0:SUB], OP.mult))
                ha2 = P.stage("V", [ha1, hbb], lambda e: e.tensor_tensor(
                    scr1[:, 0:SUB], scr1[:, 0:SUB], ps[5][:, 0:SUB], OP.add))
                ha3 = P.stage("V", [ha2], lambda e, k=k, q=q: e.tensor_scalar(
                    xn2[:, k * BLK + q * SUB:k * BLK + q * SUB + SUB],
                    scr1[:, 0:SUB], s_lng[:, k:k + 1], s_lnb[:, k:k + 1],
                    OP.mult, OP.add))
                l_scr1 = ha3
                psr[4] = ha3
                psr[5] = ha3
                hs_xn2[(k, q)] = ha3
        hs_xn2_all = list(hs_xn2.values())

        if PH <= 9:
            return _finish_early()
        if os.environ.get("KDEBUG2") and j == 0:
            dbg2["xn2"] = nc.dram_tensor("dbg2_xn2", [128, KD * BLK], BF16,
                                         kind="ExternalOutput")
            P.dma_stage("Y", hs_xn2_all, lambda e, dma: dma(
                out=dbg2["xn2"].ap(), in_=xn2[:, :]), 1, chan="dbg")
        hs_wd = []
        l_g8 = None
        for tc in range(NTC):
            def flog(e, tc=tc):
                ins = None
                for k in range(KD):
                    ins = e.matmul(ps[6][0:128, 0:E],
                                   xn2[:, k * BLK + tc * 128:k * BLK + (tc + 1) * 128],
                                   s_gatew[:, k * E:(k + 1) * E],
                                   start=(k == 0), stop=(k == KD - 1))
                return ins

            hlg = P.stage("T", hs_xn2_all + [psr[6], h_parc], flog)
            hl1 = P.stage("V", [hlg, h_gbbc, l_g8], lambda e: e.tensor_tensor(
                gl8[:, :], ps[6][0:128, 0:E], s_gatebbc[:, :], OP.add))
            psr[6] = hl1
            hm8 = P.stage("V", [hl1], lambda e: e.max(out=gm8[:, :], in_=gl8[:, :]))
            hms = P.stage("V", [hm8], lambda e: e.memset(gs8[:, :], -1e38))
            hcp = P.stage("V", [hms], lambda e: e.tensor_copy(gs8[:, 0:2], gm8[:, 0:2]))
            hmr = P.stage("V", [hcp], lambda e: e.match_replace(
                out=gq8[:, :], in_to_replace=gs8[:, :], in_values=gl8[:, :],
                imm_value=-1e30))
            hmask = P.stage("V", [hmr], lambda e: e.tensor_scalar(
                gq8[:, :], gq8[:, :], -1e30, None, OP.is_equal))
            hs12 = P.stage("V", [hm8], lambda e: e.tensor_tensor(
                gcol[:, 0:1], gm8[:, 0:1], gm8[:, 1:2], OP.add))
            harg = P.stage("V", [hs12, hl1, hmr], lambda e: e.tensor_scalar(
                gl8[:, :], gl8[:, :], 2.0, gcol[:, 0:1], OP.mult, OP.subtract))
            hsg = P.stage("S", [harg], lambda e: e.activation(
                gl8[:, :], gl8[:, :], AF.Sigmoid))
            hwd = P.stage("V", [hsg, hmask], lambda e: e.tensor_tensor(
                gq8[:, :], gq8[:, :], gl8[:, :], OP.mult))
            htr = P.stage("T", [hwd, psr[7]], lambda e: e.transpose(
                ps[7][0:8, 0:128], gq8[:, :], s_identf[:, :]))
            hwf = P.stage("V", [htr], lambda e, tc=tc: e.tensor_copy(
                wdf[:, tc * 128:(tc + 1) * 128], ps[7][0:8, 0:128]))
            psr[7] = hwf
            l_g8 = hwd
            hs_wd.append(hwf)

        if PH <= 10:
            return _finish_early()
        if os.environ.get("KDEBUG2") and j == 0:
            dbg2["wdf"] = nc.dram_tensor("dbg2_wdf", [8, BLK], BF16,
                                         kind="ExternalOutput")
            P.dma_stage("Y", hs_wd, lambda e, dma: dma(
                out=dbg2["wdf"].ap(), in_=wdf[:, :]), 1, chan="dbg")
        h_moe = {}
        for m in range(KD):
            for q in range(NSUB):
                pi = 6 + (m * NSUB + q) % 2

                def fb2(e, m=m, q=q, pi=pi):
                    return e.matmul(ps[pi][:, 0:SUB],
                                    s_b2[0:8, m * 128:(m + 1) * 128],
                                    wdf[0:8, q * SUB:q * SUB + SUB],
                                    start=True, stop=True)

                hb2 = P.stage("T", hs_wd + [psr[pi], h_par], fb2)
                hmi = P.stage("V", [hb2], lambda e, m=m, q=q, pi=pi: e.tensor_copy(
                    moe[:, m * BLK + q * SUB:m * BLK + q * SUB + SUB],
                    ps[pi][:, 0:SUB]))
                psr[pi] = hmi
                h_moe[(m, q)] = hmi

        if PH <= 11:
            return _finish_early()
        hs_abel = {}
        for g in range(KD // MGA):
            tiles = [wring_b.load(abelw.ap()[k, g], MGA * 128) for k in range(KD)]
            hgrps = {}
            for q in range(NSUB):
                base = 4 * (q % 2)

                def fgrp(e, tiles=tiles, q=q, base=base):
                    ins = None
                    for k in range(KD):
                        slot, _ = tiles[k]
                        for m4 in range(MGA):
                            ins = e.matmul(
                                ps[base + m4][:, 0:SUB],
                                g_w[slot][:, m4 * 128:(m4 + 1) * 128],
                                xn2[:, k * BLK + q * SUB:k * BLK + q * SUB + SUB],
                                start=(k == 0), stop=(k == KD - 1))
                    return ins

                hgrp = P.stage("T", [h for _, h in tiles] + hs_xn2_all +
                               [psr[base + i] for i in range(4)], fgrp)
                hgrps[q] = hgrp
            for slot, _ in tiles:
                wring_b.used(slot, [hgrps[q] for q in range(NSUB)])
            for m4 in range(MGA):
                for q in range(NSUB):
                    base = 4 * (q % 2)
                    pi = base + m4
                    m = g * MGA + m4
                    hab = P.stage("S", [hgrps[q]], lambda e, m=m, q=q, pi=pi:
                                  e.activation(
                                      abel[:, m * BLK + q * SUB:m * BLK + q * SUB + SUB],
                                      ps[pi][:, 0:SUB], AF.Gelu,
                                      bias=s_abelb[:, m:m + 1]))
                    psr[pi] = hab
                    hs_abel[(m, q)] = hab

        if PH <= 12:
            return _finish_early()
        if os.environ.get("KDEBUG2") and j == 0:
            dbg2["abel"] = nc.dram_tensor("dbg2_abel", [128, KD * BLK], BF16,
                                          kind="ExternalOutput")
            P.dma_stage("Y", list(hs_abel.values()), lambda e, dma: dma(
                out=dbg2["abel"].ap(), in_=abel[:, :]), 1, chan="dbg")
        hs_h_last = {}
        l_wbc = None
        l_scr2 = None
        for ex in range(E):
            hs_wbc = []
            for q in range(NSUB):
                pi = 6 + q % 2

                def fwb(e, ex=ex, q=q, pi=pi):
                    return e.matmul(ps[pi][:, 0:SUB],
                                    s_sel8[:, ex * 128:(ex + 1) * 128],
                                    wdf[0:8, q * SUB:q * SUB + SUB],
                                    start=True, stop=True)

                hw1 = P.stage("T", hs_wd + [psr[pi], h_par], fwb)
                hw2 = P.stage("V", [hw1, l_wbc], lambda e, q=q, pi=pi: e.tensor_copy(
                    wbc[:, q * SUB:q * SUB + SUB], ps[pi][:, 0:SUB]))
                psr[pi] = hw2
                hs_wbc.append(hw2)

            hs_h = {}
            for mg in range(MG1):
                base = 4 * ((mg + 1) % 2)
                tiles = [wring_b.load(w1t.ap()[ex, k, mg], 256) for k in range(KD)]

                def fgrp(e, tiles=tiles, base=base):
                    ins = None
                    for k in range(KD):
                        slot, _ = tiles[k]
                        for d2 in range(2):
                            for q in range(NSUB):
                                ins = e.matmul(
                                    ps[base + (d2 * NSUB + q) % 4][:, 0:SUB],
                                    g_w[slot][:, d2 * 128:(d2 + 1) * 128],
                                    xn2[:, k * BLK + q * SUB:k * BLK + q * SUB + SUB],
                                    start=(k == 0), stop=(k == KD - 1))
                    return ins

                hgrp = P.stage("T", [h for _, h in tiles] + hs_xn2_all +
                               [psr[base + i] for i in range(4)], fgrp)
                for slot, _ in tiles:
                    wring_b.used(slot, hgrp)
                for d2 in range(2):
                    for q in range(NSUB):
                        pi = base + (d2 * NSUB + q) % 4
                        df = mg * 2 + d2
                        dep_h = hs_h_last.get((df, q))
                        hh = P.stage("S", [hgrp, dep_h],
                                     lambda e, df=df, q=q, pi=pi, ex=ex:
                                     e.activation(
                                         h_sb[:, df * BLK + q * SUB:df * BLK + q * SUB + SUB],
                                         ps[pi][:, 0:SUB], AF.Gelu,
                                         bias=s_b1[:, ex * DFT + df:ex * DFT + df + 1]))
                        psr[pi] = hh
                        hs_h[(df, q)] = hh

            DFG = min(8, DFT)
            for dg in range(KD // DG2):
                hlast = {q: None for q in range(NSUB)}
                for dfg in range(DFT // DFG):
                    tiles = [wring_b.load(w2t.ap()[ex, dg, dfg * DFG + i],
                                          DG2 * 128) for i in range(DFG)]
                    first = dfg == 0
                    for q in range(NSUB):
                        base = 4 * (q % 2)
                        dep_h = [hs_h[(dfg * DFG + i, q)] for i in range(DFG)]

                        def fgrp(e, tiles=tiles, dfg=dfg, q=q, base=base):
                            ins = None
                            for i in range(DFG):
                                df = dfg * DFG + i
                                slot, _ = tiles[i]
                                for d4 in range(DG2):
                                    ins = e.matmul(
                                        ps[base + d4][:, 0:SUB],
                                        g_w[slot][:, d4 * 128:(d4 + 1) * 128],
                                        h_sb[:, df * BLK + q * SUB:df * BLK + q * SUB + SUB],
                                        start=(df == 0), stop=(df == DFT - 1))
                            return ins

                        deps_mm = [h for _, h in tiles] + dep_h +                             ([psr[base + i] for i in range(4)] if first
                             else [hlast[q]])
                        hlast[q] = P.stage("T", deps_mm, fgrp)
                    for slot, _ in tiles:
                        wring_b.used(slot, [hlast[q] for q in range(NSUB)])
                for d4 in range(DG2):
                    for q in range(NSUB):
                        base = 4 * (q % 2)
                        pi = base + d4
                        m = dg * DG2 + d4
                        he1 = P.stage("V", [hlast[q], l_scr2] + hs_wbc,
                                      lambda e, q=q, pi=pi: e.tensor_tensor(
                                          scr2[:, 0:SUB], ps[pi][:, 0:SUB],
                                          wbc[:, q * SUB:q * SUB + SUB],
                                          OP.mult))
                        psr[pi] = he1
                        he2 = P.stage("V", [he1, h_moe[(m, q)]],
                                      lambda e, m=m, q=q: e.tensor_tensor(
                                          moe[:, m * BLK + q * SUB:m * BLK + q * SUB + SUB],
                                          moe[:, m * BLK + q * SUB:m * BLK + q * SUB + SUB],
                                          scr2[:, 0:SUB], OP.add))
                        h_moe[(m, q)] = he2
                        l_scr2 = he2
            hs_h_last = hs_h
            l_wbc = he1
        hs_moe_all = list(h_moe.values())
        hs_h_reads = list(hs_h_last.values())

        if PH <= 13:
            return _finish_early()
        if os.environ.get("KDEBUG2") and j == 0:
            dbg2["moe"] = nc.dram_tensor("dbg2_moe", [128, KD * BLK], BF16,
                                         kind="ExternalOutput")
            P.dma_stage("Y", hs_moe_all, lambda e, dma: dma(
                out=dbg2["moe"].ap(), in_=moe), 1, chan="dbg")
        h_ldrot = P.dma_stage("G", hs_moe_all + [h_rot] + hs_h_reads +
                              [lw[0], lw[1]], ld_rot_fn(rot_sb, rotdram, KD), 1, chan="ldrot")
        hs_nonab = {}
        for m in range(KD):
            for q in range(NSUB):
                pi = 4 + (m * NSUB + q) % 4

                def fmm(e, m=m, q=q, pi=pi):
                    ins = None
                    for k in range(KD):
                        ins = e.matmul(ps[pi][:, 0:SUB],
                                       rot_sb[:, k * D + m * 128:k * D + (m + 1) * 128],
                                       moe[:, k * BLK + q * SUB:k * BLK + q * SUB + SUB],
                                       start=(k == 0), stop=(k == KD - 1))
                    return ins

                hmm = P.stage("T", [h_ldrot, psr[pi]] + hs_moe_all, fmm)
                hnb = P.stage("V", [hmm], lambda e, m=m, q=q, pi=pi: e.tensor_copy(
                    nonab[:, m * BLK + q * SUB:m * BLK + q * SUB + SUB],
                    ps[pi][:, 0:SUB]))
                psr[pi] = hnb
                hs_nonab[(m, q)] = hnb

        if PH <= 14:
            return _finish_early()
        if os.environ.get("KDEBUG2") and j == 0:
            dbg2["nonab"] = nc.dram_tensor("dbg2_nonab", [128, KD * BLK], BF16,
                                           kind="ExternalOutput")
            P.dma_stage("Y", list(hs_nonab.values()), lambda e, dma: dma(
                out=dbg2["nonab"].ap(), in_=nonab), 1, chan="dbg")
        hs_g = []
        for q in range(NSUB):
            def fmx(e, q=q):
                ins = None
                for k in range(KD):
                    ins = e.matmul(ps[6][0:1, 0:SUB], s_mixw[:, k:k + 1],
                                   abel[:, k * BLK + q * SUB:k * BLK + q * SUB + SUB],
                                   start=(k == 0), stop=False)
                for k in range(KD):
                    ins = e.matmul(ps[6][0:1, 0:SUB], s_mixw[:, KD + k:KD + k + 1],
                                   nonab[:, k * BLK + q * SUB:k * BLK + q * SUB + SUB],
                                   start=False, stop=(k == KD - 1))
                return ins

            dep_ab = [hs_abel[(k, q)] for k in range(KD)]
            dep_nb = [hs_nonab[(k, q)] for k in range(KD)]
            hmx = P.stage("T", dep_ab + dep_nb + [psr[6], h_parc], fmx)
            hg1 = P.stage("V", [hmx, l_srow], lambda e, q=q: e.tensor_scalar(
                g_row(q), ps[6][0:1, 0:SUB], s_mixb[0:1, 0:1], None, OP.add))
            psr[6] = hg1
            hg2 = P.stage("S", [hg1], lambda e, q=q: e.activation(
                g_row(q), g_row(q), AF.Sigmoid))
            l_srow = [hg2]
            hs_g.append(hg2)

        if PH <= 15:
            return _finish_early()
        h_lds2 = P.dma_stage("Y", [h_s2sp] + hs_xn2_all + [ln_state["last"]],
                             lambda e, dma, j=j: [
                                 dma(out=s2_sb[:, k * BLK:(k + 1) * BLK],
                                     in_=s2dram.ap().rearrange(
                                         "(j k p) l -> j k p l", k=KD, p=128)[j, k])
                                 for k in range(KD)] and None or None, KD,
                             chan="s2s")

        out_tiles = {}
        h_out_dmas = []
        l_scr1f = None
        l_scr2f = l_scr2
        for q in range(NSUB):
            (h6a, h8b) = h_ab[q]
            hba = P.stage("T", [h6a, psr[4]], lambda e, q=q: e.matmul(
                ps[4][:, 0:SUB], s_onesb[0:1, :], a_row(q), start=True, stop=True))
            hbb = P.stage("T", [h8b, psr[5]], lambda e, q=q: e.matmul(
                ps[5][:, 0:SUB], s_onesb[0:1, :], b_row(q), start=True, stop=True))
            hgb = P.stage("T", [hs_g[q], psr[6]], lambda e, q=q: e.matmul(
                ps[6][:, 0:SUB], s_onesb[0:1, :], g_row(q), start=True, stop=True))
            for k in range(KD):
                h1 = P.stage("V", [hs_nonab[(k, q)], hs_abel[(k, q)], l_scr1f],
                             lambda e, k=k, q=q: e.tensor_tensor(
                                 scr1[:, 0:SUB],
                                 nonab[:, k * BLK + q * SUB:k * BLK + q * SUB + SUB],
                                 abel[:, k * BLK + q * SUB:k * BLK + q * SUB + SUB],
                                 OP.subtract))
                h2 = P.stage("V", [h1, hgb], lambda e: e.tensor_tensor(
                    scr1[:, 0:SUB], scr1[:, 0:SUB], ps[6][:, 0:SUB], OP.mult))
                psr[6] = h2
                h3 = P.stage("V", [h_lds2, hba, l_scr2f], lambda e, k=k, q=q:
                             e.tensor_tensor(
                                 scr2[:, 0:SUB],
                                 s2_sb[:, k * BLK + q * SUB:k * BLK + q * SUB + SUB],
                                 ps[4][:, 0:SUB], OP.mult))
                psr[4] = h3
                h4 = P.stage("V", [h3, hbb], lambda e: e.tensor_tensor(
                    scr2[:, 0:SUB], scr2[:, 0:SUB], ps[5][:, 0:SUB], OP.add))
                psr[5] = h4
                h5 = P.stage("V", [h4], lambda e, k=k, q=q: e.tensor_tensor(
                    scr2[:, 0:SUB], scr2[:, 0:SUB],
                    s2_sb[:, k * BLK + q * SUB:k * BLK + q * SUB + SUB], OP.add))
                h6 = P.stage("V", [h5], lambda e, k=k, q=q: e.tensor_tensor(
                    scr2[:, 0:SUB], scr2[:, 0:SUB],
                    abel[:, k * BLK + q * SUB:k * BLK + q * SUB + SUB], OP.add))
                h7 = P.stage("V", [h6, h2], lambda e: e.tensor_tensor(
                    scr2[:, 0:SUB], scr2[:, 0:SUB], scr1[:, 0:SUB], OP.add))
                l_scr1f = h7
                l_scr2f = h7
                for pz in range(SUB // 128):
                    htr = P.stage("T", [h7, psr[7]], lambda e, pz=pz: e.transpose(
                        ps[7][:, 0:128], scr2[:, pz * 128:(pz + 1) * 128],
                        s_identf[:, :]))
                    tglob = q * (SUB // 128) + pz
                    og = (k * (SUB // 128) + pz) % 2

                    def fst(e, og=og):
                        return e.tensor_copy(o_stg[:, og * 128:(og + 1) * 128],
                                             ps[7][:, 0:128])

                    hst = P.stage("V", [htr, ostg_war[og]], fst)
                    psr[7] = hst

                    def st_out1(e, dma, j=j, k=k, tglob=tglob, og=og):
                        dma(out=yout.ap()[j * BLK + tglob * 128:
                                          j * BLK + (tglob + 1) * 128,
                                          k * 128:(k + 1) * 128],
                            in_=o_stg[:, og * 128:(og + 1) * 128])

                    h_o1 = P.dma_stage("Y", [hst], st_out1, 1, chan=f"out{og}")
                    ostg_war[og] = h_o1
                    h_out_dmas.append(h_o1)
        h_out_final.extend(h_out_dmas)
        h_blk_prev = P.barrier()

    if os.environ.get("KDEBUG"):
        dbg_specs = [("xin_sc", xin_sc), ("dt_dram", dt_dram), ("bc_dram", bc_dram),
                     ("yagin", yagin_h[0]), ("eagin", eagin), ("s2dram", s2dram),
                     ("rotdram", rotdram), ("yago8", yago_h[0])]
        dbg_outs = {}
        for nm, t_ in dbg_specs:
            dbg_outs[nm] = nc.dram_tensor(
                "dbg_" + nm, list(t_.shape), t_.dtype, kind="ExternalOutput")

        def f_dbg(e, dma):
            for nm, t_ in dbg_specs:
                dma(out=dbg_outs[nm].ap(), in_=t_.ap())

        h_dbg = P.dma_stage("G", P.barrier(), f_dbg, len(dbg_specs), chan="dbg")
        h_out_final = h_out_final + [h_dbg]

    P.stage("G", h_out_final + [h_par], lambda e: e.memset(s_fin[0:1, 0:1], 0.0))

    fb_.close()

    with nc.Block() as block:
        P.emit(block)
    es.close()
    return nc


def ld_rot_fn(rot_sb, rotdram, KD):
    def f(e, dma):
        dma(out=rot_sb.rearrange("p (k d) -> p k d", k=KD),
            in_=rotdram.ap().rearrange("(k p) d -> p k d", p=128))
    return f



def stage_core_inputs(cfg: Cfg, inp: dict, core: int):
    c = cfg
    KD, TI, TS, NST, DTR = c.KD, c.TI, c.TS, c.NST, c.DTR
    D, DI, L, E, DF = c.D, c.DI, c.L, c.E, c.DF
    MGA, DG2, MG1, BLK = c.MGA, c.DG2, c.MG1, c.BLK
    XG = c.XG
    b, q = core // 2, core % 2
    f32 = np.float32

    half = DI // 2
    if q == 0:
        sig = np.arange(DI)
    else:
        sig = np.concatenate([np.arange(half, DI), np.arange(0, half)])

    x = np.asarray(inp["L0_attn_x"], f32)
    xfull = np.ascontiguousarray(x[b])
    blkpos = [q * BLK, 2 * BLK + q * BLK]
    xown = np.ascontiguousarray(
        np.concatenate([x[b, blkpos[0]:blkpos[0] + BLK],
                        x[b, blkpos[1]:blkpos[1] + BLK]], axis=0))

    ipw = np.asarray(inp["L0_attn_in_proj_w"], f32)
    wx = ipw[:DI][sig]
    wz = ipw[DI:][sig]
    winx = np.ascontiguousarray(wx.T.reshape(KD, 128, XG, 512).transpose(0, 2, 1, 3))
    winz = np.ascontiguousarray(wz.T.reshape(KD, 128, XG, 512).transpose(0, 2, 1, 3))

    opw = np.asarray(inp["L0_attn_out_proj_w"], f32)
    wout = np.ascontiguousarray(
        opw[:, sig].T.reshape(TI, 128, KD // MGA, MGA * 128).transpose(0, 2, 1, 3))
    XP = DTR + 2 * NST
    xprojt = np.ascontiguousarray(
        np.asarray(inp["L0_attn_x_proj_w"], f32)[:, sig].T.reshape(TI, 128, XP))
    dtprojt = np.ascontiguousarray(
        np.asarray(inp["L0_attn_dt_proj_w"], f32)[sig][:TS * 128].T)
    cw = np.asarray(inp["L0_attn_conv_w"], f32)[sig, 0, :]
    convw = np.ascontiguousarray(
        cw.reshape(TI, 128, c.DCONV).transpose(1, 0, 2).reshape(128, TI * c.DCONV))
    convb = np.ascontiguousarray(
        np.asarray(inp["L0_attn_conv_b"], f32)[sig].reshape(TI, 128).T)
    dvec = np.ascontiguousarray(
        np.asarray(inp["L0_attn_D"], f32)[sig][:TS * 128].reshape(TS, 128).T)
    al = np.asarray(inp["L0_attn_A_log"], f32)[sig][:TS * 128]
    alog = np.ascontiguousarray(
        al.reshape(TS, 128, NST).transpose(1, 0, 2).reshape(128, TS * NST))
    dtb = np.ascontiguousarray(
        np.asarray(inp["L0_attn_dt_proj_b"], f32)[sig][:TS * 128].reshape(TS, 128).T)

    lng = np.ascontiguousarray(np.asarray(inp["L0_ffn_ln_g"], f32).reshape(KD, 128).T)
    lnb = np.ascontiguousarray(np.asarray(inp["L0_ffn_ln_b"], f32).reshape(KD, 128).T)
    abelb = np.ascontiguousarray(
        np.asarray(inp["L0_ffn_abel_b"], f32).reshape(KD, 128).T)
    abelw = np.ascontiguousarray(
        np.asarray(inp["L0_ffn_abel_w"], f32)
        .reshape(KD, 128, KD // MGA, MGA * 128).transpose(0, 2, 1, 3))
    gatew = np.ascontiguousarray(
        np.asarray(inp["L0_ffn_gate_w_moe"], f32).reshape(KD, 128, E))
    gateb = np.asarray(inp["L0_ffn_gate_b_moe"], f32).reshape(1, E)
    mixw = np.ascontiguousarray(
        np.asarray(inp["L0_ffn_mix_w"], f32)[:, 0].reshape(2 * KD, 128).T)
    mixb = np.asarray(inp["L0_ffn_mix_b"], f32).reshape(1, 1)
    w1t = np.ascontiguousarray(
        np.asarray(inp["L0_ffn_w1"], f32)
        .reshape(E, KD, 128, MG1, 256).transpose(0, 1, 3, 2, 4))
    w2t = np.ascontiguousarray(
        np.asarray(inp["L0_ffn_w2"], f32)
        .reshape(E, c.DFT, 128, KD // DG2, DG2 * 128).transpose(0, 3, 1, 2, 4))
    b1v = np.ascontiguousarray(
        np.asarray(inp["L0_ffn_b1"], f32)
        .reshape(E, c.DFT, 128).transpose(2, 0, 1).reshape(128, E * c.DFT))
    b2v = np.ascontiguousarray(np.asarray(inp["L0_ffn_b2"], f32))
    gens = np.asarray(inp["L0_ffn_gens"], f32)
    gmat = np.ascontiguousarray(gens[core % 3])
    gmatT = np.ascontiguousarray(gmat.T)

    identf = np.eye(128, dtype=f32)
    identb = np.eye(128, dtype=BF)
    onesf = np.ones((1, 128), dtype=f32)
    sel32 = np.zeros((32, 32 * 128), dtype=BF)
    for n in range(32):
        sel32[n, n * 128:(n + 1) * 128] = 1
    sel8 = np.zeros((8, 8 * 128), dtype=f32)
    for n in range(8):
        sel8[n, n * 128:(n + 1) * 128] = 1

    yoff = np.zeros((1, 4), dtype=np.int32)
    part = core ^ 1
    for j in range(2):
        for half, r in enumerate([core, part]):
            yoff[0, j * 2 + half] = (r * TS * 128) * (L // 2) + \
                blkpos[j] - j * (L // 2)

    return {
        "xfull": xfull, "xown": xown, "winx": winx, "winz": winz, "wout": wout,
        "xprojt": xprojt, "dtprojt": dtprojt, "convw": convw, "convb": convb,
        "dvec": dvec, "alog": alog, "dtb": dtb, "lng": lng, "lnb": lnb,
        "abelb": abelb, "abelw": abelw, "gatew": gatew, "gateb": gateb,
        "mixw": mixw, "mixb": mixb, "w1t": w1t, "w2t": w2t, "b1v": b1v,
        "b2v": b2v, "gmat": gmat, "gmatT": gmatT, "identf": identf,
        "identb": identb, "onesf": onesf, "onescol": np.ones((128, 1), dtype=f32),
        "onesb": np.ones((1, 128), dtype=BF),
        "cvals": np.tile(np.array([[c.EPS_LN, -0.5, -1.0, 1.0]], dtype=f32), (128, 1)),
        "sel32": sel32, "sel8": sel8,
        "yoff": yoff,
    }


def assemble_output(cfg: Cfg, results):
    c = cfg
    out = np.zeros((4, c.L, c.D), dtype=np.float32)
    for core in range(8):
        b, q = core // 2, core % 2
        yo = np.asarray(results[core]["yout"])
        blkpos = [q * c.BLK, 2 * c.BLK + q * c.BLK]
        out[b, blkpos[0]:blkpos[0] + c.BLK] = yo[:c.BLK]
        out[b, blkpos[1]:blkpos[1] + c.BLK] = yo[c.BLK:]
    return out


def kernel(**inputs):
    from concourse.bass_utils import run_bass_kernel_spmd
    cfg = FULL
    nc = build_program(cfg)
    in_maps = [stage_core_inputs(cfg, inputs, core) for core in range(8)]
    res = run_bass_kernel_spmd(nc, in_maps, core_ids=list(range(8)))
    return assemble_output(cfg, res.results)
